# version 15
# baseline (speedup 1.0000x reference)
import math
import numpy as np
from contextlib import ExitStack

import concourse.bass as bass
import concourse.mybir as mybir
from concourse.bass_utils import run_bass_kernel_spmd

F32 = mybir.dt.float32
BF16 = mybir.dt.bfloat16
AF = mybir.ActivationFunctionType
ALU = mybir.AluOpType
AX = mybir.AxisListType

T = 2048
D = 2048
NH = 16
HD = 64
DV = 128
NTC = 16
NDC = 16
NPAIR = 8
NOC = 16
NOEC = 16
EPS = 1e-5
LAMBDA_INIT = 0.8 - 0.6 * math.exp(-0.3 * 0.0)
SCALE = HD ** (-0.5)
DEBUG = False
DEBUG2 = False
SCRATCH_OUT = False


class Sem:
    __slots__ = ("h", "n", "name")

    def __init__(self, name):
        self.h = None
        self.n = 0
        self.name = name


class Plan:
    def __init__(self):
        self.engs = {"sync": [], "gpsimd": [], "tensor": [], "vector": [], "scalar": []}
        self.meta = {k: [] for k in self.engs}

    def emit(self, eng, fn, waits=(), inc=None, inck=1):
        if inc is not None:
            inc.n += inck
        waits = tuple((s, v) for (s, v) in waits)

        def thunk(h, fn=fn, waits=waits, inc=inc, inck=inck):
            for s, v in waits:
                if v > 0:
                    h.wait_ge(s.h, v)
            ins = fn(h)
            if inc is not None:
                assert ins is not None
                ins.then_inc(inc.h, inck)

        self.engs[eng].append(thunk)
        self.meta[eng].append((waits, inc, inck))
        return inc.n if inc is not None else None

    def wait(self, eng, *pairs):
        pairs = tuple(pairs)

        def thunk(h, pairs=pairs):
            for s, v in pairs:
                if v > 0:
                    h.wait_ge(s.h, v)

        self.engs[eng].append(thunk)
        self.meta[eng].append((pairs, None, 0))

    def cemit(self, eng, fn, sem, waits=()):
        w = [(sem, sem.n)] + list(waits)
        return self.emit(eng, fn, waits=w, inc=sem)

    def mark(self, name):
        if not hasattr(self, "marks"):
            self.marks = []
        self.marks.append((name, {k: len(v) for k, v in self.engs.items()}))

    def truncate(self, name):
        snap = dict(self.marks)[name]
        for k in self.engs:
            self.engs[k] = self.engs[k][:snap[k]]
            self.meta[k] = self.meta[k][:snap[k]]

    def check_schedule(self):
        pos = {k: 0 for k in self.engs}
        counts = {}
        for k, ms in self.meta.items():
            for (waits, inc, inck) in ms:
                for s, v in waits:
                    counts.setdefault(s.name, 0)
                if inc is not None:
                    counts.setdefault(inc.name, 0)
        progressed = True
        while progressed:
            progressed = False
            for k, ms in self.meta.items():
                while pos[k] < len(ms):
                    waits, inc, inck = ms[pos[k]]
                    if all(counts.get(s.name, 0) >= v for s, v in waits):
                        if inc is not None:
                            counts[inc.name] = counts.get(inc.name, 0) + inck
                        pos[k] += 1
                        progressed = True
                    else:
                        break
        stuck = {k: pos[k] for k in self.engs if pos[k] < len(self.meta[k])}
        if stuck:
            lines = []
            for k, i in stuck.items():
                waits, inc, inck = self.meta[k][i]
                unmet = [(s.name, v, counts.get(s.name, 0)) for s, v in waits
                         if counts.get(s.name, 0) < v]
                lines.append(f"{k}[{i}/{len(self.meta[k])}] blocked on {unmet}")
            raise RuntimeError("DEADLOCK:\n" + "\n".join(lines))
        return counts


def build_core_program(phase_limit=None):
    nc = bass.Bass("TRN2", target_bir_lowering=False)
    p = Plan()

    x0_d = nc.dram_tensor("x0", [T, D], F32, kind="ExternalInput").ap()
    wa_d = nc.dram_tensor("wa", [NPAIR, NDC, 128, 768], F32, kind="ExternalInput").ap()
    wp_d = nc.dram_tensor("wp", [NDC, 128, D], F32, kind="ExternalInput").ap()
    lq1_d = nc.dram_tensor("lq1", [128, 16], F32, kind="ExternalInput").ap()
    lk1_d = nc.dram_tensor("lk1", [128, 16], F32, kind="ExternalInput").ap()
    lq2_d = nc.dram_tensor("lq2", [128, 16], F32, kind="ExternalInput").ap()
    lk2_d = nc.dram_tensor("lk2", [128, 16], F32, kind="ExternalInput").ap()
    rwt_d = nc.dram_tensor("rwt", [D, 16384], F32, kind="ExternalInput").ap()
    rb_d = nc.dram_tensor("rb", [1, 16384], F32, kind="ExternalInput").ap()
    ew_d = nc.dram_tensor("ew", [NOC, 128, 8, NDC, 128], F32, kind="ExternalInput").ap()
    ebb_d = nc.dram_tensor("ebb", [8, D], F32, kind="ExternalInput").ap()
    out_d = nc.dram_tensor("out", [T, D], F32, kind="ExternalOutput").ap()

    scrkind = "ExternalOutput" if SCRATCH_OUT else "Internal"
    x1_d = nc.dram_tensor("x1scr", [NTC, 128, D], F32, kind=scrkind).ap()
    l2t_d = nc.dram_tensor("l2tscr", [NDC, 128, T], BF16, kind=scrkind).ap()
    fin_d = nc.dram_tensor("finscr", [NTC, 128, D], BF16, kind=scrkind).ap()
    dbg = {}
    if DEBUG2:
        dbg["outt2"] = nc.dram_tensor("dbg_outt2", [128, NH * T], BF16, kind="ExternalOutput").ap()
    if DEBUG:
        dbg["ln1t"] = nc.dram_tensor("dbg_ln1t", [128, NDC * T], BF16, kind="ExternalOutput").ap()
        dbg["outt"] = nc.dram_tensor("dbg_outt", [128, NH * T], BF16, kind="ExternalOutput").ap()
        dbg["c"] = nc.dram_tensor("dbg_c", [128, 128], F32, kind="ExternalOutput").ap()
        dbg["x1"] = nc.dram_tensor("dbg_x1", [NTC, 128, D], F32, kind="ExternalOutput").ap()
        dbg["l2t"] = nc.dram_tensor("dbg_l2t", [NDC, 128, T], BF16, kind="ExternalOutput").ap()
        dbg["fin"] = nc.dram_tensor("dbg_fin", [NTC, 128, D], BF16, kind="ExternalOutput").ap()
        dbg["den"] = nc.dram_tensor("dbg_den", [NH, 2, 1, 1024], F32, kind="ExternalOutput").ap()
        dbg["stt"] = nc.dram_tensor("dbg_stt", [NTC, 128, 8], F32, kind="ExternalOutput").ap()
        dbg["lg0"] = nc.dram_tensor("dbg_lg0", [128, 1024], F32, kind="ExternalOutput").ap()
        dbg["ab0"] = nc.dram_tensor("dbg_ab0", [128, 1024], BF16, kind="ExternalOutput").ap()
        dbg["sg0"] = nc.dram_tensor("dbg_sg0", [128, 1024], BF16, kind="ExternalOutput").ap()
        dbg["wb0"] = nc.dram_tensor("dbg_wb0", [128, 1024], BF16, kind="ExternalOutput").ap()
        dbg["m20"] = nc.dram_tensor("dbg_m20", [128, 128], F32, kind="ExternalOutput").ap()
        dbg["crd0"] = nc.dram_tensor("dbg_crd0", [128, 8], F32, kind="ExternalOutput").ap()

    sDX = Sem("sDX")
    sDW = Sem("sDW")
    sT = Sem("sT")
    sA = Sem("sA")
    sV = Sem("sV")
    sG = Sem("sG")
    sO = Sem("sO")
    sems = [sDX, sDW, sT, sA, sV, sG, sO]

    st = ExitStack()
    SB = lambda name, shape, dt: st.enter_context(nc.sbuf_tensor(name, shape, dt))
    PS = lambda name, shape, dt: st.enter_context(nc.psum_tensor(name, shape, dt))

    with st:
        for s in sems:
            s.h = st.enter_context(nc.semaphore(s.name))

        ones_k = SB("ones_k", [128, 1], BF16)
        onesr_b = SB("onesr_b", [1, 128], BF16)
        onesr_f = SB("onesr_f", [1, 128], F32)
        onesr08 = SB("onesr08", [1, 128], F32)
        ident_b = SB("ident_b", [128, 128], BF16)
        ident_f = SB("ident_f", [128, 128], F32)
        sel = SB("sel", [8, 8 * 128], BF16)
        lamneg = SB("lamneg", [128, 1], F32)
        lq1 = SB("lq1_s", [128, 16], F32)
        lk1 = SB("lk1_s", [128, 16], F32)
        lq2 = SB("lq2_s", [128, 16], F32)
        lk2 = SB("lk2_s", [128, 16], F32)
        lam_sc = SB("lam_sc", [1, 4], F32)
        eb_s = SB("eb_s", [8, D], BF16)

        p.emit("gpsimd", lambda g: g.memset(ones_k[:], 1.0))
        p.emit("gpsimd", lambda g: g.memset(onesr_b[:], 1.0))
        p.emit("gpsimd", lambda g: g.memset(onesr_f[:], 1.0))
        p.emit("gpsimd", lambda g: g.memset(onesr08[:], 1.0 - LAMBDA_INIT))
        p.emit("gpsimd", lambda g: g.memset(ident_b[:], 0.0))
        p.emit("gpsimd", lambda g: g.affine_select(
            out=ident_b[:], in_=ident_b[:], compare_op=ALU.not_equal, fill=1.0,
            base=0, pattern=[[-1, 128]], channel_multiplier=1))
        p.emit("gpsimd", lambda g: g.memset(ident_f[:], 0.0))
        p.emit("gpsimd", lambda g: g.affine_select(
            out=ident_f[:], in_=ident_f[:], compare_op=ALU.not_equal, fill=1.0,
            base=0, pattern=[[-1, 128]], channel_multiplier=1))
        p.emit("gpsimd", lambda g: g.memset(sel[:], 0.0))
        for e in range(8):
            p.emit("gpsimd", lambda g, e=e: g.affine_select(
                out=sel[:, e * 128:(e + 1) * 128], in_=sel[:, e * 128:(e + 1) * 128],
                compare_op=ALU.not_equal, fill=1.0, base=-e,
                pattern=[[0, 128]], channel_multiplier=1))
        g_consts = p.emit("gpsimd", lambda g: g.engine_nop(), inc=sG)

        for src, dst in [(lq1_d, lq1), (lk1_d, lk1), (lq2_d, lq2), (lk2_d, lk2)]:
            p.emit("sync", lambda h, s=src, d=dst: h.dma_start(out=d[:], in_=s), inc=sDX, inck=16)
        p.emit("gpsimd", lambda g: g.dma_start(out=eb_s[:], in_=ebb_d), inc=sDW, inck=16)
        dw_rbeb = sDW.n
        dx_lam = sDX.n

        ph = ExitStack()
        s_ln = ExitStack()
        s_ap = ExitStack()
        ln1T = s_ln.enter_context(nc.sbuf_tensor("ln1T", [128, NDC * T], BF16, side="right"))
        outT = s_ap.enter_context(nc.sbuf_tensor("outT", [128, NH * T], BF16))
        with ph:
            x0c = [ph.enter_context(nc.sbuf_tensor(f"x0c{i}", [128, D], F32)) for i in range(3)]
            ln1n = [ph.enter_context(nc.sbuf_tensor(f"ln1n{i}", [128, D], BF16)) for i in range(2)]
            sqs = ph.enter_context(nc.sbuf_tensor("sqs", [128, D], BF16))
            stt_t = ph.enter_context(nc.sbuf_tensor("stt", [128, 16], F32))
            trP = [ph.enter_context(nc.psum_tensor(f"trP{i}", [128, T], BF16)) for i in range(2)]
            lamP = [ph.enter_context(nc.psum_tensor(f"lamP{i}", [1, 1], F32)) for i in range(2)]
            repP = ph.enter_context(nc.psum_tensor("repP", [128, 1], F32))

            for j, (qa, ka, ps_) in enumerate([(lq1, lk1, lamP[0]), (lq2, lk2, lamP[1])]):
                for f in range(16):
                    p.emit("tensor", lambda t, qa=qa, ka=ka, ps_=ps_, f=f:
                           t.matmul(out=ps_[:], lhsT=qa[:, f:f + 1], rhs=ka[:, f:f + 1],
                                    start=(f == 0), stop=(f == 15)),
                           waits=[(sDX, dx_lam)] if f == 0 else (),
                           inc=sT if f == 15 else None)
            t_lamdots = sT.n
            p.emit("scalar", lambda s: s.activation(out=lam_sc[:, 0:1], in_=lamP[0][:], func=AF.Exp),
                   waits=[(sT, t_lamdots)])
            p.emit("scalar", lambda s: s.activation(out=lam_sc[:, 1:2], in_=lamP[1][:], func=AF.Exp),
                   inc=sA)
            a_lamexp = sA.n
            p.cemit("vector", lambda v: v.tensor_sub(out=lam_sc[:, 2:3], in0=lam_sc[:, 0:1], in1=lam_sc[:, 1:2]),
                    sV, waits=[(sA, a_lamexp)])
            p.cemit("vector", lambda v: v.tensor_scalar(out=lam_sc[:, 2:3], in0=lam_sc[:, 2:3],
                                                        scalar1=LAMBDA_INIT, scalar2=None, op0=ALU.add), sV)
            p.cemit("vector", lambda v: v.tensor_scalar(out=lam_sc[:, 3:4], in0=lam_sc[:, 2:3],
                                                        scalar1=-1.0, scalar2=None, op0=ALU.mult), sV)
            v_lam = sV.n
            p.emit("tensor", lambda t: t.matmul(out=repP[:], lhsT=onesr_f[:], rhs=lam_sc[:, 3:4],
                                                start=True, stop=True),
                   waits=[(sV, v_lam), (sG, g_consts)], inc=sT)
            t_rep = sT.n
            p.emit("scalar", lambda s: s.copy(out=lamneg[:], in_=repP[:]), waits=[(sT, t_rep)], inc=sA)
            a_lamneg = sA.n

            dx_x0 = [0] * NTC
            for tc in range(min(3, NTC)):
                dx_x0[tc] = p.emit("sync", lambda h, tc=tc: h.dma_start(
                    out=x0c[tc % 3][:], in_=x0_d[tc * 128:(tc + 1) * 128, :]), inc=sDX, inck=16)

            a_aff = [0] * NTC
            a_ev = [0] * NTC
            t_tr = [0] * NTC
            v_st_l = [0] * NTC
            dbg_sttd = 0
            for tc in range(NTC):
                xc = x0c[tc % 3]
                ln = ln1n[tc % 2]
                stt = stt_t[:, (tc % 2) * 8:(tc % 2) * 8 + 8]
                p.emit("scalar", lambda s, xc=xc, tc=tc, stt=stt: s.activation(
                    out=sqs[:], in_=xc[:], func=AF.Square, accum_out=stt[:, 0:1]),
                    waits=[(sDX, dx_x0[tc]),
                           (sV, v_st_l[tc - 2] if tc >= 2 else 0),
                           (sV, a_aff[tc - 2] if tc >= 2 else 0)], inc=sA)
                a_sq = sA.n
                p.cemit("vector", lambda v, xc=xc, stt=stt: v.reduce_sum(out=stt[:, 1:2], in_=xc[:], axis=AX.X),
                        sV, waits=[(sDX, dx_x0[tc])])
                p.cemit("vector", lambda v, stt=stt: v.tensor_scalar(out=stt[:, 2:3], in0=stt[:, 1:2],
                                                            scalar1=1.0 / D, scalar2=None, op0=ALU.mult), sV)
                p.cemit("vector", lambda v, stt=stt: v.tensor_tensor(out=stt[:, 3:4], in0=stt[:, 2:3], in1=stt[:, 2:3], op=ALU.mult), sV)
                p.cemit("vector", lambda v, stt=stt: v.tensor_scalar(out=stt[:, 4:5], in0=stt[:, 0:1],
                                                            scalar1=1.0 / D, scalar2=stt[:, 3:4],
                                                            op0=ALU.mult, op1=ALU.subtract), sV,
                        waits=[(sA, a_sq)])
                p.cemit("vector", lambda v, stt=stt: v.tensor_scalar(out=stt[:, 4:5], in0=stt[:, 4:5],
                                                            scalar1=EPS, scalar2=None, op0=ALU.add), sV)
                p.cemit("vector", lambda v, stt=stt: v.reciprocal(out=stt[:, 5:6], in_=stt[:, 4:5]), sV)
                v_st = sV.n
                v_st_l[tc] = v_st
                p.emit("scalar", lambda s, stt=stt: s.activation(out=stt[:, 6:7], in_=stt[:, 5:6], func=AF.Sqrt),
                       waits=[(sV, v_st)], inc=sA)
                a_rstd = sA.n
                a_aff[tc] = p.cemit("vector", lambda v, xc=xc, ln=ln, stt=stt: v.tensor_scalar(
                    out=ln[:], in0=xc[:], scalar1=stt[:, 2:3], scalar2=stt[:, 6:7],
                    op0=ALU.subtract, op1=ALU.mult),
                    sV, waits=[(sA, a_rstd), (sA, a_ev[tc - 2] if tc >= 2 else 0)])
                if DEBUG:
                    p.emit("sync", lambda h, tc=tc: h.dma_start(out=dbg["stt"][tc], in_=stt[:]),
                           waits=[(sV, a_aff[tc])], inc=sDX, inck=16)
                    dbg_sttd = sDX.n
                if tc + 3 < NTC:
                    dx_x0[tc + 3] = p.emit("sync", lambda h, tc=tc: h.dma_start(
                        out=x0c[(tc + 3) % 3][:], in_=x0_d[(tc + 3) * 128:(tc + 4) * 128, :]),
                        waits=[(sV, a_aff[tc])], inc=sDX, inck=16)
                tp = trP[tc % 2]
                for j in range(NDC):
                    p.emit("tensor", lambda t, tp=tp, ln=ln, j=j, tc=tc: t.transpose(
                        out=tp[:, j * 128:(j + 1) * 128], in_=ln[:, j * 128:(j + 1) * 128], identity=ident_b[:]),
                        waits=([(sG, g_consts), (sV, a_aff[tc]),
                                (sA, a_ev[tc - 2] if tc >= 2 else 0)] if j == 0 else ()),
                        inc=sT if j == NDC - 1 else None)
                t_tr[tc] = sT.n
                a_ev[tc] = p.emit("scalar", lambda s, tp=tp, tc=tc: s.activation(
                    out=ln1T[:].rearrange("p (j t) -> p j t", t=T)[:, :, tc * 128:(tc + 1) * 128],
                    in_=tp[:].rearrange("p (j c) -> p j c", c=128),
                    func=AF.Copy),
                    waits=[(sT, t_tr[tc])], inc=sA)
            a_ln1_done = sA.n
            if DEBUG:
                p.emit("sync", lambda h: h.dma_start(out=dbg["ln1t"], in_=ln1T[:]),
                       waits=[(sA, a_ln1_done)], inc=sDX, inck=16)

        p.mark("ln1")
        ph = ExitStack()
        with ph:
            wab = ph.enter_context(nc.sbuf_tensor("wab", [128, NDC * 768], BF16, side="right"))
            qstk = [ph.enter_context(nc.sbuf_tensor(f"qstk{i}", [128, T], BF16)) for i in range(2)]
            kstk = [ph.enter_context(nc.sbuf_tensor(f"kstk{i}", [128, T], BF16)) for i in range(2)]
            vnat = ph.enter_context(nc.sbuf_tensor("vnat", [128, NTC * 256], BF16))
            ptb = [ph.enter_context(nc.sbuf_tensor(f"ptb{i}", [128, 1024], BF16)) for i in range(2)]
            sqb = ph.enter_context(nc.sbuf_tensor("sqb", [128, T], BF16))

            nrm2 = ph.enter_context(nc.sbuf_tensor("nrm", [1, 16], F32))
            den_s = ph.enter_context(nc.sbuf_tensor("den_s", [1, 1024], F32))
            rec_s = ph.enter_context(nc.sbuf_tensor("rec_s", [1, 1024], F32))
            bca_s = ph.enter_context(nc.sbuf_tensor("bca_s", [128, 1024], BF16))
            bM = [ph.enter_context(nc.sbuf_tensor(f"bM{i}", [128, 1], F32)) for i in range(2)]
            qkP = [ph.enter_context(nc.psum_tensor(f"qkP{i}", [128, 1024], F32)) for i in range(2)]
            pvP = ph.enter_context(nc.psum_tensor("pvP", [128, 1024], F32))
            dnP = ph.enter_context(nc.psum_tensor("dnP", [1, 1024], F32))

            dw_wa = [0] * NPAIR
            for dc2 in range(NDC):
                p.emit("gpsimd", lambda g, dc=dc2: g.dma_start(
                    out=wab[:, dc * 768:(dc + 1) * 768], in_=wa_d[0, dc]), inc=sDW, inck=16)
            dw_wa[0] = sDW.n

            t_qkv_done = [0] * NPAIR
            t_qkv_v_end = [0] * NPAIR
            v_out_hist = []
            a_bca_hist = [0]
            a_den_hist = [0]
            v_den_hist = [0]
            pending_tail = [None]

            def flush_tail():
                if pending_tail[0] is not None:
                    fn = pending_tail[0]
                    pending_tail[0] = None
                    fn()

            for pr in range(NPAIR):
                h0 = 2 * pr
                flush_tail()
                a_evq = 0
                grp = 0
                a_evac_hist = []
                for jc in range(4):
                    for th in range(2):
                        ps_ = qkP[grp % 2]
                        w0 = [(sDW, dw_wa[pr]), (sA, a_ln1_done)]
                        if len(a_evac_hist) >= 2:
                            w0.append((sA, a_evac_hist[-2]))
                        if pr == 0 and jc == 0 and th == 0:
                            w0.append((sA, a_lamneg))
                        if jc == 0 and th == 0 and v_out_hist:
                            w0.append((sV, v_out_hist[-1]))
                        for dc in range(NDC):
                            for q in range(2):
                                p.emit("tensor", lambda t, ps_=ps_, dc=dc, jc=jc, th=th, q=q:
                                       t.matmul(out=ps_[:, q * 512:(q + 1) * 512],
                                                lhsT=wab[:, dc * 768 + jc * 128: dc * 768 + (jc + 1) * 128],
                                                rhs=ln1T[:, dc * T + th * 1024 + q * 512: dc * T + th * 1024 + (q + 1) * 512],
                                                start=(dc == 0), stop=(dc == NDC - 1)),
                                       waits=w0 if (dc == 0 and q == 0) else (),
                                       inc=sT if (dc == NDC - 1 and q == 1) else None)
                        t_grp = sT.n
                        if jc == 0:
                            mk = lambda s, ps_=ps_, th=th: (
                                s.activation(out=qstk[0][0:64, th * 1024:(th + 1) * 1024],
                                             in_=ps_[0:64, :], func=AF.Copy, scale=SCALE),
                                s.activation(out=qstk[1][0:64, th * 1024:(th + 1) * 1024],
                                             in_=ps_[64:128, :], func=AF.Copy, scale=SCALE))
                        elif jc == 1:
                            mk = lambda s, ps_=ps_, th=th: (
                                s.activation(out=kstk[0][0:64, th * 1024:(th + 1) * 1024],
                                             in_=ps_[0:64, :], func=AF.Copy),
                                s.activation(out=kstk[1][0:64, th * 1024:(th + 1) * 1024],
                                             in_=ps_[64:128, :], func=AF.Copy))
                        elif jc == 2:
                            mk = lambda s, ps_=ps_, th=th: (
                                s.activation(out=qstk[0][64:128, th * 1024:(th + 1) * 1024],
                                             in_=ps_[0:64, :], func=AF.Copy, scale=SCALE),
                                s.activation(out=qstk[1][64:128, th * 1024:(th + 1) * 1024],
                                             in_=ps_[64:128, :], func=AF.Copy, scale=SCALE))
                        else:
                            mk = lambda s, ps_=ps_, th=th: (
                                s.activation(out=kstk[0][64:128, th * 1024:(th + 1) * 1024],
                                             in_=ps_[0:64, :], func=AF.Copy, scale=lamneg[0:64, :]),
                                s.activation(out=kstk[1][64:128, th * 1024:(th + 1) * 1024],
                                             in_=ps_[64:128, :], func=AF.Copy, scale=lamneg[64:128, :]))

                        def evac(s, mk=mk):
                            a, b = mk(s)
                            return b
                        p.emit("scalar", lambda s, evac=evac: evac(s), waits=[(sT, t_grp)], inc=sA)
                        a_evac_hist.append(sA.n)
                        grp += 1
                a_qk_evac = sA.n

                a_vev = [0] * NTC
                for tc in range(NTC):
                    reg = pvP[:, (tc % 2) * 512:(tc % 2) * 512 + 256]
                    w0 = []
                    if tc >= 2:
                        w0.append((sA, a_vev[tc - 2]))
                    if pr > 0:
                        w0.append((sV, v_out_hist[-1]))
                    for dc in range(NDC):
                        p.emit("tensor", lambda t, reg=reg, dc=dc, tc=tc:
                               t.matmul(out=reg,
                                        lhsT=ln1T[:, dc * T + tc * 128:dc * T + (tc + 1) * 128],
                                        rhs=wab[:, dc * 768 + 512:dc * 768 + 768],
                                        start=(dc == 0), stop=(dc == NDC - 1)),
                               waits=w0 if dc == 0 else (),
                               inc=sT if dc == NDC - 1 else None)
                    t_v = sT.n
                    a_vev[tc] = p.emit("scalar", lambda s, reg=reg, tc=tc: s.activation(
                        out=vnat[:, tc * 256:(tc + 1) * 256], in_=reg, func=AF.Copy),
                        waits=[(sT, t_v)], inc=sA)
                a_v_done = sA.n
                t_qkv_v_end[pr] = sT.n
                p.mark(f"attn_qkv{pr}")

                if pr + 1 < NPAIR:
                    for dc2 in range(NDC):
                        p.emit("gpsimd", lambda g, dc=dc2, pr=pr: g.dma_start(
                            out=wab[:, dc * 768:(dc + 1) * 768], in_=wa_d[pr + 1, dc]),
                            waits=[(sT, t_v)] if dc2 == 0 else (), inc=sDW, inck=16)
                    dw_wa[pr + 1] = sDW.n

                a_bM_l = [0, 0]
                for hoff in range(2):
                    h = h0 + hoff
                    qs, ks = qstk[hoff], kstk[hoff]
                    nrm = nrm2[:, hoff * 8:hoff * 8 + 8]
                    for idx, stk in enumerate((qs, ks)):
                        p.emit("vector", lambda v, stk=stk: v.tensor_mul(out=sqb[:], in0=stk[:], in1=stk[:]),
                               waits=[(sA, a_qk_evac), (sT, sT.n)], inc=sV)
                        v_sq = sV.n
                        for r in range(2):
                            w0 = [(sV, v_sq), (sV, v_den_hist[-1])] if r == 0 else [(sV, sV.n)]
                            for q in range(2):
                                p.emit("tensor", lambda t, r=r, q=q: t.matmul(
                                    out=dnP[:, q * 512:(q + 1) * 512], lhsT=ones_k[:],
                                    rhs=sqb[:, r * 1024 + q * 512:r * 1024 + (q + 1) * 512],
                                    start=True, stop=True),
                                    waits=w0 if q == 0 else (),
                                    inc=sT if q == 1 else None)
                            t_qn = sT.n
                            p.emit("vector", lambda v, idx=idx, r=r, nrm=nrm: v.reduce_max(
                                out=nrm[:, idx * 2 + r:idx * 2 + r + 1], in_=dnP[:], axis=AX.X),
                                waits=[(sT, t_qn)], inc=sV)
                    p.cemit("vector", lambda v, nrm=nrm: v.tensor_max(out=nrm[:, 0:1], in0=nrm[:, 0:1], in1=nrm[:, 1:2]), sV)
                    p.cemit("vector", lambda v, nrm=nrm: v.tensor_max(out=nrm[:, 1:2], in0=nrm[:, 2:3], in1=nrm[:, 3:4]), sV)
                    p.cemit("vector", lambda v, nrm=nrm: v.tensor_add(out=nrm[:, 2:3], in0=nrm[:, 0:1], in1=nrm[:, 1:2]), sV)
                    p.cemit("vector", lambda v, nrm=nrm: v.tensor_scalar(out=nrm[:, 3:4], in0=nrm[:, 2:3],
                                                                scalar1=-0.5, scalar2=None, op0=ALU.mult), sV)
                    v_negm = sV.n
                    w_rep = [(sV, v_negm), (sA, a_v_done)]
                    if hoff == 1:
                        w_rep.append((sA, a_bM_l[0]))
                    if v_out_hist:
                        w_rep.append((sV, v_out_hist[-1]))
                    p.emit("tensor", lambda t, nrm=nrm: t.matmul(out=pvP[:, 0:1], lhsT=onesr_f[:], rhs=nrm[:, 3:4],
                                                        start=True, stop=True),
                           waits=w_rep, inc=sT)
                    t_rep2 = sT.n
                    p.emit("scalar", lambda s, hoff=hoff: s.copy(out=bM[hoff][:], in_=pvP[:, 0:1]),
                           waits=[(sT, t_rep2)], inc=sA)
                    a_bM_l[hoff] = sA.n
                    p.mark(f"pre_h{h}")

                for hoff in range(2):
                    h = h0 + hoff
                    qs, ks = qstk[hoff], kstk[hoff]
                    a_bM = a_bM_l[hoff]
                    for th in range(2):
                        flush_tail()
                        t_s = [0] * NTC
                        a_exp = [0] * NTC
                        t_dp = [0] * NTC
                        def emit_S(sc, w_extra):
                            ps_ = qkP[sc % 2]
                            for q in range(2):
                                p.emit("tensor", lambda t, ps_=ps_, sc=sc, q=q, ks=ks, qs=qs, th=th:
                                       t.matmul(out=ps_[:, q * 512:(q + 1) * 512],
                                                lhsT=ks[:, sc * 128:(sc + 1) * 128],
                                                rhs=qs[:, th * 1024 + q * 512:th * 1024 + (q + 1) * 512],
                                                start=True, stop=True),
                                       waits=w_extra if q == 0 else (),
                                       inc=sT if q == 1 else None)
                            t_s[sc] = sT.n
                        emit_S(0, [(sA, a_bM), (sA, a_bca_hist[-1])])
                        emit_S(1, [(sA, a_bca_hist[-1])])
                        for sc in range(NTC):
                            w_exp = [(sT, t_s[sc])]
                            p.emit("scalar", lambda s, sc=sc, hoff=hoff: s.activation(
                                out=ptb[sc % 2][:], in_=qkP[sc % 2][:], func=AF.Exp, bias=bM[hoff][:]),
                                waits=w_exp, inc=sA)
                            a_exp[sc] = sA.n
                            for q in range(2):
                                wdn = [(sA, a_exp[sc])] if q == 0 else []
                                if sc == 0 and q == 0:
                                    wdn.append((sV, v_den_hist[-1]))
                                p.emit("tensor", lambda t, sc=sc, q=q: t.matmul(
                                    out=dnP[:, q * 512:(q + 1) * 512], lhsT=ones_k[:],
                                    rhs=ptb[sc % 2][:, q * 512:(q + 1) * 512],
                                    start=(sc == 0), stop=(sc == NTC - 1)),
                                    waits=wdn)
                            for q in range(2):
                                wpv = []
                                if sc == 0 and q == 0 and v_out_hist:
                                    wpv.append((sV, v_out_hist[-1]))
                                p.emit("tensor", lambda t, sc=sc, q=q, hoff=hoff: t.matmul(
                                    out=pvP[:, q * 512:(q + 1) * 512],
                                    lhsT=vnat[:, sc * 256 + hoff * 128:sc * 256 + (hoff + 1) * 128],
                                    rhs=ptb[sc % 2][:, q * 512:(q + 1) * 512],
                                    start=(sc == 0), stop=(sc == NTC - 1)),
                                    waits=wpv,
                                    inc=sT if (q == 1 and sc == NTC - 1) else None)
                            t_dp[sc] = sT.n
                            if sc + 2 < NTC:
                                emit_S(sc + 2, [(sA, a_exp[sc])])
                        p.cemit("vector", lambda v: v.reciprocal(out=rec_s[:], in_=dnP[:]),
                                sV, waits=[(sT, t_dp[NTC - 1])])
                        v_rec = sV.n
                        a_den_hist.append(0)
                        v_den_hist.append(v_rec)
                        if DEBUG:
                            p.emit("sync", lambda hd_, h=h, th=th: hd_.dma_start(
                                out=dbg["den"][h, th], in_=rec_s[:]), waits=[(sV, v_rec)], inc=sDX, inck=16)

                        def tail2(h=h, th=th, v_rec=v_rec, a_exp15=a_exp[NTC - 1], t_dp15=t_dp[NTC - 1]):
                            for q in range(2):
                                p.emit("tensor", lambda t, q=q: t.matmul(
                                    out=qkP[0][:, q * 512:(q + 1) * 512], lhsT=onesr08[:],
                                    rhs=rec_s[:, q * 512:(q + 1) * 512], start=True, stop=True),
                                    waits=[(sV, v_rec), (sA, a_exp15)] if q == 0 else (),
                                    inc=sT if q == 1 else None)
                            t_bc = sT.n
                            p.emit("scalar", lambda s: s.activation(out=bca_s[:], in_=qkP[0][:], func=AF.Copy),
                                   waits=[(sT, t_bc)], inc=sA)
                            a_bca = sA.n
                            a_bca_hist.append(a_bca)
                            p.emit("vector", lambda v, h=h, th=th: v.tensor_mul(
                                out=outT[:, h * T + th * 1024:h * T + (th + 1) * 1024],
                                in0=pvP[:], in1=bca_s[:]),
                                waits=[(sA, a_bca), (sT, t_dp15)], inc=sV)
                            v_out_hist.append(sV.n)
                        pending_tail[0] = tail2
                        p.mark(f"th_h{h}_{th}")
                    p.mark(f"attn_h{h}")
            flush_tail()
            a_attn_v_done = v_out_hist[-1]
            if DEBUG2:
                p.emit("sync", lambda h: h.dma_start(out=dbg["outt2"], in_=outT[:]),
                       waits=[(sV, a_attn_v_done)], inc=sDX, inck=16)
            if DEBUG:
                p.emit("sync", lambda h: h.dma_start(out=dbg["outt"], in_=outT[:]),
                       waits=[(sV, a_attn_v_done)], inc=sDX, inck=16)

        p.mark("attn")
        s_ln.close()

        ph = ExitStack()
        with ph:
            wpb = ph.enter_context(nc.sbuf_tensor("wpb", [128, NDC * D], BF16, side="right"))
            px0l = [ph.enter_context(nc.sbuf_tensor(f"px0{i}", [128, D], F32)) for i in range(2)]
            x1c = [ph.enter_context(nc.sbuf_tensor(f"px1{i}", [128, D], F32)) for i in range(2)]
            l2n = [ph.enter_context(nc.sbuf_tensor(f"l2n{i}", [128, D], BF16)) for i in range(3)]
            l2st = [ph.enter_context(nc.sbuf_tensor(f"l2st{i}", [128, D], BF16)) for i in range(2)]
            pj_s = [ph.enter_context(nc.sbuf_tensor(f"pj_s{i}", [128, D], F32)) for i in range(2)]

            pst_t = ph.enter_context(nc.sbuf_tensor("pst", [128, 24], F32))
            pjP = [ph.enter_context(nc.psum_tensor(f"pjP{i}", [128, 1024], F32)) for i in range(3)]
            trP2 = ph.enter_context(nc.psum_tensor("trP2", [128, T], BF16))

            for dc2 in range(NDC):
                p.emit("gpsimd", lambda g, dc=dc2: g.dma_start(
                    out=wpb[:, dc * D:(dc + 1) * D], in_=wp_d[dc]),
                    waits=[(sT, t_qkv_v_end[NPAIR - 1])] if dc2 == 0 else (),
                    inc=sDW, inck=16)
            dw_wp = sDW.n

            dx_px0 = [0] * NTC
            for tc in range(2):
                dx_px0[tc] = p.emit("sync", lambda h, tc=tc: h.dma_start(
                    out=px0l[tc % 2][:], in_=x0_d[tc * 128:(tc + 1) * 128, :]),
                    waits=[(sV, a_attn_v_done), (sA, a_bca_hist[-1])] if tc == 0 else (),
                    inc=sDX, inck=16)

            v_x1 = [0] * NTC
            dx_x1st = [0] * NTC
            a_l2aff_l = [0] * NTC
            dx_l2st = [0] * NTC
            v_r2_l = [0] * NTC
            a_pjh_l = [0] * (2 * NTC)
            a_l2ev = [0] * NTC
            t_half = {}
            pend_l2 = []
            for tc in range(NTC):
                pst = pst_t[:, (tc % 2) * 12:(tc % 2) * 12 + 12]
                for half in range(2):
                    ps_ = pjP[(2 * tc + half) % 3]
                    w0 = [(sDW, dw_wp), (sV, a_attn_v_done)]
                    if 2 * tc + half >= 3:
                        w0.append((sA, a_pjh_l[2 * tc + half - 3]))
                    for dvc in range(NDC):
                        for q in range(2):
                            p.emit("tensor", lambda t, ps_=ps_, dvc=dvc, tc=tc, half=half, q=q:
                                   t.matmul(out=ps_[:, q * 512:(q + 1) * 512],
                                            lhsT=outT[:, dvc * T + tc * 128:dvc * T + (tc + 1) * 128],
                                            rhs=wpb[:, dvc * D + half * 1024 + q * 512:dvc * D + half * 1024 + (q + 1) * 512],
                                            start=(dvc == 0), stop=(dvc == NDC - 1)),
                                   waits=w0 if (dvc == 0 and q == 0) else (),
                                   inc=sT if (dvc == NDC - 1 and q == 1) else None)
                    t_half[(tc, half)] = sT.n
                    if half == 1 and len(pend_l2) >= 2:
                        pend_l2.pop(0)()
                pj = pj_s[tc % 2]
                for half in range(2):
                    p.emit("scalar", lambda s, half=half, pj=pj, tc=tc: s.activation(
                        out=pj[:, half * 1024:(half + 1) * 1024], in_=pjP[(2 * tc + half) % 3][:], func=AF.Copy),
                        waits=[(sT, t_half[(tc, half)]),
                               (sV, v_x1[tc - 1] if tc >= 1 else 0)], inc=sA)
                    a_pjh_l[2 * tc + half] = sA.n
                scr = pj_s[(tc + 1) % 2]
                for half in range(2):
                    p.emit("scalar", lambda s, half=half, pj=pj, pst=pst, scr=scr: s.activation(
                        out=scr[:, half * 1024:(half + 1) * 1024], in_=pj[:, half * 1024:(half + 1) * 1024],
                        func=AF.Square,
                        accum_out=pst[:, half:half + 1]),
                        waits=[(sV, v_r2_l[tc - 2] if tc >= 2 else 0),
                               (sV, v_x1[tc - 1] if tc >= 1 else 0)], inc=sA)
                    a_sqh = sA.n
                    p.emit("vector", lambda v, half=half, pj=pj, pst=pst: v.reduce_sum(
                        out=pst[:, 2 + half:3 + half], in_=pj[:, half * 1024:(half + 1) * 1024], axis=AX.X),
                        waits=[(sA, a_sqh)], inc=sV)
                p.cemit("vector", lambda v, pst=pst: v.tensor_add(out=pst[:, 4:5], in0=pst[:, 2:3], in1=pst[:, 3:4]), sV)
                p.cemit("vector", lambda v, pst=pst: v.tensor_scalar(out=pst[:, 4:5], in0=pst[:, 4:5],
                                                            scalar1=1.0 / D, scalar2=None, op0=ALU.mult), sV)
                p.cemit("vector", lambda v, pst=pst: v.tensor_add(out=pst[:, 5:6], in0=pst[:, 0:1], in1=pst[:, 1:2]), sV)
                p.cemit("vector", lambda v, pst=pst: v.tensor_tensor(out=pst[:, 6:7], in0=pst[:, 4:5], in1=pst[:, 4:5], op=ALU.mult), sV)
                p.cemit("vector", lambda v, pst=pst: v.tensor_scalar(out=pst[:, 5:6], in0=pst[:, 5:6],
                                                            scalar1=1.0 / D, scalar2=pst[:, 6:7],
                                                            op0=ALU.mult, op1=ALU.subtract), sV)
                p.cemit("vector", lambda v, pst=pst: v.tensor_scalar(out=pst[:, 5:6], in0=pst[:, 5:6],
                                                            scalar1=EPS, scalar2=None, op0=ALU.add), sV)
                p.cemit("vector", lambda v, pst=pst: v.reciprocal(out=pst[:, 7:8], in_=pst[:, 5:6]), sV)
                v_r1 = sV.n
                p.emit("scalar", lambda s, pst=pst: s.activation(out=pst[:, 8:9], in_=pst[:, 7:8], func=AF.Sqrt),
                       waits=[(sV, v_r1)], inc=sA)
                a_r1 = sA.n
                p.cemit("vector", lambda v, pst=pst: v.tensor_tensor(out=pst[:, 9:10], in0=pst[:, 4:5], in1=pst[:, 8:9], op=ALU.mult),
                        sV, waits=[(sA, a_r1), (sDX, dx_px0[tc])])
                xc = px0l[tc % 2]
                xo = x1c[tc % 2]
                p.cemit("vector", lambda v, xc=xc, pst=pst: v.tensor_scalar(out=xc[:], in0=xc[:],
                                                                   scalar1=pst[:, 9:10], scalar2=None,
                                                                   op0=ALU.subtract), sV)
                for half in range(2):
                    wstt = [(sV, sV.n)]
                    if half == 0 and tc >= 2:
                        wstt.append((sDX, dx_x1st[tc - 2]))
                    p.emit("vector", lambda v, xc=xc, xo=xo, half=half, pj=pj, pst=pst: v.scalar_tensor_tensor(
                        out=xo[:, half * 1024:(half + 1) * 1024], in0=pj[:, half * 1024:(half + 1) * 1024],
                        scalar=pst[:, 8:9], in1=xc[:, half * 1024:(half + 1) * 1024],
                        op0=ALU.mult, op1=ALU.add), waits=wstt, inc=sV)
                v_x1[tc] = sV.n
                dx_x1st[tc] = p.emit("sync", lambda h, xo=xo, tc=tc: h.dma_start(out=x1_d[tc], in_=xo[:]),
                       waits=[(sV, v_x1[tc])], inc=sDX, inck=16)
                if DEBUG:
                    p.emit("sync", lambda h, xo=xo, tc=tc: h.dma_start(out=dbg["x1"][tc], in_=xo[:]),
                           inc=sDX, inck=16)
                if tc + 2 < NTC:
                    dx_px0[tc + 2] = p.emit("sync", lambda h, tc=tc: h.dma_start(
                        out=px0l[tc % 2][:], in_=x0_d[(tc + 2) * 128:(tc + 3) * 128, :]),
                        waits=[(sV, v_x1[tc])], inc=sDX, inck=16)
                p.emit("scalar", lambda s, xo=xo, pst=pst, scr=scr: s.activation(
                    out=scr[:], in_=xo[:], func=AF.Square, accum_out=pst[:, 0:1]),
                    waits=[(sV, v_x1[tc])], inc=sA)
                a_sq2 = sA.n
                p.cemit("vector", lambda v, xo=xo, pst=pst: v.reduce_sum(out=pst[:, 2:3], in_=xo[:], axis=AX.X),
                        sV, waits=[(sA, a_sq2)])
                p.cemit("vector", lambda v, pst=pst: v.tensor_scalar(out=pst[:, 4:5], in0=pst[:, 2:3],
                                                            scalar1=1.0 / D, scalar2=None, op0=ALU.mult), sV)
                p.cemit("vector", lambda v, pst=pst: v.tensor_tensor(out=pst[:, 6:7], in0=pst[:, 4:5], in1=pst[:, 4:5], op=ALU.mult), sV)
                p.cemit("vector", lambda v, pst=pst: v.tensor_scalar(out=pst[:, 5:6], in0=pst[:, 0:1],
                                                            scalar1=1.0 / D, scalar2=pst[:, 6:7],
                                                            op0=ALU.mult, op1=ALU.subtract), sV)
                p.cemit("vector", lambda v, pst=pst: v.tensor_scalar(out=pst[:, 5:6], in0=pst[:, 5:6],
                                                            scalar1=EPS, scalar2=None, op0=ALU.add), sV)
                p.cemit("vector", lambda v, pst=pst: v.reciprocal(out=pst[:, 7:8], in_=pst[:, 5:6]), sV)
                v_r2 = sV.n
                v_r2_l[tc] = v_r2
                p.emit("scalar", lambda s, pst=pst: s.activation(out=pst[:, 10:11], in_=pst[:, 7:8], func=AF.Sqrt),
                       waits=[(sV, v_r2)], inc=sA)
                a_r2 = sA.n
                ln = l2n[tc % 3]
                p.cemit("vector", lambda v, xo=xo, ln=ln, pst=pst: v.tensor_scalar(
                    out=ln[:], in0=xo[:], scalar1=pst[:, 4:5], scalar2=pst[:, 10:11],
                    op0=ALU.subtract, op1=ALU.mult),
                    sV, waits=[(sA, a_r2), (sA, a_l2ev[tc - 3] if tc >= 3 else 0)])
                a_l2aff = sV.n
                a_l2aff_l[tc] = a_l2aff
                def l2move(tc=tc, ln=ln, a_l2aff=a_l2aff):
                    for j in range(NDC):
                        p.emit("tensor", lambda t, ln=ln, j=j: t.transpose(
                            out=trP2[:, j * 128:(j + 1) * 128], in_=ln[:, j * 128:(j + 1) * 128],
                            identity=ident_b[:]),
                            waits=[(sV, a_l2aff), (sA, a_l2ev[tc - 1] if tc >= 1 else 0)] if j == 0 else (),
                            inc=sT if j == NDC - 1 else None)
                    t_tr2 = sT.n
                    stg = l2st[tc % 2]
                    wev = [(sT, t_tr2)]
                    if tc >= 2:
                        wev.append((sDX, dx_l2st[tc - 2]))
                    a_l2ev[tc] = p.emit("scalar", lambda s, stg=stg: s.activation(
                        out=stg[:], in_=trP2[:], func=AF.Copy),
                        waits=wev, inc=sA)
                    for j2 in range(2):
                        p.emit("sync", lambda h, stg=stg, tc=tc, j2=j2: h.dma_start(
                            out=l2t_d[:, :, tc * 128:(tc + 1) * 128].transpose([1, 0, 2])[:, j2 * 8:(j2 + 1) * 8, :],
                            in_=stg[:].rearrange("p (j c) -> p j c", c=128)[:, j2 * 8:(j2 + 1) * 8, :]),
                            waits=[(sA, a_l2ev[tc])], inc=sDX, inck=16)
                    dx_l2st[tc] = sDX.n
                pend_l2.append(l2move)
            while pend_l2:
                pend_l2.pop(0)()
            dx_l2t = sDX.n
            a_proj_done = sA.n
            v_proj_done = sV.n

        p.mark("proj")
        s_ap.close()

        l2T = st.enter_context(nc.sbuf_tensor("l2T", [128, NDC * T], BF16))
        rwt = [st.enter_context(nc.sbuf_tensor(f"rwt{i}", [128, NDC * 1024], BF16)) for i in range(2)]
        ewt = rwt
        c_nat = st.enter_context(nc.sbuf_tensor("c_nat", [128, 128], F32))

        for dc2 in range(NDC):
            p.emit("sync", lambda h, dc=dc2: h.dma_start(
                out=l2T[:, dc * T:(dc + 1) * T], in_=l2t_d[dc]),
                waits=[(sDX, dx_l2t)] if dc2 == 0 else (), inc=sDX, inck=16)
        dx_l2T = sDX.n
        dw_l2T = 0
        if DEBUG:
            for dc2 in range(NDC):
                p.emit("sync", lambda h, dc=dc2: h.dma_start(out=dbg["l2t"][dc], in_=l2T[:, dc * T:(dc + 1) * T]),
                       waits=[(sDX, dx_l2T)], inc=sDX, inck=16)
        p.emit("gpsimd", lambda g: g.memset(c_nat[:], 0.0),
               waits=[(sV, v_proj_done), (sDX, dx_l2t)], inc=sG)
        g_czero = sG.n

        ph = ExitStack()
        with ph:
            RW = 1024
            NOE2 = 16384 // RW
            G = RW // 8

            rbb = [ph.enter_context(nc.sbuf_tensor(f"rbb{i}", [1, RW], BF16)) for i in range(2)]
            lgb = [ph.enter_context(nc.sbuf_tensor(f"lgb{i}", [128, RW], F32)) for i in range(3)]
            Ab = ph.enter_context(nc.sbuf_tensor("Ab", [128, G * 4], F32))
            Bb = ph.enter_context(nc.sbuf_tensor("Bb", [128, G * 4], F32))
            Cb = ph.enter_context(nc.sbuf_tensor("Cb", [128, G * 2], F32))
            Db = ph.enter_context(nc.sbuf_tensor("Db", [128, G * 2], F32))
            Bm = ph.enter_context(nc.sbuf_tensor("Bm", [128, G * 2], F32))
            sm1 = [ph.enter_context(nc.sbuf_tensor(f"sm1_{i}", [128, G], F32)) for i in range(2)]
            sm2 = [ph.enter_context(nc.sbuf_tensor(f"sm2_{i}", [128, G], F32)) for i in range(2)]
            sEg = ph.enter_context(nc.sbuf_tensor("sEg", [128, G], F32))
            sB2 = ph.enter_context(nc.sbuf_tensor("sB2", [128, G], F32))
            sDm = ph.enter_context(nc.sbuf_tensor("sDm", [128, G], F32))
            ssum = [ph.enter_context(nc.sbuf_tensor(f"ssum{i}", [128, G], F32)) for i in range(2)]
            ab = [ph.enter_context(nc.sbuf_tensor(f"ab{i}", [128, RW], BF16)) for i in range(2)]
            d2b = [ph.enter_context(nc.sbuf_tensor(f"d2b{i}", [128, RW], BF16)) for i in range(2)]
            sgb = [ph.enter_context(nc.sbuf_tensor(f"sgb{i}", [128, RW], BF16)) for i in range(2)]
            wb = [ph.enter_context(nc.sbuf_tensor(f"wbuf{i}", [128, RW], BF16)) for i in range(2)]

            crd = ph.enter_context(nc.sbuf_tensor("crd", [128, 8], F32))
            lgP = [ph.enter_context(nc.psum_tensor(f"lgP{i}", [128, RW], F32)) for i in range(3)]

            dw_rw = [0] * NOE2
            p.emit("gpsimd", lambda g: g.dma_start(out=rbb[0][:], in_=rb_d[:, 0:RW]),
                   waits=[(sV, v_proj_done), (sDX, dx_l2t)], inc=sDW, inck=16)
            for dc2 in range(NDC):
                p.emit("gpsimd", lambda g, dc=dc2: g.dma_start(
                    out=rwt[0][:, dc * RW:(dc + 1) * RW],
                    in_=rwt_d[dc * 128:(dc + 1) * 128, 0:RW]), inc=sDW, inck=16)
            dw_rw[0] = sDW.n

            t_lg = {}
            a_lg = {}
            dbg_t0 = 0
            tile_idx = 0
            a_sg = {}
            v_w_hist = {}
            v_d2_hist = {}
            t_last_router_mm = 0
            for oe in range(NOE2):
                rw = rwt[oe % 2]
                rb_cur = rbb[oe % 2]
                for tc in range(NTC):
                    ps_ = lgP[tile_idx % 3]
                    w0 = [(sDW, dw_rw[oe]), (sDX, dx_l2T), (sDW, dw_l2T)]
                    if tile_idx >= 3:
                        w0.append((sA, a_lg[tile_idx - 3]))
                    for dc in range(NDC):
                        for q in range(2):
                            p.emit("tensor", lambda t, ps_=ps_, rw=rw, dc=dc, tc=tc, q=q:
                                   t.matmul(out=ps_[:, q * 512:(q + 1) * 512],
                                            lhsT=l2T[:, dc * T + tc * 128:dc * T + (tc + 1) * 128],
                                            rhs=rw[:, dc * RW + q * 512:dc * RW + (q + 1) * 512],
                                            start=(dc == 0), stop=False),
                                   waits=w0 if (dc == 0 and q == 0) else ())
                    for q in range(2):
                        p.emit("tensor", lambda t, ps_=ps_, rb_cur=rb_cur, q=q: t.matmul(
                            out=ps_[:, q * 512:(q + 1) * 512], lhsT=onesr_b[:],
                            rhs=rb_cur[:, q * 512:(q + 1) * 512],
                            start=False, stop=True), inc=sT if q == 1 else None)
                    t_lg[tile_idx] = sT.n
                    t_last_router_mm = sT.n
                    lg = lgb[tile_idx % 3]
                    wlg = [(sT, t_lg[tile_idx])]
                    if tile_idx >= 3:
                        wlg.append((sV, v_d2_hist[tile_idx - 3]))
                    a_lg[tile_idx] = p.emit("scalar", lambda s, ps_=ps_, lg=lg: s.activation(
                        out=lg[:], in_=ps_[:], func=AF.Copy),
                        waits=wlg, inc=sA)
                    par = tile_idx % 2
                    Lv = lg[:].rearrange("p (e g) -> p e g", g=G)
                    Av = Ab[:].rearrange("p (e g) -> p e g", g=G)
                    Bv = Bb[:].rearrange("p (e g) -> p e g", g=G)
                    Cv = Cb[:].rearrange("p (e g) -> p e g", g=G)
                    Dv = Db[:].rearrange("p (e g) -> p e g", g=G)
                    Bmv = Bm[:].rearrange("p (e g) -> p e g", g=G)
                    wl = [(sA, a_lg[tile_idx])]
                    if tile_idx >= 2:
                        wl.append((sV, v_d2_hist[tile_idx - 2]))
                    p.emit("vector", lambda v, Lv=Lv, Av=Av: v.tensor_tensor(
                        out=Av, in0=Lv[:, 0::2, :], in1=Lv[:, 1::2, :], op=ALU.max), waits=wl, inc=sV)
                    p.cemit("vector", lambda v, Lv=Lv, Bv=Bv: v.tensor_tensor(
                        out=Bv, in0=Lv[:, 0::2, :], in1=Lv[:, 1::2, :], op=ALU.min), sV)
                    p.cemit("vector", lambda v, Av=Av, Cv=Cv: v.tensor_tensor(
                        out=Cv, in0=Av[:, 0::2, :], in1=Av[:, 1::2, :], op=ALU.max), sV)
                    p.cemit("vector", lambda v, Av=Av, Dv=Dv: v.tensor_tensor(
                        out=Dv, in0=Av[:, 0::2, :], in1=Av[:, 1::2, :], op=ALU.min), sV)
                    p.cemit("vector", lambda v, Bv=Bv, Bmv=Bmv: v.tensor_tensor(
                        out=Bmv, in0=Bv[:, 0::2, :], in1=Bv[:, 1::2, :], op=ALU.max), sV)
                    p.cemit("vector", lambda v, Cv=Cv, par=par: v.tensor_tensor(
                        out=sm1[par][:], in0=Cv[:, 0, :], in1=Cv[:, 1, :], op=ALU.max), sV)
                    p.cemit("vector", lambda v, Cv=Cv: v.tensor_tensor(
                        out=sEg[:], in0=Cv[:, 0, :], in1=Cv[:, 1, :], op=ALU.min), sV)
                    p.cemit("vector", lambda v, Dv=Dv: v.tensor_tensor(
                        out=sDm[:], in0=Dv[:, 0, :], in1=Dv[:, 1, :], op=ALU.max), sV)
                    p.cemit("vector", lambda v, Bmv=Bmv: v.tensor_tensor(
                        out=sB2[:], in0=Bmv[:, 0, :], in1=Bmv[:, 1, :], op=ALU.max), sV)
                    p.cemit("vector", lambda v, par=par: v.tensor_tensor(
                        out=sm2[par][:], in0=sB2[:], in1=sDm[:], op=ALU.max), sV)
                    p.cemit("vector", lambda v, par=par: v.tensor_tensor(
                        out=sm2[par][:], in0=sm2[par][:], in1=sEg[:], op=ALU.max), sV)
                    p.cemit("vector", lambda v, par=par: v.tensor_tensor(
                        out=ssum[par][:], in0=sm1[par][:], in1=sm2[par][:], op=ALU.add), sV)
                    m2v = sm2[par][:].rearrange("p (one g) -> p one g", one=1)
                    sv = ssum[par][:].rearrange("p (one g) -> p one g", one=1)
                    m2bc, Lbc = bass.broadcast_tensor_aps(m2v, Lv)
                    p.cemit("vector", lambda v, Lbc=Lbc, m2bc=m2bc, par=par: v.tensor_tensor(
                        out=ab[par][:].rearrange("p (e g) -> p e g", g=G),
                        in0=Lbc, in1=m2bc, op=ALU.is_ge), sV)
                    sbc, _ = bass.broadcast_tensor_aps(sv, Lv)
                    wd2 = [(sV, sV.n)]
                    if tile_idx >= 2:
                        wd2.append((sA, a_sg[tile_idx - 2]))
                    p.emit("vector", lambda v, Lv=Lv, sbc=sbc, par=par: v.scalar_tensor_tensor(
                        out=d2b[par][:].rearrange("p (e g) -> p e g", g=G),
                        in0=Lv, scalar=2.0, in1=sbc, op0=ALU.mult, op1=ALU.subtract),
                        waits=wd2, inc=sV)
                    v_d2 = sV.n
                    v_d2_hist[tile_idx] = v_d2
                    wsg = [(sV, v_d2)]
                    if tile_idx >= 2:
                        wsg.append((sV, v_w_hist[tile_idx - 2]))
                    a_sg[tile_idx] = p.emit("scalar", lambda s, par=par: s.activation(
                        out=sgb[par][:], in_=d2b[par][:], func=AF.Sigmoid),
                        waits=wsg, inc=sA)
                    if DEBUG and tile_idx == 0:
                        p.emit("sync", lambda h: h.dma_start(out=dbg["lg0"], in_=lgb[0][:]),
                               waits=[(sV, v_d2)], inc=sDX, inck=16)
                        p.emit("sync", lambda h: h.dma_start(out=dbg["ab0"], in_=ab[0][:]),
                               inc=sDX, inck=16)
                        p.emit("sync", lambda h: h.dma_start(out=dbg["m20"], in_=sm2[0][:]),
                               inc=sDX, inck=16)
                        p.emit("sync", lambda h: h.dma_start(out=dbg["sg0"], in_=sgb[0][:]),
                               waits=[(sA, a_sg[0])], inc=sDX, inck=16)
                    def fin_tile(ti):
                        par2 = ti % 2
                        tcl = ti % NTC
                        p.cemit("vector", lambda v, par2=par2: v.tensor_tensor(
                            out=wb[par2][:], in0=ab[par2][:], in1=sgb[par2][:], op=ALU.mult),
                            sV, waits=[(sA, a_sg[ti])])
                        if DEBUG and ti == 0:
                            p.emit("sync", lambda h: h.dma_start(out=dbg["wb0"], in_=wb[0][:]),
                                   waits=[(sV, sV.n)], inc=sDX, inck=16)
                        p.cemit("vector", lambda v, par2=par2: v.reduce_sum(
                            out=crd[:], in_=wb[par2][:].rearrange("p (e g) -> p e g", g=G), axis=AX.X), sV)
                        if DEBUG and ti == 0:
                            p.emit("sync", lambda h: h.dma_start(out=dbg["crd0"], in_=crd[:]),
                                   waits=[(sV, sV.n)], inc=sDX, inck=16)
                            dbg_t0 = sDX.n
                        wca = [(sG, g_czero)] if ti < NTC else []
                        if DEBUG and ti == 0:
                            wca.append((sDX, dbg_t0))
                        p.cemit("vector", lambda v, tcl=tcl: v.tensor_add(
                            out=c_nat[:, tcl * 8:(tcl + 1) * 8], in0=c_nat[:, tcl * 8:(tcl + 1) * 8],
                            in1=crd[:]), sV, waits=wca)
                        v_w_hist[ti] = sV.n
                    if tile_idx >= 1:
                        fin_tile(tile_idx - 1)
                    tile_idx += 1
                if oe + 1 < NOE2:
                    w_rwt = []
                    if oe >= 1:
                        w_rwt.append((sT, t_lg[oe * NTC - 1]))
                    p.emit("gpsimd", lambda g, oe=oe: g.dma_start(
                        out=rbb[(oe + 1) % 2][:], in_=rb_d[:, (oe + 1) * RW:(oe + 2) * RW]),
                        waits=w_rwt, inc=sDW, inck=16)
                    for dc2 in range(NDC):
                        p.emit("gpsimd", lambda g, dc=dc2, oe=oe: g.dma_start(
                            out=rwt[(oe + 1) % 2][:, dc * RW:(dc + 1) * RW],
                            in_=rwt_d[dc * 128:(dc + 1) * 128, (oe + 1) * RW:(oe + 2) * RW]),
                            inc=sDW, inck=16)
                    dw_rw[oe + 1] = sDW.n
            fin_tile(tile_idx - 1)
            v_comb_done = v_w_hist[tile_idx - 1]
            p.emit("gpsimd", lambda g: g.dma_start(out=ewt[0][:], in_=ew_d[0]),
                   waits=[(sT, t_last_router_mm)], inc=sDW, inck=16)
            dw_ew0 = sDW.n
            if DEBUG:
                p.emit("sync", lambda h: h.dma_start(out=dbg["c"], in_=c_nat[:]),
                       waits=[(sV, v_comb_done)], inc=sDX, inck=16)

        p.mark("router")
        ph = ExitStack()
        with ph:
            cTb = ph.enter_context(nc.sbuf_tensor("cTb", [8, T], BF16))
            cb = [ph.enter_context(nc.sbuf_tensor(f"cb{i}", [128, T], BF16)) for i in range(8)]
            facc = ph.enter_context(nc.sbuf_tensor("facc", [128, D], F32))
            tmpb = [ph.enter_context(nc.sbuf_tensor(f"tmpb{i}", [128, 1024], BF16)) for i in range(2)]
            eob = [ph.enter_context(nc.sbuf_tensor(f"eob{i}", [128, 1024], BF16)) for i in range(2)]
            fTb = ph.enter_context(nc.sbuf_tensor("fTb", [128, D], BF16))
            fst = [ph.enter_context(nc.sbuf_tensor(f"fst{i}", [128, D], BF16)) for i in range(2)]
            eoH = [ph.enter_context(nc.psum_tensor(f"eoH{i}", [128, 1024], F32)) for i in range(3)]
            trPf = ph.enter_context(nc.psum_tensor("trPf", [128, T], BF16))

            t_ctr = [0] * NTC
            for tc in range(NTC):
                reg = eoH[0][0:8, (tc % 2) * 512:(tc % 2) * 512 + 128]
                p.emit("tensor", lambda t, tc=tc, reg=reg: t.transpose(
                    out=reg, in_=c_nat[:, tc * 8:(tc + 1) * 8], identity=ident_f[:]),
                    waits=[(sV, v_comb_done)] if tc == 0 else
                          ([(sA, t_ctr[tc - 2])] if tc >= 2 else ()),
                    inc=sT)
                t_now = sT.n
                t_ctr[tc] = p.emit("scalar", lambda s, tc=tc, reg=reg: s.activation(
                    out=cTb[:, tc * 128:(tc + 1) * 128], in_=reg, func=AF.Copy),
                    waits=[(sT, t_now)], inc=sA)
            p.emit("vector", lambda v: v.engine_nop(), waits=[(sA, t_ctr[NTC - 1])], inc=sV)
            v_ctb = sV.n
            a_cb = [0] * 8
            for e in range(8):
                for half in range(2):
                    pse = eoH[1 + half]
                    for q in range(2):
                        p.emit("tensor", lambda t, e=e, half=half, q=q, pse=pse: t.matmul(
                            out=pse[:, q * 512:(q + 1) * 512], lhsT=sel[:, e * 128:(e + 1) * 128],
                            rhs=cTb[:, half * 1024 + q * 512:half * 1024 + (q + 1) * 512],
                            start=True, stop=True),
                            waits=[(sV, v_ctb)] if (e == 0 and half == 0 and q == 0) else
                                  ([(sA, a_cb[e - 1])] if (half == 0 and q == 0 and e >= 1) else ()),
                            inc=sT if q == 1 else None)
                    t_cb = sT.n
                    p.emit("scalar", lambda s, e=e, half=half, pse=pse: s.activation(
                        out=cb[e][:, half * 1024:(half + 1) * 1024], in_=pse[:], func=AF.Copy),
                        waits=[(sT, t_cb)], inc=sA)
                a_cb[e] = sA.n
            a_cb_done = sA.n

            dw_ew = [0] * NOC
            dw_ew[0] = dw_ew0

            a_fstg = [0] * NOC
            g_add_hist = []
            v_tmp_hist = []
            a_eo_hist = []
            t_oc_end = [0] * NOC
            v_ft_l = [0] * NOC
            dx_fst = [0] * NOC
            for oc in range(NOC):
                eb_ = ewt[oc % 2]
                fp = 0
                w_bias = [(sA, a_cb_done), (sDW, dw_ew[oc]), (sDW, dw_rbeb)]
                if oc >= 1:
                    w_bias.append((sV, v_ft_l[oc - 1]))
                for half in range(2):
                    pse = eoH[half]
                    for q in range(2):
                        p.emit("tensor", lambda t, oc=oc, half=half, q=q, pse=pse: t.matmul(
                            out=pse[:, q * 512:(q + 1) * 512], lhsT=eb_s[:, oc * 128:(oc + 1) * 128],
                            rhs=cTb[:, half * 1024 + q * 512:half * 1024 + (q + 1) * 512],
                            start=True, stop=True),
                            waits=(w_bias if (half == 0 and q == 0) else ()),
                            inc=sT if q == 1 else None)
                    t_b = sT.n
                    p.emit("scalar", lambda s, half=half, pse=pse: s.activation(
                        out=facc[:, half * 1024:(half + 1) * 1024], in_=pse[:], func=AF.Copy),
                        waits=[(sT, t_b)], inc=sA)
                a_binit = sA.n
                if oc + 1 < NOC:
                    w_ew = [(sT, t_oc_end[oc - 1])] if oc >= 1 else []
                    p.emit("gpsimd", lambda g, oc=oc: g.dma_start(
                        out=ewt[(oc + 1) % 2][:], in_=ew_d[oc + 1]),
                        waits=w_ew, inc=sDW, inck=16)
                    dw_ew[oc + 1] = sDW.n
                hk = 0
                for e in range(8):
                    for half in range(2):
                        pse = eoH[hk % 3]
                        w0 = []
                        if hk >= 3:
                            w0.append((sA, a_eo_hist[-3]))
                        if e == 0 and half == 0:
                            w0.append((sA, a_binit))
                        for dc in range(NDC):
                            for q in range(2):
                                p.emit("tensor", lambda t, pse=pse, eb_=eb_, e=e, dc=dc, half=half, q=q:
                                       t.matmul(out=pse[:, q * 512:(q + 1) * 512],
                                                lhsT=eb_[:, (e * NDC + dc) * 128:(e * NDC + dc + 1) * 128],
                                                rhs=l2T[:, dc * T + half * 1024 + q * 512:dc * T + half * 1024 + (q + 1) * 512],
                                                start=(dc == 0), stop=(dc == NDC - 1)),
                                       waits=w0 if (dc == 0 and q == 0) else (),
                                       inc=sT if (dc == NDC - 1 and q == 1) else None)
                        t_eo = sT.n
                        eb2 = eob[hk % 2]
                        p.emit("scalar", lambda s, pse=pse, eb2=eb2: s.activation(
                            out=eb2[:], in_=pse[:], func=AF.Copy),
                            waits=[(sT, t_eo), (sV, v_tmp_hist[-2] if len(v_tmp_hist) >= 2 else 0)],
                            inc=sA)
                        a_eo = sA.n
                        tb = tmpb[hk % 2]
                        wv = [(sA, a_eo), (sA, a_cb_done)]
                        if len(g_add_hist) >= 2:
                            wv.append((sG, g_add_hist[-2]))
                        p.emit("vector", lambda v, eb2=eb2, tb=tb, e=e, half=half: v.tensor_mul(
                            out=tb[:], in0=eb2[:], in1=cb[e][:, half * 1024:(half + 1) * 1024]),
                            waits=wv, inc=sV)
                        v_tmp_hist.append(sV.n)
                        a_eo_hist.append(a_eo)
                        p.emit("gpsimd", lambda g, tb=tb, half=half: g.tensor_tensor(
                            out=facc[:, half * 1024:(half + 1) * 1024],
                            in0=facc[:, half * 1024:(half + 1) * 1024],
                            in1=tb[:], op=ALU.add),
                            waits=[(sV, v_tmp_hist[-1])], inc=sG)
                        g_add_hist.append(sG.n)
                        hk += 1
                g_oc = sG.n
                t_oc_end[oc] = sT.n
                p.emit("vector", lambda v: v.tensor_copy(out=fTb[:], in_=facc[:]),
                       waits=[(sG, g_oc)], inc=sV)
                v_ft = sV.n
                v_ft_l[oc] = v_ft
                for j in range(NTC):
                    p.emit("tensor", lambda t, j=j: t.transpose(
                        out=trPf[:, j * 128:(j + 1) * 128], in_=fTb[:, j * 128:(j + 1) * 128],
                        identity=ident_b[:]),
                        waits=[(sV, v_ft), (sA, a_fstg[oc - 1] if oc >= 1 else 0)] if j == 0 else (),
                        inc=sT if j == NTC - 1 else None)
                t_trf = sT.n
                stg = fst[oc % 2]
                wfs = [(sT, t_trf)]
                if oc >= 2:
                    wfs.append((sDX, dx_fst[oc - 2]))
                a_fstg[oc] = p.emit("scalar", lambda s, stg=stg: s.activation(
                    out=stg[:], in_=trPf[:], func=AF.Copy),
                    waits=wfs, inc=sA)
                for j2 in range(2):
                    p.emit("sync", lambda h, stg=stg, oc=oc, j2=j2: h.dma_start(
                        out=fin_d[:, :, oc * 128:(oc + 1) * 128].transpose([1, 0, 2])[:, j2 * 8:(j2 + 1) * 8, :],
                        in_=stg[:].rearrange("p (j c) -> p j c", c=128)[:, j2 * 8:(j2 + 1) * 8, :]),
                        waits=[(sA, a_fstg[oc])], inc=sDX, inck=16)
                dx_fst[oc] = sDX.n
            dx_fin = sDX.n
            if DEBUG:
                for tc in range(NTC):
                    p.emit("sync", lambda h, tc=tc: h.dma_start(out=dbg["fin"][tc], in_=fin_d[tc]),
                           waits=[(sDX, dx_fin)], inc=sDX, inck=16)

        p.mark("expert")
        ph = ExitStack()
        with ph:
            fnb = [ph.enter_context(nc.sbuf_tensor(f"fnb{i}", [128, D], BF16)) for i in range(2)]
            fx1 = [ph.enter_context(nc.sbuf_tensor(f"fx1{i}", [128, D], F32)) for i in range(2)]
            x2c = [ph.enter_context(nc.sbuf_tensor(f"x2c{i}", [128, D], F32)) for i in range(2)]
            sqs3 = ph.enter_context(nc.sbuf_tensor("sqs3", [128, D], BF16))
            fstt_t = ph.enter_context(nc.sbuf_tensor("fstt", [128, 24], F32))

            dx_fn = [0] * NTC
            dx_fx = [0] * NTC
            for tc in range(2):
                dx_fn[tc] = p.emit("sync", lambda h, tc=tc: h.dma_start(
                    out=fnb[tc % 2][:], in_=fin_d[tc]), waits=[(sDX, dx_fin)], inc=sDX, inck=16)
                dx_fx[tc] = p.emit("sync", lambda h, tc=tc: h.dma_start(
                    out=fx1[tc % 2][:], in_=x1_d[tc]), inc=sDX, inck=16)
            v_x2 = [0] * NTC
            v_f1_l = [0] * NTC
            for tc in range(NTC):
                fstt = fstt_t[:, (tc % 2) * 12:(tc % 2) * 12 + 12]
                fn = fnb[tc % 2]
                fx = fx1[tc % 2]
                xo = x2c[tc % 2]
                p.emit("scalar", lambda s, fn=fn, fstt=fstt: s.activation(
                    out=sqs3[:], in_=fn[:], func=AF.Square, accum_out=fstt[:, 0:1]),
                    waits=[(sDX, dx_fn[tc]),
                           (sV, v_f1_l[tc - 2] if tc >= 2 else 0)], inc=sA)
                a_s3 = sA.n
                p.cemit("vector", lambda v, fn=fn, fstt=fstt: v.reduce_sum(out=fstt[:, 1:2], in_=fn[:], axis=AX.X),
                        sV, waits=[(sA, a_s3)])
                p.cemit("vector", lambda v, fstt=fstt: v.tensor_scalar(out=fstt[:, 2:3], in0=fstt[:, 1:2],
                                                            scalar1=1.0 / D, scalar2=None, op0=ALU.mult), sV)
                p.cemit("vector", lambda v, fstt=fstt: v.tensor_tensor(out=fstt[:, 3:4], in0=fstt[:, 2:3], in1=fstt[:, 2:3], op=ALU.mult), sV)
                p.cemit("vector", lambda v, fstt=fstt: v.tensor_scalar(out=fstt[:, 4:5], in0=fstt[:, 0:1],
                                                            scalar1=1.0 / D, scalar2=fstt[:, 3:4],
                                                            op0=ALU.mult, op1=ALU.subtract), sV)
                p.cemit("vector", lambda v, fstt=fstt: v.tensor_scalar(out=fstt[:, 4:5], in0=fstt[:, 4:5],
                                                            scalar1=EPS, scalar2=None, op0=ALU.add), sV)
                p.cemit("vector", lambda v, fstt=fstt: v.reciprocal(out=fstt[:, 5:6], in_=fstt[:, 4:5]), sV)
                v_f1 = sV.n
                v_f1_l[tc] = v_f1
                p.emit("scalar", lambda s, fstt=fstt: s.activation(out=fstt[:, 6:7], in_=fstt[:, 5:6], func=AF.Sqrt),
                       waits=[(sV, v_f1)], inc=sA)
                a_f1 = sA.n
                p.cemit("vector", lambda v, fstt=fstt: v.tensor_tensor(out=fstt[:, 7:8], in0=fstt[:, 2:3], in1=fstt[:, 6:7], op=ALU.mult),
                        sV, waits=[(sA, a_f1), (sDX, dx_fx[tc])])
                p.cemit("vector", lambda v, fx=fx, fstt=fstt: v.tensor_scalar(out=fx[:], in0=fx[:],
                                                                   scalar1=fstt[:, 7:8], scalar2=None,
                                                                   op0=ALU.subtract), sV)
                p.cemit("vector", lambda v, fn=fn, fx=fx, xo=xo, fstt=fstt: v.scalar_tensor_tensor(
                    out=xo[:], in0=fn[:], scalar=fstt[:, 6:7], in1=fx[:],
                    op0=ALU.mult, op1=ALU.add),
                    sV, waits=[(sO, (tc - 2 + 1) * 16) if tc >= 2 else (sO, 0)])
                v_x2[tc] = sV.n
                p.emit("sync", lambda h, xo=xo, tc=tc: h.dma_start(
                    out=out_d[tc * 128:(tc + 1) * 128, :], in_=xo[:]),
                    waits=[(sV, v_x2[tc])], inc=sO, inck=16)
                if tc + 2 < NTC:
                    dx_fn[tc + 2] = p.emit("sync", lambda h, tc=tc: h.dma_start(
                        out=fnb[tc % 2][:], in_=fin_d[tc + 2]), waits=[(sV, v_x2[tc])], inc=sDX, inck=16)
                    dx_fx[tc + 2] = p.emit("sync", lambda h, tc=tc: h.dma_start(
                        out=fx1[tc % 2][:], in_=x1_d[tc + 2]),
                        waits=[(sV, v_x2[tc])], inc=sDX, inck=16)
            p.wait("sync", (sO, NTC * 16))

        p.mark("final")
        if phase_limit is not None:
            p.truncate(phase_limit)
        p.check_schedule()
        with nc.Block() as block:
            @block.sync
            def _(h):
                for fn in p.engs["sync"]:
                    fn(h)

            @block.gpsimd
            def _(h):
                for fn in p.engs["gpsimd"]:
                    fn(h)

            @block.tensor
            def _(h):
                for fn in p.engs["tensor"]:
                    fn(h)

            @block.vector
            def _(h):
                for fn in p.engs["vector"]:
                    fn(h)

            @block.scalar
            def _(h):
                for fn in p.engs["scalar"]:
                    fn(h)

    return nc


_NC_CACHE = None


def _get_nc():
    global _NC_CACHE
    if _NC_CACHE is None:
        _NC_CACHE = build_core_program()
    return _NC_CACHE


def _prep_shared(L0_attn_W_attn, L0_attn_W_proj, L0_attn_lambda_q1, L0_attn_lambda_k1,
                 L0_attn_lambda_q2, L0_attn_lambda_k2, L0_ffn_router_w, L0_ffn_router_b,
                 L0_ffn_expert_w, L0_ffn_expert_b):
    C = np.ascontiguousarray
    wa = np.asarray(L0_attn_W_attn, dtype=np.float32)
    cols = []
    for pr in range(NPAIR):
        cols.append(np.concatenate([
            np.arange(128 * pr, 128 * (pr + 1)),
            1024 + np.arange(128 * pr, 128 * (pr + 1)),
            2048 + np.arange(128 * pr, 128 * (pr + 1)),
            3072 + np.arange(128 * pr, 128 * (pr + 1)),
            4096 + np.arange(256 * pr, 256 * (pr + 1)),
        ]))
    wa_re = np.stack([wa[:, c] for c in cols], axis=0)
    wa_re = C(wa_re.reshape(NPAIR, NDC, 128, 768))
    wp_re = C(np.asarray(L0_attn_W_proj, dtype=np.float32).reshape(NDC, 128, D))
    rwt = np.asarray(L0_ffn_router_w, dtype=np.float32).T
    rwt = C(rwt.reshape(D, 16, 128, 8).transpose(0, 1, 3, 2).reshape(D, 16384))
    rb = np.asarray(L0_ffn_router_b, dtype=np.float32)
    rb = C(rb.reshape(16, 128, 8).transpose(0, 2, 1).reshape(1, 16384))
    ew = np.asarray(L0_ffn_expert_w, dtype=np.float32)
    ew_re = C(ew.reshape(8, NOC, 128, NDC, 128).transpose(1, 4, 0, 3, 2))
    ebb = C(np.asarray(L0_ffn_expert_b, dtype=np.float32))
    lam = lambda x: C(np.asarray(x, dtype=np.float32).reshape(128, 16))
    return {
        "wa": wa_re, "wp": wp_re, "rwt": rwt, "rb": rb, "ew": ew_re, "ebb": ebb,
        "lq1": lam(L0_attn_lambda_q1), "lk1": lam(L0_attn_lambda_k1),
        "lq2": lam(L0_attn_lambda_q2), "lk2": lam(L0_attn_lambda_k2),
    }


def kernel(L0_attn_x, L0_attn_W_attn, L0_attn_W_proj, L0_attn_lambda_q1, L0_attn_lambda_k1,
           L0_attn_lambda_q2, L0_attn_lambda_k2, L0_ffn_router_w, L0_ffn_router_b,
           L0_ffn_expert_w, L0_ffn_expert_b):
    x = np.ascontiguousarray(np.asarray(L0_attn_x, dtype=np.float32))
    shared = _prep_shared(L0_attn_W_attn, L0_attn_W_proj, L0_attn_lambda_q1, L0_attn_lambda_k1,
                          L0_attn_lambda_q2, L0_attn_lambda_k2, L0_ffn_router_w, L0_ffn_router_b,
                          L0_ffn_expert_w, L0_ffn_expert_b)
    nc = _get_nc()
    in_maps = []
    for c in range(8):
        m = {"x0": np.ascontiguousarray(x[c])}
        m.update(shared)
        in_maps.append(m)
    res = run_bass_kernel_spmd(nc, in_maps, core_ids=list(range(8)))
    out = np.stack([np.asarray(res.results[c]["out"]) for c in range(8)], axis=0)
    return out.astype(np.float32)


if __name__ == "__main__":
    rng = np.random.default_rng(0)
    args = {
        "L0_attn_x": rng.standard_normal((8, T, D), dtype=np.float32),
        "L0_attn_W_attn": rng.standard_normal((D, 3 * D), dtype=np.float32) * 0.02,
        "L0_attn_W_proj": rng.standard_normal((D, D), dtype=np.float32) * 0.02,
        "L0_attn_lambda_q1": rng.standard_normal(D).astype(np.float32) * 0.1,
        "L0_attn_lambda_k1": rng.standard_normal(D).astype(np.float32) * 0.1,
        "L0_attn_lambda_q2": rng.standard_normal(D).astype(np.float32) * 0.1,
        "L0_attn_lambda_k2": rng.standard_normal(D).astype(np.float32) * 0.1,
        "L0_ffn_router_w": rng.standard_normal((16384, D), dtype=np.float32) * 0.02,
        "L0_ffn_router_b": rng.standard_normal(16384).astype(np.float32) * 0.02,
        "L0_ffn_expert_w": rng.standard_normal((8, D, D), dtype=np.float32) * 0.02,
        "L0_ffn_expert_b": rng.standard_normal((8, D)).astype(np.float32) * 0.02,
    }
    o = kernel(**args)
    print("kernel ran, out:", o.shape, o.dtype, np.abs(o).mean())


# version 16
# speedup vs baseline: 1.0485x; 1.0485x over previous
import math
import numpy as np
from contextlib import ExitStack

import concourse.bass as bass
import concourse.mybir as mybir
from concourse.bass_utils import run_bass_kernel_spmd

F32 = mybir.dt.float32
BF16 = mybir.dt.bfloat16
AF = mybir.ActivationFunctionType
ALU = mybir.AluOpType
AX = mybir.AxisListType

T = 2048
D = 2048
NH = 16
HD = 64
DV = 128
NTC = 16
NDC = 16
NPAIR = 8
NOC = 16
NOEC = 16
EPS = 1e-5
LAMBDA_INIT = 0.8 - 0.6 * math.exp(-0.3 * 0.0)
SCALE = HD ** (-0.5)
DEBUG = False
DEBUG2 = False
SCRATCH_OUT = False


class Sem:
    __slots__ = ("h", "n", "name")

    def __init__(self, name):
        self.h = None
        self.n = 0
        self.name = name


class Plan:
    def __init__(self):
        self.engs = {"sync": [], "gpsimd": [], "tensor": [], "vector": [], "scalar": []}
        self.meta = {k: [] for k in self.engs}

    def emit(self, eng, fn, waits=(), inc=None, inck=1):
        if inc is not None:
            inc.n += inck
        waits = tuple((s, v) for (s, v) in waits)

        def thunk(h, fn=fn, waits=waits, inc=inc, inck=inck):
            for s, v in waits:
                if v > 0:
                    h.wait_ge(s.h, v)
            ins = fn(h)
            if inc is not None:
                assert ins is not None
                ins.then_inc(inc.h, inck)

        self.engs[eng].append(thunk)
        self.meta[eng].append((waits, inc, inck))
        return inc.n if inc is not None else None

    def wait(self, eng, *pairs):
        pairs = tuple(pairs)

        def thunk(h, pairs=pairs):
            for s, v in pairs:
                if v > 0:
                    h.wait_ge(s.h, v)

        self.engs[eng].append(thunk)
        self.meta[eng].append((pairs, None, 0))

    def cemit(self, eng, fn, sem, waits=()):
        w = [(sem, sem.n)] + list(waits)
        return self.emit(eng, fn, waits=w, inc=sem)

    def mark(self, name):
        if not hasattr(self, "marks"):
            self.marks = []
        self.marks.append((name, {k: len(v) for k, v in self.engs.items()}))

    def truncate(self, name):
        snap = dict(self.marks)[name]
        for k in self.engs:
            self.engs[k] = self.engs[k][:snap[k]]
            self.meta[k] = self.meta[k][:snap[k]]

    def check_schedule(self):
        pos = {k: 0 for k in self.engs}
        counts = {}
        for k, ms in self.meta.items():
            for (waits, inc, inck) in ms:
                for s, v in waits:
                    counts.setdefault(s.name, 0)
                if inc is not None:
                    counts.setdefault(inc.name, 0)
        progressed = True
        while progressed:
            progressed = False
            for k, ms in self.meta.items():
                while pos[k] < len(ms):
                    waits, inc, inck = ms[pos[k]]
                    if all(counts.get(s.name, 0) >= v for s, v in waits):
                        if inc is not None:
                            counts[inc.name] = counts.get(inc.name, 0) + inck
                        pos[k] += 1
                        progressed = True
                    else:
                        break
        stuck = {k: pos[k] for k in self.engs if pos[k] < len(self.meta[k])}
        if stuck:
            lines = []
            for k, i in stuck.items():
                waits, inc, inck = self.meta[k][i]
                unmet = [(s.name, v, counts.get(s.name, 0)) for s, v in waits
                         if counts.get(s.name, 0) < v]
                lines.append(f"{k}[{i}/{len(self.meta[k])}] blocked on {unmet}")
            raise RuntimeError("DEADLOCK:\n" + "\n".join(lines))
        return counts


def build_core_program(phase_limit=None):
    nc = bass.Bass("TRN2", target_bir_lowering=False)
    p = Plan()

    x0_d = nc.dram_tensor("x0", [T, D], F32, kind="ExternalInput").ap()
    wa_d = nc.dram_tensor("wa", [NPAIR, NDC, 128, 768], F32, kind="ExternalInput").ap()
    wp_d = nc.dram_tensor("wp", [NDC, 128, D], F32, kind="ExternalInput").ap()
    lq1_d = nc.dram_tensor("lq1", [128, 16], F32, kind="ExternalInput").ap()
    lk1_d = nc.dram_tensor("lk1", [128, 16], F32, kind="ExternalInput").ap()
    lq2_d = nc.dram_tensor("lq2", [128, 16], F32, kind="ExternalInput").ap()
    lk2_d = nc.dram_tensor("lk2", [128, 16], F32, kind="ExternalInput").ap()
    rwt_d = nc.dram_tensor("rwt", [D, 16384], F32, kind="ExternalInput").ap()
    rb_d = nc.dram_tensor("rb", [1, 16384], F32, kind="ExternalInput").ap()
    ew_d = nc.dram_tensor("ew", [NOC, 128, 8, NDC, 128], F32, kind="ExternalInput").ap()
    ebb_d = nc.dram_tensor("ebb", [8, D], F32, kind="ExternalInput").ap()
    out_d = nc.dram_tensor("out", [T, D], F32, kind="ExternalOutput").ap()

    scrkind = "ExternalOutput" if SCRATCH_OUT else "Internal"
    x1_d = nc.dram_tensor("x1scr", [NTC, 128, D], F32, kind=scrkind).ap()
    l2t_d = nc.dram_tensor("l2tscr", [NDC, 128, T], BF16, kind=scrkind).ap()
    fin_d = nc.dram_tensor("finscr", [NTC, 128, D], BF16, kind=scrkind).ap()
    dbg = {}
    if DEBUG2:
        dbg["outt2"] = nc.dram_tensor("dbg_outt2", [128, NH * T], BF16, kind="ExternalOutput").ap()
    if DEBUG:
        dbg["ln1t"] = nc.dram_tensor("dbg_ln1t", [128, NDC * T], BF16, kind="ExternalOutput").ap()
        dbg["outt"] = nc.dram_tensor("dbg_outt", [128, NH * T], BF16, kind="ExternalOutput").ap()
        dbg["c"] = nc.dram_tensor("dbg_c", [128, 128], F32, kind="ExternalOutput").ap()
        dbg["x1"] = nc.dram_tensor("dbg_x1", [NTC, 128, D], F32, kind="ExternalOutput").ap()
        dbg["l2t"] = nc.dram_tensor("dbg_l2t", [NDC, 128, T], BF16, kind="ExternalOutput").ap()
        dbg["fin"] = nc.dram_tensor("dbg_fin", [NTC, 128, D], BF16, kind="ExternalOutput").ap()
        dbg["den"] = nc.dram_tensor("dbg_den", [NH, 2, 1, 1024], F32, kind="ExternalOutput").ap()
        dbg["stt"] = nc.dram_tensor("dbg_stt", [NTC, 128, 8], F32, kind="ExternalOutput").ap()
        dbg["lg0"] = nc.dram_tensor("dbg_lg0", [128, 1024], F32, kind="ExternalOutput").ap()
        dbg["ab0"] = nc.dram_tensor("dbg_ab0", [128, 1024], BF16, kind="ExternalOutput").ap()
        dbg["sg0"] = nc.dram_tensor("dbg_sg0", [128, 1024], BF16, kind="ExternalOutput").ap()
        dbg["wb0"] = nc.dram_tensor("dbg_wb0", [128, 1024], BF16, kind="ExternalOutput").ap()
        dbg["m20"] = nc.dram_tensor("dbg_m20", [128, 128], F32, kind="ExternalOutput").ap()
        dbg["crd0"] = nc.dram_tensor("dbg_crd0", [128, 8], F32, kind="ExternalOutput").ap()

    sDX = Sem("sDX")
    sDW = Sem("sDW")
    sT = Sem("sT")
    sA = Sem("sA")
    sV = Sem("sV")
    sG = Sem("sG")
    sO = Sem("sO")
    sems = [sDX, sDW, sT, sA, sV, sG, sO]

    st = ExitStack()
    SB = lambda name, shape, dt: st.enter_context(nc.sbuf_tensor(name, shape, dt))
    PS = lambda name, shape, dt: st.enter_context(nc.psum_tensor(name, shape, dt))

    with st:
        for s in sems:
            s.h = st.enter_context(nc.semaphore(s.name))

        ones_k = SB("ones_k", [128, 1], BF16)
        onesr_b = SB("onesr_b", [1, 128], BF16)
        onesr_f = SB("onesr_f", [1, 128], F32)
        onesr08 = SB("onesr08", [1, 128], F32)
        ident_b = SB("ident_b", [128, 128], BF16)
        ident_f = SB("ident_f", [128, 128], F32)
        sel = SB("sel", [8, 8 * 128], BF16)
        lamneg = SB("lamneg", [128, 1], F32)
        lq1 = SB("lq1_s", [128, 16], F32)
        lk1 = SB("lk1_s", [128, 16], F32)
        lq2 = SB("lq2_s", [128, 16], F32)
        lk2 = SB("lk2_s", [128, 16], F32)
        lam_sc = SB("lam_sc", [1, 4], F32)
        eb_s = SB("eb_s", [8, D], BF16)

        p.emit("gpsimd", lambda g: g.memset(ones_k[:], 1.0))
        p.emit("gpsimd", lambda g: g.memset(onesr_b[:], 1.0))
        p.emit("gpsimd", lambda g: g.memset(onesr_f[:], 1.0))
        p.emit("gpsimd", lambda g: g.memset(onesr08[:], 1.0 - LAMBDA_INIT))
        p.emit("gpsimd", lambda g: g.memset(ident_b[:], 0.0))
        p.emit("gpsimd", lambda g: g.affine_select(
            out=ident_b[:], in_=ident_b[:], compare_op=ALU.not_equal, fill=1.0,
            base=0, pattern=[[-1, 128]], channel_multiplier=1))
        p.emit("gpsimd", lambda g: g.memset(ident_f[:], 0.0))
        p.emit("gpsimd", lambda g: g.affine_select(
            out=ident_f[:], in_=ident_f[:], compare_op=ALU.not_equal, fill=1.0,
            base=0, pattern=[[-1, 128]], channel_multiplier=1))
        p.emit("gpsimd", lambda g: g.memset(sel[:], 0.0))
        for e in range(8):
            p.emit("gpsimd", lambda g, e=e: g.affine_select(
                out=sel[:, e * 128:(e + 1) * 128], in_=sel[:, e * 128:(e + 1) * 128],
                compare_op=ALU.not_equal, fill=1.0, base=-e,
                pattern=[[0, 128]], channel_multiplier=1))
        g_consts = p.emit("gpsimd", lambda g: g.engine_nop(), inc=sG)

        for src, dst in [(lq1_d, lq1), (lk1_d, lk1), (lq2_d, lq2), (lk2_d, lk2)]:
            p.emit("sync", lambda h, s=src, d=dst: h.dma_start(out=d[:], in_=s), inc=sDX, inck=16)
        p.emit("gpsimd", lambda g: g.dma_start(out=eb_s[:], in_=ebb_d), inc=sDW, inck=16)
        dw_rbeb = sDW.n
        dx_lam = sDX.n

        ph = ExitStack()
        s_ln = ExitStack()
        s_ap = ExitStack()
        ln1T = s_ln.enter_context(nc.sbuf_tensor("ln1T", [128, NDC * T], BF16, side="right"))
        outT = s_ap.enter_context(nc.sbuf_tensor("outT", [128, NH * T], BF16))
        with ph:
            x0c = [ph.enter_context(nc.sbuf_tensor(f"x0c{i}", [128, D], F32)) for i in range(3)]
            ln1n = [ph.enter_context(nc.sbuf_tensor(f"ln1n{i}", [128, D], BF16)) for i in range(2)]
            sqs = ph.enter_context(nc.sbuf_tensor("sqs", [128, D], BF16))
            stt_t = ph.enter_context(nc.sbuf_tensor("stt", [128, 16], F32))
            trP = [ph.enter_context(nc.psum_tensor(f"trP{i}", [128, T], BF16)) for i in range(2)]
            lamP = [ph.enter_context(nc.psum_tensor(f"lamP{i}", [1, 1], F32)) for i in range(2)]
            repP = ph.enter_context(nc.psum_tensor("repP", [128, 1], F32))

            for j, (qa, ka, ps_) in enumerate([(lq1, lk1, lamP[0]), (lq2, lk2, lamP[1])]):
                for f in range(16):
                    p.emit("tensor", lambda t, qa=qa, ka=ka, ps_=ps_, f=f:
                           t.matmul(out=ps_[:], lhsT=qa[:, f:f + 1], rhs=ka[:, f:f + 1],
                                    start=(f == 0), stop=(f == 15)),
                           waits=[(sDX, dx_lam)] if f == 0 else (),
                           inc=sT if f == 15 else None)
            t_lamdots = sT.n
            p.emit("scalar", lambda s: s.activation(out=lam_sc[:, 0:1], in_=lamP[0][:], func=AF.Exp),
                   waits=[(sT, t_lamdots)])
            p.emit("scalar", lambda s: s.activation(out=lam_sc[:, 1:2], in_=lamP[1][:], func=AF.Exp),
                   inc=sA)
            a_lamexp = sA.n
            p.cemit("vector", lambda v: v.tensor_sub(out=lam_sc[:, 2:3], in0=lam_sc[:, 0:1], in1=lam_sc[:, 1:2]),
                    sV, waits=[(sA, a_lamexp)])
            p.cemit("vector", lambda v: v.tensor_scalar(out=lam_sc[:, 2:3], in0=lam_sc[:, 2:3],
                                                        scalar1=LAMBDA_INIT, scalar2=None, op0=ALU.add), sV)
            p.cemit("vector", lambda v: v.tensor_scalar(out=lam_sc[:, 3:4], in0=lam_sc[:, 2:3],
                                                        scalar1=-1.0, scalar2=None, op0=ALU.mult), sV)
            v_lam = sV.n
            p.emit("tensor", lambda t: t.matmul(out=repP[:], lhsT=onesr_f[:], rhs=lam_sc[:, 3:4],
                                                start=True, stop=True),
                   waits=[(sV, v_lam), (sG, g_consts)], inc=sT)
            t_rep = sT.n
            p.emit("scalar", lambda s: s.copy(out=lamneg[:], in_=repP[:]), waits=[(sT, t_rep)], inc=sA)
            a_lamneg = sA.n

            dx_x0 = [0] * NTC
            for tc in range(min(3, NTC)):
                dx_x0[tc] = p.emit("sync", lambda h, tc=tc: h.dma_start(
                    out=x0c[tc % 3][:], in_=x0_d[tc * 128:(tc + 1) * 128, :]), inc=sDX, inck=16)

            a_aff = [0] * NTC
            a_ev = [0] * NTC
            t_tr = [0] * NTC
            v_st_l = [0] * NTC
            dbg_sttd = 0
            for tc in range(NTC):
                xc = x0c[tc % 3]
                ln = ln1n[tc % 2]
                stt = stt_t[:, (tc % 2) * 8:(tc % 2) * 8 + 8]
                p.emit("scalar", lambda s, xc=xc, tc=tc, stt=stt: s.activation(
                    out=sqs[:], in_=xc[:], func=AF.Square, accum_out=stt[:, 0:1]),
                    waits=[(sDX, dx_x0[tc]),
                           (sV, v_st_l[tc - 2] if tc >= 2 else 0),
                           (sV, a_aff[tc - 2] if tc >= 2 else 0)], inc=sA)
                a_sq = sA.n
                p.cemit("vector", lambda v, xc=xc, stt=stt: v.reduce_sum(out=stt[:, 1:2], in_=xc[:], axis=AX.X),
                        sV, waits=[(sDX, dx_x0[tc])])
                p.cemit("vector", lambda v, stt=stt: v.tensor_scalar(out=stt[:, 2:3], in0=stt[:, 1:2],
                                                            scalar1=1.0 / D, scalar2=None, op0=ALU.mult), sV)
                p.cemit("vector", lambda v, stt=stt: v.tensor_tensor(out=stt[:, 3:4], in0=stt[:, 2:3], in1=stt[:, 2:3], op=ALU.mult), sV)
                p.cemit("vector", lambda v, stt=stt: v.tensor_scalar(out=stt[:, 4:5], in0=stt[:, 0:1],
                                                            scalar1=1.0 / D, scalar2=stt[:, 3:4],
                                                            op0=ALU.mult, op1=ALU.subtract), sV,
                        waits=[(sA, a_sq)])
                p.cemit("vector", lambda v, stt=stt: v.tensor_scalar(out=stt[:, 4:5], in0=stt[:, 4:5],
                                                            scalar1=EPS, scalar2=None, op0=ALU.add), sV)
                p.cemit("vector", lambda v, stt=stt: v.reciprocal(out=stt[:, 5:6], in_=stt[:, 4:5]), sV)
                v_st = sV.n
                v_st_l[tc] = v_st
                p.emit("scalar", lambda s, stt=stt: s.activation(out=stt[:, 6:7], in_=stt[:, 5:6], func=AF.Sqrt),
                       waits=[(sV, v_st)], inc=sA)
                a_rstd = sA.n
                a_aff[tc] = p.cemit("vector", lambda v, xc=xc, ln=ln, stt=stt: v.tensor_scalar(
                    out=ln[:], in0=xc[:], scalar1=stt[:, 2:3], scalar2=stt[:, 6:7],
                    op0=ALU.subtract, op1=ALU.mult),
                    sV, waits=[(sA, a_rstd), (sA, a_ev[tc - 2] if tc >= 2 else 0)])
                if DEBUG:
                    p.emit("sync", lambda h, tc=tc: h.dma_start(out=dbg["stt"][tc], in_=stt[:]),
                           waits=[(sV, a_aff[tc])], inc=sDX, inck=16)
                    dbg_sttd = sDX.n
                if tc + 3 < NTC:
                    dx_x0[tc + 3] = p.emit("sync", lambda h, tc=tc: h.dma_start(
                        out=x0c[(tc + 3) % 3][:], in_=x0_d[(tc + 3) * 128:(tc + 4) * 128, :]),
                        waits=[(sV, a_aff[tc])], inc=sDX, inck=16)
                tp = trP[tc % 2]
                for j in range(NDC):
                    p.emit("tensor", lambda t, tp=tp, ln=ln, j=j, tc=tc: t.transpose(
                        out=tp[:, j * 128:(j + 1) * 128], in_=ln[:, j * 128:(j + 1) * 128], identity=ident_b[:]),
                        waits=([(sG, g_consts), (sV, a_aff[tc]),
                                (sA, a_ev[tc - 2] if tc >= 2 else 0)] if j == 0 else ()),
                        inc=sT if j == NDC - 1 else None)
                t_tr[tc] = sT.n
                a_ev[tc] = p.emit("scalar", lambda s, tp=tp, tc=tc: s.activation(
                    out=ln1T[:].rearrange("p (j t) -> p j t", t=T)[:, :, tc * 128:(tc + 1) * 128],
                    in_=tp[:].rearrange("p (j c) -> p j c", c=128),
                    func=AF.Copy),
                    waits=[(sT, t_tr[tc])], inc=sA)
            a_ln1_done = sA.n
            if DEBUG:
                p.emit("sync", lambda h: h.dma_start(out=dbg["ln1t"], in_=ln1T[:]),
                       waits=[(sA, a_ln1_done)], inc=sDX, inck=16)

        p.mark("ln1")
        ph = ExitStack()
        with ph:
            wab = ph.enter_context(nc.sbuf_tensor("wab", [128, NDC * 768], BF16, side="right"))
            qstk = [ph.enter_context(nc.sbuf_tensor(f"qstk{i}", [128, T], BF16)) for i in range(2)]
            kstk = [ph.enter_context(nc.sbuf_tensor(f"kstk{i}", [128, T], BF16)) for i in range(2)]
            vnat = ph.enter_context(nc.sbuf_tensor("vnat", [128, NTC * 256], BF16))
            ptb = [ph.enter_context(nc.sbuf_tensor(f"ptb{i}", [128, 1024], BF16)) for i in range(2)]
            sqb = ph.enter_context(nc.sbuf_tensor("sqb", [128, T], BF16))

            nrm2 = ph.enter_context(nc.sbuf_tensor("nrm", [1, 16], F32))
            den_s = ph.enter_context(nc.sbuf_tensor("den_s", [1, 1024], F32))
            rec_s = ph.enter_context(nc.sbuf_tensor("rec_s", [1, 1024], F32))
            bca_s = ph.enter_context(nc.sbuf_tensor("bca_s", [128, 1024], BF16))
            bM = [ph.enter_context(nc.sbuf_tensor(f"bM{i}", [128, 1], F32)) for i in range(2)]
            qkP = [ph.enter_context(nc.psum_tensor(f"qkP{i}", [128, 1024], F32)) for i in range(2)]
            pvP = ph.enter_context(nc.psum_tensor("pvP", [128, 1024], F32))
            dnP = ph.enter_context(nc.psum_tensor("dnP", [1, 1024], F32))

            dw_wa = [0] * NPAIR
            for dc2 in range(NDC):
                p.emit("gpsimd", lambda g, dc=dc2: g.dma_start(
                    out=wab[:, dc * 768:(dc + 1) * 768], in_=wa_d[0, dc]), inc=sDW, inck=16)
            dw_wa[0] = sDW.n

            t_qkv_done = [0] * NPAIR
            t_qkv_v_end = [0] * NPAIR
            v_out_hist = []
            a_bca_hist = [0]
            a_den_hist = [0]
            v_den_hist = [0]
            pending_tail = [None]

            def flush_tail():
                if pending_tail[0] is not None:
                    fn = pending_tail[0]
                    pending_tail[0] = None
                    fn()

            for pr in range(NPAIR):
                h0 = 2 * pr
                flush_tail()
                a_evq = 0
                grp = 0
                a_evac_hist = []
                for jc in range(4):
                    for th in range(2):
                        ps_ = qkP[grp % 2]
                        w0 = [(sDW, dw_wa[pr]), (sA, a_ln1_done)]
                        if len(a_evac_hist) >= 2:
                            w0.append((sA, a_evac_hist[-2]))
                        if pr == 0 and jc == 0 and th == 0:
                            w0.append((sA, a_lamneg))
                        if jc == 0 and th == 0 and v_out_hist:
                            w0.append((sV, v_out_hist[-1]))
                        for dc in range(NDC):
                            for q in range(2):
                                p.emit("tensor", lambda t, ps_=ps_, dc=dc, jc=jc, th=th, q=q:
                                       t.matmul(out=ps_[:, q * 512:(q + 1) * 512],
                                                lhsT=wab[:, dc * 768 + jc * 128: dc * 768 + (jc + 1) * 128],
                                                rhs=ln1T[:, dc * T + th * 1024 + q * 512: dc * T + th * 1024 + (q + 1) * 512],
                                                start=(dc == 0), stop=(dc == NDC - 1)),
                                       waits=w0 if (dc == 0 and q == 0) else (),
                                       inc=sT if (dc == NDC - 1 and q == 1) else None)
                        t_grp = sT.n
                        if jc == 0:
                            mk = lambda s, ps_=ps_, th=th: (
                                s.activation(out=qstk[0][0:64, th * 1024:(th + 1) * 1024],
                                             in_=ps_[0:64, :], func=AF.Copy, scale=SCALE),
                                s.activation(out=qstk[1][0:64, th * 1024:(th + 1) * 1024],
                                             in_=ps_[64:128, :], func=AF.Copy, scale=SCALE))
                        elif jc == 1:
                            mk = lambda s, ps_=ps_, th=th: (
                                s.activation(out=kstk[0][0:64, th * 1024:(th + 1) * 1024],
                                             in_=ps_[0:64, :], func=AF.Copy),
                                s.activation(out=kstk[1][0:64, th * 1024:(th + 1) * 1024],
                                             in_=ps_[64:128, :], func=AF.Copy))
                        elif jc == 2:
                            mk = lambda s, ps_=ps_, th=th: (
                                s.activation(out=qstk[0][64:128, th * 1024:(th + 1) * 1024],
                                             in_=ps_[0:64, :], func=AF.Copy, scale=SCALE),
                                s.activation(out=qstk[1][64:128, th * 1024:(th + 1) * 1024],
                                             in_=ps_[64:128, :], func=AF.Copy, scale=SCALE))
                        else:
                            mk = lambda s, ps_=ps_, th=th: (
                                s.activation(out=kstk[0][64:128, th * 1024:(th + 1) * 1024],
                                             in_=ps_[0:64, :], func=AF.Copy, scale=lamneg[0:64, :]),
                                s.activation(out=kstk[1][64:128, th * 1024:(th + 1) * 1024],
                                             in_=ps_[64:128, :], func=AF.Copy, scale=lamneg[64:128, :]))

                        def evac(s, mk=mk):
                            a, b = mk(s)
                            return b
                        p.emit("scalar", lambda s, evac=evac: evac(s), waits=[(sT, t_grp)], inc=sA)
                        a_evac_hist.append(sA.n)
                        grp += 1
                a_qk_evac = sA.n

                a_vev = [0] * NTC
                for tc in range(NTC):
                    reg = pvP[:, (tc % 2) * 512:(tc % 2) * 512 + 256]
                    w0 = []
                    if tc >= 2:
                        w0.append((sA, a_vev[tc - 2]))
                    if pr > 0:
                        w0.append((sV, v_out_hist[-1]))
                    for dc in range(NDC):
                        p.emit("tensor", lambda t, reg=reg, dc=dc, tc=tc:
                               t.matmul(out=reg,
                                        lhsT=ln1T[:, dc * T + tc * 128:dc * T + (tc + 1) * 128],
                                        rhs=wab[:, dc * 768 + 512:dc * 768 + 768],
                                        start=(dc == 0), stop=(dc == NDC - 1)),
                               waits=w0 if dc == 0 else (),
                               inc=sT if dc == NDC - 1 else None)
                    t_v = sT.n
                    a_vev[tc] = p.emit("scalar", lambda s, reg=reg, tc=tc: s.activation(
                        out=vnat[:, tc * 256:(tc + 1) * 256], in_=reg, func=AF.Copy),
                        waits=[(sT, t_v)], inc=sA)
                a_v_done = sA.n
                t_qkv_v_end[pr] = sT.n
                p.mark(f"attn_qkv{pr}")

                if pr + 1 < NPAIR:
                    for dc2 in range(NDC):
                        p.emit("gpsimd", lambda g, dc=dc2, pr=pr: g.dma_start(
                            out=wab[:, dc * 768:(dc + 1) * 768], in_=wa_d[pr + 1, dc]),
                            waits=[(sT, t_v)] if dc2 == 0 else (), inc=sDW, inck=16)
                    dw_wa[pr + 1] = sDW.n

                a_bM_l = [0, 0]
                for hoff in range(2):
                    h = h0 + hoff
                    qs, ks = qstk[hoff], kstk[hoff]
                    nrm = nrm2[:, hoff * 8:hoff * 8 + 8]
                    for idx, stk in enumerate((qs, ks)):
                        p.emit("vector", lambda v, stk=stk: v.tensor_mul(out=sqb[:], in0=stk[:], in1=stk[:]),
                               waits=[(sA, a_qk_evac), (sT, sT.n)], inc=sV)
                        v_sq = sV.n
                        for r in range(2):
                            w0 = [(sV, v_sq), (sV, v_den_hist[-1])] if r == 0 else [(sV, sV.n)]
                            for q in range(2):
                                p.emit("tensor", lambda t, r=r, q=q: t.matmul(
                                    out=dnP[:, q * 512:(q + 1) * 512], lhsT=ones_k[:],
                                    rhs=sqb[:, r * 1024 + q * 512:r * 1024 + (q + 1) * 512],
                                    start=True, stop=True),
                                    waits=w0 if q == 0 else (),
                                    inc=sT if q == 1 else None)
                            t_qn = sT.n
                            p.emit("vector", lambda v, idx=idx, r=r, nrm=nrm: v.reduce_max(
                                out=nrm[:, idx * 2 + r:idx * 2 + r + 1], in_=dnP[:], axis=AX.X),
                                waits=[(sT, t_qn)], inc=sV)
                    p.cemit("vector", lambda v, nrm=nrm: v.tensor_max(out=nrm[:, 0:1], in0=nrm[:, 0:1], in1=nrm[:, 1:2]), sV)
                    p.cemit("vector", lambda v, nrm=nrm: v.tensor_max(out=nrm[:, 1:2], in0=nrm[:, 2:3], in1=nrm[:, 3:4]), sV)
                    p.cemit("vector", lambda v, nrm=nrm: v.tensor_add(out=nrm[:, 2:3], in0=nrm[:, 0:1], in1=nrm[:, 1:2]), sV)
                    p.cemit("vector", lambda v, nrm=nrm: v.tensor_scalar(out=nrm[:, 3:4], in0=nrm[:, 2:3],
                                                                scalar1=-0.5, scalar2=None, op0=ALU.mult), sV)
                    v_negm = sV.n
                    w_rep = [(sV, v_negm), (sA, a_v_done)]
                    if hoff == 1:
                        w_rep.append((sA, a_bM_l[0]))
                    if v_out_hist:
                        w_rep.append((sV, v_out_hist[-1]))
                    p.emit("tensor", lambda t, nrm=nrm: t.matmul(out=pvP[:, 0:1], lhsT=onesr_f[:], rhs=nrm[:, 3:4],
                                                        start=True, stop=True),
                           waits=w_rep, inc=sT)
                    t_rep2 = sT.n
                    p.emit("scalar", lambda s, hoff=hoff: s.copy(out=bM[hoff][:], in_=pvP[:, 0:1]),
                           waits=[(sT, t_rep2)], inc=sA)
                    a_bM_l[hoff] = sA.n
                    p.mark(f"pre_h{h}")

                for hoff in range(2):
                    h = h0 + hoff
                    qs, ks = qstk[hoff], kstk[hoff]
                    a_bM = a_bM_l[hoff]
                    for th in range(2):
                        flush_tail()
                        t_s = [0] * NTC
                        a_exp = [0] * NTC
                        t_dp = [0] * NTC
                        def emit_S(sc, w_extra):
                            ps_ = qkP[sc % 2]
                            for q in range(2):
                                p.emit("tensor", lambda t, ps_=ps_, sc=sc, q=q, ks=ks, qs=qs, th=th:
                                       t.matmul(out=ps_[:, q * 512:(q + 1) * 512],
                                                lhsT=ks[:, sc * 128:(sc + 1) * 128],
                                                rhs=qs[:, th * 1024 + q * 512:th * 1024 + (q + 1) * 512],
                                                start=True, stop=True),
                                       waits=w_extra if q == 0 else (),
                                       inc=sT if q == 1 else None)
                            t_s[sc] = sT.n
                        emit_S(0, [(sA, a_bM), (sA, a_bca_hist[-1])])
                        emit_S(1, [(sA, a_bca_hist[-1])])
                        for sc in range(NTC):
                            w_exp = [(sT, t_s[sc])]
                            p.emit("scalar", lambda s, sc=sc, hoff=hoff: s.activation(
                                out=ptb[sc % 2][:], in_=qkP[sc % 2][:], func=AF.Exp, bias=bM[hoff][:]),
                                waits=w_exp, inc=sA)
                            a_exp[sc] = sA.n
                            for q in range(2):
                                wdn = [(sA, a_exp[sc])] if q == 0 else []
                                if sc == 0 and q == 0:
                                    wdn.append((sV, v_den_hist[-1]))
                                p.emit("tensor", lambda t, sc=sc, q=q: t.matmul(
                                    out=dnP[:, q * 512:(q + 1) * 512], lhsT=ones_k[:],
                                    rhs=ptb[sc % 2][:, q * 512:(q + 1) * 512],
                                    start=(sc == 0), stop=(sc == NTC - 1)),
                                    waits=wdn)
                            for q in range(2):
                                wpv = []
                                if sc == 0 and q == 0 and v_out_hist:
                                    wpv.append((sV, v_out_hist[-1]))
                                p.emit("tensor", lambda t, sc=sc, q=q, hoff=hoff: t.matmul(
                                    out=pvP[:, q * 512:(q + 1) * 512],
                                    lhsT=vnat[:, sc * 256 + hoff * 128:sc * 256 + (hoff + 1) * 128],
                                    rhs=ptb[sc % 2][:, q * 512:(q + 1) * 512],
                                    start=(sc == 0), stop=(sc == NTC - 1)),
                                    waits=wpv,
                                    inc=sT if (q == 1 and sc == NTC - 1) else None)
                            t_dp[sc] = sT.n
                            if sc + 2 < NTC:
                                emit_S(sc + 2, [(sA, a_exp[sc])])
                        p.cemit("vector", lambda v: v.reciprocal(out=rec_s[:], in_=dnP[:]),
                                sV, waits=[(sT, t_dp[NTC - 1])])
                        v_rec = sV.n
                        a_den_hist.append(0)
                        v_den_hist.append(v_rec)
                        if DEBUG:
                            p.emit("sync", lambda hd_, h=h, th=th: hd_.dma_start(
                                out=dbg["den"][h, th], in_=rec_s[:]), waits=[(sV, v_rec)], inc=sDX, inck=16)

                        def tail2(h=h, th=th, v_rec=v_rec, a_exp15=a_exp[NTC - 1], t_dp15=t_dp[NTC - 1]):
                            for q in range(2):
                                p.emit("tensor", lambda t, q=q: t.matmul(
                                    out=qkP[0][:, q * 512:(q + 1) * 512], lhsT=onesr08[:],
                                    rhs=rec_s[:, q * 512:(q + 1) * 512], start=True, stop=True),
                                    waits=[(sV, v_rec), (sA, a_exp15)] if q == 0 else (),
                                    inc=sT if q == 1 else None)
                            t_bc = sT.n
                            p.emit("scalar", lambda s: s.activation(out=bca_s[:], in_=qkP[0][:], func=AF.Copy),
                                   waits=[(sT, t_bc)], inc=sA)
                            a_bca = sA.n
                            a_bca_hist.append(a_bca)
                            p.emit("vector", lambda v, h=h, th=th: v.tensor_mul(
                                out=outT[:, h * T + th * 1024:h * T + (th + 1) * 1024],
                                in0=pvP[:], in1=bca_s[:]),
                                waits=[(sA, a_bca), (sT, t_dp15)], inc=sV)
                            v_out_hist.append(sV.n)
                        pending_tail[0] = tail2
                        p.mark(f"th_h{h}_{th}")
                    p.mark(f"attn_h{h}")
            flush_tail()
            a_attn_v_done = v_out_hist[-1]
            if DEBUG2:
                p.emit("sync", lambda h: h.dma_start(out=dbg["outt2"], in_=outT[:]),
                       waits=[(sV, a_attn_v_done)], inc=sDX, inck=16)
            if DEBUG:
                p.emit("sync", lambda h: h.dma_start(out=dbg["outt"], in_=outT[:]),
                       waits=[(sV, a_attn_v_done)], inc=sDX, inck=16)

        p.mark("attn")
        s_ln.close()

        ph = ExitStack()
        with ph:
            wpb = ph.enter_context(nc.sbuf_tensor("wpb", [128, NDC * D], BF16, side="right"))
            px0l = [ph.enter_context(nc.sbuf_tensor(f"px0{i}", [128, D], F32)) for i in range(2)]
            x1c = [ph.enter_context(nc.sbuf_tensor(f"px1{i}", [128, D], F32)) for i in range(2)]
            l2n = [ph.enter_context(nc.sbuf_tensor(f"l2n{i}", [128, D], BF16)) for i in range(3)]
            l2st = [ph.enter_context(nc.sbuf_tensor(f"l2st{i}", [128, D], BF16)) for i in range(2)]
            pj_s = [ph.enter_context(nc.sbuf_tensor(f"pj_s{i}", [128, D], F32)) for i in range(2)]

            pst_t = ph.enter_context(nc.sbuf_tensor("pst", [128, 24], F32))
            pjP = [ph.enter_context(nc.psum_tensor(f"pjP{i}", [128, 1024], F32)) for i in range(3)]
            trP2 = ph.enter_context(nc.psum_tensor("trP2", [128, T], BF16))

            for dc2 in range(NDC):
                p.emit("gpsimd", lambda g, dc=dc2: g.dma_start(
                    out=wpb[:, dc * D:(dc + 1) * D], in_=wp_d[dc]),
                    waits=[(sT, t_qkv_v_end[NPAIR - 1])] if dc2 == 0 else (),
                    inc=sDW, inck=16)
            dw_wp = sDW.n

            dx_px0 = [0] * NTC
            for tc in range(2):
                dx_px0[tc] = p.emit("sync", lambda h, tc=tc: h.dma_start(
                    out=px0l[tc % 2][:], in_=x0_d[tc * 128:(tc + 1) * 128, :]),
                    waits=[(sV, a_attn_v_done), (sA, a_bca_hist[-1])] if tc == 0 else (),
                    inc=sDX, inck=16)

            v_x1 = [0] * NTC
            dx_x1st = [0] * NTC
            a_l2aff_l = [0] * NTC
            dx_l2st = [0] * NTC
            v_r2_l = [0] * NTC
            a_pjh_l = [0] * (2 * NTC)
            a_l2ev = [0] * NTC
            t_half = {}
            pend_l2 = []
            for tc in range(NTC):
                pst = pst_t[:, (tc % 2) * 12:(tc % 2) * 12 + 12]
                for half in range(2):
                    ps_ = pjP[(2 * tc + half) % 3]
                    w0 = [(sDW, dw_wp), (sV, a_attn_v_done)]
                    if 2 * tc + half >= 3:
                        w0.append((sA, a_pjh_l[2 * tc + half - 3]))
                    for dvc in range(NDC):
                        for q in range(2):
                            p.emit("tensor", lambda t, ps_=ps_, dvc=dvc, tc=tc, half=half, q=q:
                                   t.matmul(out=ps_[:, q * 512:(q + 1) * 512],
                                            lhsT=outT[:, dvc * T + tc * 128:dvc * T + (tc + 1) * 128],
                                            rhs=wpb[:, dvc * D + half * 1024 + q * 512:dvc * D + half * 1024 + (q + 1) * 512],
                                            start=(dvc == 0), stop=(dvc == NDC - 1)),
                                   waits=w0 if (dvc == 0 and q == 0) else (),
                                   inc=sT if (dvc == NDC - 1 and q == 1) else None)
                    t_half[(tc, half)] = sT.n
                    if half == 1 and len(pend_l2) >= 2:
                        pend_l2.pop(0)()
                pj = pj_s[tc % 2]
                for half in range(2):
                    p.emit("scalar", lambda s, half=half, pj=pj, tc=tc: s.activation(
                        out=pj[:, half * 1024:(half + 1) * 1024], in_=pjP[(2 * tc + half) % 3][:], func=AF.Copy),
                        waits=[(sT, t_half[(tc, half)]),
                               (sV, v_x1[tc - 1] if tc >= 1 else 0)], inc=sA)
                    a_pjh_l[2 * tc + half] = sA.n
                scr = pj_s[(tc + 1) % 2]
                for half in range(2):
                    p.emit("scalar", lambda s, half=half, pj=pj, pst=pst, scr=scr: s.activation(
                        out=scr[:, half * 1024:(half + 1) * 1024], in_=pj[:, half * 1024:(half + 1) * 1024],
                        func=AF.Square,
                        accum_out=pst[:, half:half + 1]),
                        waits=[(sV, v_r2_l[tc - 2] if tc >= 2 else 0),
                               (sV, v_x1[tc - 1] if tc >= 1 else 0)], inc=sA)
                    a_sqh = sA.n
                    p.emit("vector", lambda v, half=half, pj=pj, pst=pst: v.reduce_sum(
                        out=pst[:, 2 + half:3 + half], in_=pj[:, half * 1024:(half + 1) * 1024], axis=AX.X),
                        waits=[(sA, a_sqh)], inc=sV)
                p.cemit("vector", lambda v, pst=pst: v.tensor_add(out=pst[:, 4:5], in0=pst[:, 2:3], in1=pst[:, 3:4]), sV)
                p.cemit("vector", lambda v, pst=pst: v.tensor_scalar(out=pst[:, 4:5], in0=pst[:, 4:5],
                                                            scalar1=1.0 / D, scalar2=None, op0=ALU.mult), sV)
                p.cemit("vector", lambda v, pst=pst: v.tensor_add(out=pst[:, 5:6], in0=pst[:, 0:1], in1=pst[:, 1:2]), sV)
                p.cemit("vector", lambda v, pst=pst: v.tensor_tensor(out=pst[:, 6:7], in0=pst[:, 4:5], in1=pst[:, 4:5], op=ALU.mult), sV)
                p.cemit("vector", lambda v, pst=pst: v.tensor_scalar(out=pst[:, 5:6], in0=pst[:, 5:6],
                                                            scalar1=1.0 / D, scalar2=pst[:, 6:7],
                                                            op0=ALU.mult, op1=ALU.subtract), sV)
                p.cemit("vector", lambda v, pst=pst: v.tensor_scalar(out=pst[:, 5:6], in0=pst[:, 5:6],
                                                            scalar1=EPS, scalar2=None, op0=ALU.add), sV)
                p.cemit("vector", lambda v, pst=pst: v.reciprocal(out=pst[:, 7:8], in_=pst[:, 5:6]), sV)
                v_r1 = sV.n
                p.emit("scalar", lambda s, pst=pst: s.activation(out=pst[:, 8:9], in_=pst[:, 7:8], func=AF.Sqrt),
                       waits=[(sV, v_r1)], inc=sA)
                a_r1 = sA.n
                p.cemit("vector", lambda v, pst=pst: v.tensor_tensor(out=pst[:, 9:10], in0=pst[:, 4:5], in1=pst[:, 8:9], op=ALU.mult),
                        sV, waits=[(sA, a_r1), (sDX, dx_px0[tc])])
                xc = px0l[tc % 2]
                xo = x1c[tc % 2]
                p.cemit("vector", lambda v, xc=xc, pst=pst: v.tensor_scalar(out=xc[:], in0=xc[:],
                                                                   scalar1=pst[:, 9:10], scalar2=None,
                                                                   op0=ALU.subtract), sV)
                for half in range(2):
                    wstt = [(sV, sV.n)]
                    if half == 0 and tc >= 2:
                        wstt.append((sDX, dx_x1st[tc - 2]))
                    p.emit("vector", lambda v, xc=xc, xo=xo, half=half, pj=pj, pst=pst: v.scalar_tensor_tensor(
                        out=xo[:, half * 1024:(half + 1) * 1024], in0=pj[:, half * 1024:(half + 1) * 1024],
                        scalar=pst[:, 8:9], in1=xc[:, half * 1024:(half + 1) * 1024],
                        op0=ALU.mult, op1=ALU.add), waits=wstt, inc=sV)
                v_x1[tc] = sV.n
                dx_x1st[tc] = p.emit("sync", lambda h, xo=xo, tc=tc: h.dma_start(out=x1_d[tc], in_=xo[:]),
                       waits=[(sV, v_x1[tc])], inc=sDX, inck=16)
                if DEBUG:
                    p.emit("sync", lambda h, xo=xo, tc=tc: h.dma_start(out=dbg["x1"][tc], in_=xo[:]),
                           inc=sDX, inck=16)
                if tc + 2 < NTC:
                    dx_px0[tc + 2] = p.emit("sync", lambda h, tc=tc: h.dma_start(
                        out=px0l[tc % 2][:], in_=x0_d[(tc + 2) * 128:(tc + 3) * 128, :]),
                        waits=[(sV, v_x1[tc])], inc=sDX, inck=16)
                p.emit("scalar", lambda s, xo=xo, pst=pst, scr=scr: s.activation(
                    out=scr[:], in_=xo[:], func=AF.Square, accum_out=pst[:, 0:1]),
                    waits=[(sV, v_x1[tc])], inc=sA)
                a_sq2 = sA.n
                p.cemit("vector", lambda v, xo=xo, pst=pst: v.reduce_sum(out=pst[:, 2:3], in_=xo[:], axis=AX.X),
                        sV, waits=[(sA, a_sq2)])
                p.cemit("vector", lambda v, pst=pst: v.tensor_scalar(out=pst[:, 4:5], in0=pst[:, 2:3],
                                                            scalar1=1.0 / D, scalar2=None, op0=ALU.mult), sV)
                p.cemit("vector", lambda v, pst=pst: v.tensor_tensor(out=pst[:, 6:7], in0=pst[:, 4:5], in1=pst[:, 4:5], op=ALU.mult), sV)
                p.cemit("vector", lambda v, pst=pst: v.tensor_scalar(out=pst[:, 5:6], in0=pst[:, 0:1],
                                                            scalar1=1.0 / D, scalar2=pst[:, 6:7],
                                                            op0=ALU.mult, op1=ALU.subtract), sV)
                p.cemit("vector", lambda v, pst=pst: v.tensor_scalar(out=pst[:, 5:6], in0=pst[:, 5:6],
                                                            scalar1=EPS, scalar2=None, op0=ALU.add), sV)
                p.cemit("vector", lambda v, pst=pst: v.reciprocal(out=pst[:, 7:8], in_=pst[:, 5:6]), sV)
                v_r2 = sV.n
                v_r2_l[tc] = v_r2
                p.emit("scalar", lambda s, pst=pst: s.activation(out=pst[:, 10:11], in_=pst[:, 7:8], func=AF.Sqrt),
                       waits=[(sV, v_r2)], inc=sA)
                a_r2 = sA.n
                ln = l2n[tc % 3]
                p.cemit("vector", lambda v, xo=xo, ln=ln, pst=pst: v.tensor_scalar(
                    out=ln[:], in0=xo[:], scalar1=pst[:, 4:5], scalar2=pst[:, 10:11],
                    op0=ALU.subtract, op1=ALU.mult),
                    sV, waits=[(sA, a_r2), (sA, a_l2ev[tc - 3] if tc >= 3 else 0)])
                a_l2aff = sV.n
                a_l2aff_l[tc] = a_l2aff
                def l2move(tc=tc, ln=ln, a_l2aff=a_l2aff):
                    for j in range(NDC):
                        p.emit("tensor", lambda t, ln=ln, j=j: t.transpose(
                            out=trP2[:, j * 128:(j + 1) * 128], in_=ln[:, j * 128:(j + 1) * 128],
                            identity=ident_b[:]),
                            waits=[(sV, a_l2aff), (sA, a_l2ev[tc - 1] if tc >= 1 else 0)] if j == 0 else (),
                            inc=sT if j == NDC - 1 else None)
                    t_tr2 = sT.n
                    stg = l2st[tc % 2]
                    wev = [(sT, t_tr2)]
                    if tc >= 2:
                        wev.append((sDX, dx_l2st[tc - 2]))
                    a_l2ev[tc] = p.emit("scalar", lambda s, stg=stg: s.activation(
                        out=stg[:], in_=trP2[:], func=AF.Copy),
                        waits=wev, inc=sA)
                    for j2 in range(2):
                        p.emit("sync", lambda h, stg=stg, tc=tc, j2=j2: h.dma_start(
                            out=l2t_d[:, :, tc * 128:(tc + 1) * 128].transpose([1, 0, 2])[:, j2 * 8:(j2 + 1) * 8, :],
                            in_=stg[:].rearrange("p (j c) -> p j c", c=128)[:, j2 * 8:(j2 + 1) * 8, :]),
                            waits=[(sA, a_l2ev[tc])], inc=sDX, inck=16)
                    dx_l2st[tc] = sDX.n
                pend_l2.append(l2move)
            while pend_l2:
                pend_l2.pop(0)()
            dx_l2t = sDX.n
            a_proj_done = sA.n
            v_proj_done = sV.n

        p.mark("proj")
        s_ap.close()

        l2T = st.enter_context(nc.sbuf_tensor("l2T", [128, NDC * T], BF16))
        rwt = [st.enter_context(nc.sbuf_tensor(f"rwt{i}", [128, NDC * 1024], BF16)) for i in range(2)]
        ewt = rwt
        c_nat = st.enter_context(nc.sbuf_tensor("c_nat", [128, 128], F32))

        for dc2 in range(NDC):
            p.emit("sync", lambda h, dc=dc2: h.dma_start(
                out=l2T[:, dc * T:(dc + 1) * T], in_=l2t_d[dc]),
                waits=[(sDX, dx_l2t)] if dc2 == 0 else (), inc=sDX, inck=16)
        dx_l2T = sDX.n
        dw_l2T = 0
        if DEBUG:
            for dc2 in range(NDC):
                p.emit("sync", lambda h, dc=dc2: h.dma_start(out=dbg["l2t"][dc], in_=l2T[:, dc * T:(dc + 1) * T]),
                       waits=[(sDX, dx_l2T)], inc=sDX, inck=16)
        p.emit("gpsimd", lambda g: g.memset(c_nat[:], 0.0),
               waits=[(sV, v_proj_done), (sDX, dx_l2t)], inc=sG)
        g_czero = sG.n

        ph = ExitStack()
        with ph:
            RW = 1024
            NOE2 = 16384 // RW
            G = RW // 8

            rbb = [ph.enter_context(nc.sbuf_tensor(f"rbb{i}", [1, RW], BF16)) for i in range(2)]
            lgb = [ph.enter_context(nc.sbuf_tensor(f"lgb{i}", [128, RW], F32)) for i in range(3)]
            Ab = ph.enter_context(nc.sbuf_tensor("Ab", [128, G * 4], F32))
            Bb = ph.enter_context(nc.sbuf_tensor("Bb", [128, G * 4], F32))
            Cb = ph.enter_context(nc.sbuf_tensor("Cb", [128, G * 2], F32))
            Db = ph.enter_context(nc.sbuf_tensor("Db", [128, G * 2], F32))
            Bm = ph.enter_context(nc.sbuf_tensor("Bm", [128, G * 2], F32))
            sm1 = [ph.enter_context(nc.sbuf_tensor(f"sm1_{i}", [128, G], F32)) for i in range(2)]
            sm2 = [ph.enter_context(nc.sbuf_tensor(f"sm2_{i}", [128, G], F32)) for i in range(2)]
            sEg = ph.enter_context(nc.sbuf_tensor("sEg", [128, G], F32))
            sB2 = ph.enter_context(nc.sbuf_tensor("sB2", [128, G], F32))
            sDm = ph.enter_context(nc.sbuf_tensor("sDm", [128, G], F32))
            ssum = [ph.enter_context(nc.sbuf_tensor(f"ssum{i}", [128, G], F32)) for i in range(2)]
            ab = [ph.enter_context(nc.sbuf_tensor(f"ab{i}", [128, RW], BF16)) for i in range(2)]
            d2b = [ph.enter_context(nc.sbuf_tensor(f"d2b{i}", [128, RW], BF16)) for i in range(2)]
            sgb = [ph.enter_context(nc.sbuf_tensor(f"sgb{i}", [128, RW], BF16)) for i in range(2)]
            wb = [ph.enter_context(nc.sbuf_tensor(f"wbuf{i}", [128, RW], BF16)) for i in range(2)]

            crd = ph.enter_context(nc.sbuf_tensor("crd", [128, 8], F32))
            lgP = [ph.enter_context(nc.psum_tensor(f"lgP{i}", [128, RW], F32)) for i in range(3)]

            dw_rw = [0] * NOE2
            p.emit("gpsimd", lambda g: g.dma_start(out=rbb[0][:], in_=rb_d[:, 0:RW]),
                   waits=[(sV, v_proj_done), (sDX, dx_l2t)], inc=sDW, inck=16)
            for dc2 in range(NDC):
                p.emit("gpsimd", lambda g, dc=dc2: g.dma_start(
                    out=rwt[0][:, dc * RW:(dc + 1) * RW],
                    in_=rwt_d[dc * 128:(dc + 1) * 128, 0:RW]), inc=sDW, inck=16)
            dw_rw[0] = sDW.n

            t_lg = {}
            a_lg = {}
            dbg_t0 = 0
            tile_idx = 0
            a_sg = {}
            v_w_hist = {}
            v_d2_hist = {}
            t_last_router_mm = 0
            for oe in range(NOE2):
                rw = rwt[oe % 2]
                rb_cur = rbb[oe % 2]
                for tc in range(NTC):
                    ps_ = lgP[tile_idx % 3]
                    w0 = [(sDW, dw_rw[oe]), (sDX, dx_l2T), (sDW, dw_l2T)]
                    if tile_idx >= 3:
                        w0.append((sA, a_lg[tile_idx - 3]))
                    for dc in range(NDC):
                        for q in range(2):
                            p.emit("tensor", lambda t, ps_=ps_, rw=rw, dc=dc, tc=tc, q=q:
                                   t.matmul(out=ps_[:, q * 512:(q + 1) * 512],
                                            lhsT=l2T[:, dc * T + tc * 128:dc * T + (tc + 1) * 128],
                                            rhs=rw[:, dc * RW + q * 512:dc * RW + (q + 1) * 512],
                                            start=(dc == 0), stop=False),
                                   waits=w0 if (dc == 0 and q == 0) else ())
                    for q in range(2):
                        p.emit("tensor", lambda t, ps_=ps_, rb_cur=rb_cur, q=q: t.matmul(
                            out=ps_[:, q * 512:(q + 1) * 512], lhsT=onesr_b[:],
                            rhs=rb_cur[:, q * 512:(q + 1) * 512],
                            start=False, stop=True), inc=sT if q == 1 else None)
                    t_lg[tile_idx] = sT.n
                    t_last_router_mm = sT.n
                    lg = lgb[tile_idx % 3]
                    wlg = [(sT, t_lg[tile_idx])]
                    if tile_idx >= 3:
                        wlg.append((sV, v_d2_hist[tile_idx - 3]))
                    a_lg[tile_idx] = p.emit("scalar", lambda s, ps_=ps_, lg=lg: s.activation(
                        out=lg[:], in_=ps_[:], func=AF.Copy),
                        waits=wlg, inc=sA)
                    par = tile_idx % 2
                    Lv = lg[:].rearrange("p (e g) -> p e g", g=G)
                    Av = Ab[:].rearrange("p (e g) -> p e g", g=G)
                    Bv = Bb[:].rearrange("p (e g) -> p e g", g=G)
                    Cv = Cb[:].rearrange("p (e g) -> p e g", g=G)
                    Dv = Db[:].rearrange("p (e g) -> p e g", g=G)
                    Bmv = Bm[:].rearrange("p (e g) -> p e g", g=G)
                    wl = [(sA, a_lg[tile_idx])]
                    if tile_idx >= 2:
                        wl.append((sV, v_d2_hist[tile_idx - 2]))
                    p.emit("vector", lambda v, Lv=Lv, Av=Av: v.tensor_tensor(
                        out=Av, in0=Lv[:, 0::2, :], in1=Lv[:, 1::2, :], op=ALU.max), waits=wl)
                    p.emit("vector", lambda v, Lv=Lv, Bv=Bv: v.tensor_tensor(
                        out=Bv, in0=Lv[:, 0::2, :], in1=Lv[:, 1::2, :], op=ALU.min))
                    p.emit("vector", lambda v, Av=Av, Cv=Cv: v.tensor_tensor(
                        out=Cv, in0=Av[:, 0::2, :], in1=Av[:, 1::2, :], op=ALU.max))
                    p.emit("vector", lambda v, Av=Av, Dv=Dv: v.tensor_tensor(
                        out=Dv, in0=Av[:, 0::2, :], in1=Av[:, 1::2, :], op=ALU.min))
                    p.emit("vector", lambda v, Bv=Bv, Bmv=Bmv: v.tensor_tensor(
                        out=Bmv, in0=Bv[:, 0::2, :], in1=Bv[:, 1::2, :], op=ALU.max))
                    p.cemit("vector", lambda v, Cv=Cv, par=par: v.tensor_tensor(
                        out=sm1[par][:], in0=Cv[:, 0, :], in1=Cv[:, 1, :], op=ALU.max), sV)
                    p.cemit("vector", lambda v, Cv=Cv: v.tensor_tensor(
                        out=sEg[:], in0=Cv[:, 0, :], in1=Cv[:, 1, :], op=ALU.min), sV)
                    p.cemit("vector", lambda v, Dv=Dv: v.tensor_tensor(
                        out=sDm[:], in0=Dv[:, 0, :], in1=Dv[:, 1, :], op=ALU.max), sV)
                    p.cemit("vector", lambda v, Bmv=Bmv: v.tensor_tensor(
                        out=sB2[:], in0=Bmv[:, 0, :], in1=Bmv[:, 1, :], op=ALU.max), sV)
                    p.cemit("vector", lambda v, par=par: v.tensor_tensor(
                        out=sm2[par][:], in0=sB2[:], in1=sDm[:], op=ALU.max), sV)
                    p.cemit("vector", lambda v, par=par: v.tensor_tensor(
                        out=sm2[par][:], in0=sm2[par][:], in1=sEg[:], op=ALU.max), sV)
                    p.cemit("vector", lambda v, par=par: v.tensor_tensor(
                        out=ssum[par][:], in0=sm1[par][:], in1=sm2[par][:], op=ALU.add), sV)
                    m2v = sm2[par][:].rearrange("p (one g) -> p one g", one=1)
                    sv = ssum[par][:].rearrange("p (one g) -> p one g", one=1)
                    m2bc, Lbc = bass.broadcast_tensor_aps(m2v, Lv)
                    p.emit("vector", lambda v, Lbc=Lbc, m2bc=m2bc, par=par: v.tensor_tensor(
                        out=ab[par][:].rearrange("p (e g) -> p e g", g=G),
                        in0=Lbc, in1=m2bc, op=ALU.is_ge), waits=[(sV, sV.n)])
                    sbc, _ = bass.broadcast_tensor_aps(sv, Lv)
                    wd2 = [(sV, sV.n)]
                    if tile_idx >= 2:
                        wd2.append((sA, a_sg[tile_idx - 2]))
                    p.emit("vector", lambda v, Lv=Lv, sbc=sbc, par=par: v.scalar_tensor_tensor(
                        out=d2b[par][:].rearrange("p (e g) -> p e g", g=G),
                        in0=Lv, scalar=2.0, in1=sbc, op0=ALU.mult, op1=ALU.subtract),
                        waits=wd2, inc=sV)
                    v_d2 = sV.n
                    v_d2_hist[tile_idx] = v_d2
                    wsg = [(sV, v_d2)]
                    if tile_idx >= 2:
                        wsg.append((sV, v_w_hist[tile_idx - 2]))
                    a_sg[tile_idx] = p.emit("scalar", lambda s, par=par: s.activation(
                        out=sgb[par][:], in_=d2b[par][:], func=AF.Sigmoid),
                        waits=wsg, inc=sA)
                    if DEBUG and tile_idx == 0:
                        p.emit("sync", lambda h: h.dma_start(out=dbg["lg0"], in_=lgb[0][:]),
                               waits=[(sV, v_d2)], inc=sDX, inck=16)
                        p.emit("sync", lambda h: h.dma_start(out=dbg["ab0"], in_=ab[0][:]),
                               inc=sDX, inck=16)
                        p.emit("sync", lambda h: h.dma_start(out=dbg["m20"], in_=sm2[0][:]),
                               inc=sDX, inck=16)
                        p.emit("sync", lambda h: h.dma_start(out=dbg["sg0"], in_=sgb[0][:]),
                               waits=[(sA, a_sg[0])], inc=sDX, inck=16)
                    def fin_tile(ti):
                        par2 = ti % 2
                        tcl = ti % NTC
                        p.cemit("vector", lambda v, par2=par2: v.tensor_tensor(
                            out=wb[par2][:], in0=ab[par2][:], in1=sgb[par2][:], op=ALU.mult),
                            sV, waits=[(sA, a_sg[ti])])
                        if DEBUG and ti == 0:
                            p.emit("sync", lambda h: h.dma_start(out=dbg["wb0"], in_=wb[0][:]),
                                   waits=[(sV, sV.n)], inc=sDX, inck=16)
                        p.cemit("vector", lambda v, par2=par2: v.reduce_sum(
                            out=crd[:], in_=wb[par2][:].rearrange("p (e g) -> p e g", g=G), axis=AX.X), sV)
                        if DEBUG and ti == 0:
                            p.emit("sync", lambda h: h.dma_start(out=dbg["crd0"], in_=crd[:]),
                                   waits=[(sV, sV.n)], inc=sDX, inck=16)
                            dbg_t0 = sDX.n
                        wca = [(sG, g_czero)] if ti < NTC else []
                        if DEBUG and ti == 0:
                            wca.append((sDX, dbg_t0))
                        p.cemit("vector", lambda v, tcl=tcl: v.tensor_add(
                            out=c_nat[:, tcl * 8:(tcl + 1) * 8], in0=c_nat[:, tcl * 8:(tcl + 1) * 8],
                            in1=crd[:]), sV, waits=wca)
                        v_w_hist[ti] = sV.n
                    if tile_idx >= 1:
                        fin_tile(tile_idx - 1)
                    tile_idx += 1
                if oe + 1 < NOE2:
                    w_rwt = []
                    if oe >= 1:
                        w_rwt.append((sT, t_lg[oe * NTC - 1]))
                    p.emit("gpsimd", lambda g, oe=oe: g.dma_start(
                        out=rbb[(oe + 1) % 2][:], in_=rb_d[:, (oe + 1) * RW:(oe + 2) * RW]),
                        waits=w_rwt, inc=sDW, inck=16)
                    for dc2 in range(NDC):
                        p.emit("gpsimd", lambda g, dc=dc2, oe=oe: g.dma_start(
                            out=rwt[(oe + 1) % 2][:, dc * RW:(dc + 1) * RW],
                            in_=rwt_d[dc * 128:(dc + 1) * 128, (oe + 1) * RW:(oe + 2) * RW]),
                            inc=sDW, inck=16)
                    dw_rw[oe + 1] = sDW.n
            fin_tile(tile_idx - 1)
            v_comb_done = v_w_hist[tile_idx - 1]
            p.emit("gpsimd", lambda g: g.dma_start(out=ewt[0][:], in_=ew_d[0]),
                   waits=[(sT, t_last_router_mm)], inc=sDW, inck=16)
            dw_ew0 = sDW.n
            if DEBUG:
                p.emit("sync", lambda h: h.dma_start(out=dbg["c"], in_=c_nat[:]),
                       waits=[(sV, v_comb_done)], inc=sDX, inck=16)

        p.mark("router")
        ph = ExitStack()
        with ph:
            cTb = ph.enter_context(nc.sbuf_tensor("cTb", [8, T], BF16))
            cb = [ph.enter_context(nc.sbuf_tensor(f"cb{i}", [128, T], BF16)) for i in range(8)]
            facc = ph.enter_context(nc.sbuf_tensor("facc", [128, D], F32))
            tmpb = [ph.enter_context(nc.sbuf_tensor(f"tmpb{i}", [128, 1024], BF16)) for i in range(2)]
            eob = [ph.enter_context(nc.sbuf_tensor(f"eob{i}", [128, 1024], BF16)) for i in range(2)]
            fTb = ph.enter_context(nc.sbuf_tensor("fTb", [128, D], BF16))
            fst = [ph.enter_context(nc.sbuf_tensor(f"fst{i}", [128, D], BF16)) for i in range(2)]
            eoH = [ph.enter_context(nc.psum_tensor(f"eoH{i}", [128, 1024], F32)) for i in range(3)]
            trPf = ph.enter_context(nc.psum_tensor("trPf", [128, T], BF16))

            t_ctr = [0] * NTC
            for tc in range(NTC):
                reg = eoH[0][0:8, (tc % 2) * 512:(tc % 2) * 512 + 128]
                p.emit("tensor", lambda t, tc=tc, reg=reg: t.transpose(
                    out=reg, in_=c_nat[:, tc * 8:(tc + 1) * 8], identity=ident_f[:]),
                    waits=[(sV, v_comb_done)] if tc == 0 else
                          ([(sA, t_ctr[tc - 2])] if tc >= 2 else ()),
                    inc=sT)
                t_now = sT.n
                t_ctr[tc] = p.emit("scalar", lambda s, tc=tc, reg=reg: s.activation(
                    out=cTb[:, tc * 128:(tc + 1) * 128], in_=reg, func=AF.Copy),
                    waits=[(sT, t_now)], inc=sA)
            p.emit("vector", lambda v: v.engine_nop(), waits=[(sA, t_ctr[NTC - 1])], inc=sV)
            v_ctb = sV.n
            a_cb = [0] * 8
            for e in range(8):
                for half in range(2):
                    pse = eoH[1 + half]
                    for q in range(2):
                        p.emit("tensor", lambda t, e=e, half=half, q=q, pse=pse: t.matmul(
                            out=pse[:, q * 512:(q + 1) * 512], lhsT=sel[:, e * 128:(e + 1) * 128],
                            rhs=cTb[:, half * 1024 + q * 512:half * 1024 + (q + 1) * 512],
                            start=True, stop=True),
                            waits=[(sV, v_ctb)] if (e == 0 and half == 0 and q == 0) else
                                  ([(sA, a_cb[e - 1])] if (half == 0 and q == 0 and e >= 1) else ()),
                            inc=sT if q == 1 else None)
                    t_cb = sT.n
                    p.emit("scalar", lambda s, e=e, half=half, pse=pse: s.activation(
                        out=cb[e][:, half * 1024:(half + 1) * 1024], in_=pse[:], func=AF.Copy),
                        waits=[(sT, t_cb)], inc=sA)
                a_cb[e] = sA.n
            a_cb_done = sA.n

            dw_ew = [0] * NOC
            dw_ew[0] = dw_ew0

            a_fstg = [0] * NOC
            g_add_hist = []
            v_tmp_hist = []
            a_eo_hist = []
            t_oc_end = [0] * NOC
            v_ft_l = [0] * NOC
            dx_fst = [0] * NOC
            for oc in range(NOC):
                eb_ = ewt[oc % 2]
                fp = 0
                w_bias = [(sA, a_cb_done), (sDW, dw_ew[oc]), (sDW, dw_rbeb)]
                if oc >= 1:
                    w_bias.append((sV, v_ft_l[oc - 1]))
                for half in range(2):
                    pse = eoH[half]
                    for q in range(2):
                        p.emit("tensor", lambda t, oc=oc, half=half, q=q, pse=pse: t.matmul(
                            out=pse[:, q * 512:(q + 1) * 512], lhsT=eb_s[:, oc * 128:(oc + 1) * 128],
                            rhs=cTb[:, half * 1024 + q * 512:half * 1024 + (q + 1) * 512],
                            start=True, stop=True),
                            waits=(w_bias if (half == 0 and q == 0) else ()),
                            inc=sT if q == 1 else None)
                    t_b = sT.n
                    p.emit("scalar", lambda s, half=half, pse=pse: s.activation(
                        out=facc[:, half * 1024:(half + 1) * 1024], in_=pse[:], func=AF.Copy),
                        waits=[(sT, t_b)], inc=sA)
                a_binit = sA.n
                if oc + 1 < NOC:
                    w_ew = [(sT, t_oc_end[oc - 1])] if oc >= 1 else []
                    p.emit("gpsimd", lambda g, oc=oc: g.dma_start(
                        out=ewt[(oc + 1) % 2][:], in_=ew_d[oc + 1]),
                        waits=w_ew, inc=sDW, inck=16)
                    dw_ew[oc + 1] = sDW.n
                hk = 0
                for e in range(8):
                    for half in range(2):
                        pse = eoH[hk % 3]
                        w0 = []
                        if hk >= 3:
                            w0.append((sA, a_eo_hist[-3]))
                        if e == 0 and half == 0:
                            w0.append((sA, a_binit))
                        for dc in range(NDC):
                            for q in range(2):
                                p.emit("tensor", lambda t, pse=pse, eb_=eb_, e=e, dc=dc, half=half, q=q:
                                       t.matmul(out=pse[:, q * 512:(q + 1) * 512],
                                                lhsT=eb_[:, (e * NDC + dc) * 128:(e * NDC + dc + 1) * 128],
                                                rhs=l2T[:, dc * T + half * 1024 + q * 512:dc * T + half * 1024 + (q + 1) * 512],
                                                start=(dc == 0), stop=(dc == NDC - 1)),
                                       waits=w0 if (dc == 0 and q == 0) else (),
                                       inc=sT if (dc == NDC - 1 and q == 1) else None)
                        t_eo = sT.n
                        eb2 = eob[hk % 2]
                        p.emit("scalar", lambda s, pse=pse, eb2=eb2: s.activation(
                            out=eb2[:], in_=pse[:], func=AF.Copy),
                            waits=[(sT, t_eo), (sV, v_tmp_hist[-2] if len(v_tmp_hist) >= 2 else 0)],
                            inc=sA)
                        a_eo = sA.n
                        tb = tmpb[hk % 2]
                        wv = [(sA, a_eo), (sA, a_cb_done)]
                        if len(g_add_hist) >= 2:
                            wv.append((sG, g_add_hist[-2]))
                        p.emit("vector", lambda v, eb2=eb2, tb=tb, e=e, half=half: v.tensor_mul(
                            out=tb[:], in0=eb2[:], in1=cb[e][:, half * 1024:(half + 1) * 1024]),
                            waits=wv, inc=sV)
                        v_tmp_hist.append(sV.n)
                        a_eo_hist.append(a_eo)
                        p.emit("gpsimd", lambda g, tb=tb, half=half: g.tensor_tensor(
                            out=facc[:, half * 1024:(half + 1) * 1024],
                            in0=facc[:, half * 1024:(half + 1) * 1024],
                            in1=tb[:], op=ALU.add),
                            waits=[(sV, v_tmp_hist[-1])], inc=sG)
                        g_add_hist.append(sG.n)
                        hk += 1
                g_oc = sG.n
                t_oc_end[oc] = sT.n
                p.emit("vector", lambda v: v.tensor_copy(out=fTb[:], in_=facc[:]),
                       waits=[(sG, g_oc)], inc=sV)
                v_ft = sV.n
                v_ft_l[oc] = v_ft
                for j in range(NTC):
                    p.emit("tensor", lambda t, j=j: t.transpose(
                        out=trPf[:, j * 128:(j + 1) * 128], in_=fTb[:, j * 128:(j + 1) * 128],
                        identity=ident_b[:]),
                        waits=[(sV, v_ft), (sA, a_fstg[oc - 1] if oc >= 1 else 0)] if j == 0 else (),
                        inc=sT if j == NTC - 1 else None)
                t_trf = sT.n
                stg = fst[oc % 2]
                wfs = [(sT, t_trf)]
                if oc >= 2:
                    wfs.append((sDX, dx_fst[oc - 2]))
                a_fstg[oc] = p.emit("scalar", lambda s, stg=stg: s.activation(
                    out=stg[:], in_=trPf[:], func=AF.Copy),
                    waits=wfs, inc=sA)
                for j2 in range(2):
                    p.emit("sync", lambda h, stg=stg, oc=oc, j2=j2: h.dma_start(
                        out=fin_d[:, :, oc * 128:(oc + 1) * 128].transpose([1, 0, 2])[:, j2 * 8:(j2 + 1) * 8, :],
                        in_=stg[:].rearrange("p (j c) -> p j c", c=128)[:, j2 * 8:(j2 + 1) * 8, :]),
                        waits=[(sA, a_fstg[oc])], inc=sDX, inck=16)
                dx_fst[oc] = sDX.n
            dx_fin = sDX.n
            if DEBUG:
                for tc in range(NTC):
                    p.emit("sync", lambda h, tc=tc: h.dma_start(out=dbg["fin"][tc], in_=fin_d[tc]),
                           waits=[(sDX, dx_fin)], inc=sDX, inck=16)

        p.mark("expert")
        ph = ExitStack()
        with ph:
            fnb = [ph.enter_context(nc.sbuf_tensor(f"fnb{i}", [128, D], BF16)) for i in range(2)]
            fx1 = [ph.enter_context(nc.sbuf_tensor(f"fx1{i}", [128, D], F32)) for i in range(2)]
            x2c = [ph.enter_context(nc.sbuf_tensor(f"x2c{i}", [128, D], F32)) for i in range(2)]
            sqs3 = ph.enter_context(nc.sbuf_tensor("sqs3", [128, D], BF16))
            fstt_t = ph.enter_context(nc.sbuf_tensor("fstt", [128, 24], F32))

            dx_fn = [0] * NTC
            dx_fx = [0] * NTC
            for tc in range(2):
                dx_fn[tc] = p.emit("sync", lambda h, tc=tc: h.dma_start(
                    out=fnb[tc % 2][:], in_=fin_d[tc]), waits=[(sDX, dx_fin)], inc=sDX, inck=16)
                dx_fx[tc] = p.emit("sync", lambda h, tc=tc: h.dma_start(
                    out=fx1[tc % 2][:], in_=x1_d[tc]), inc=sDX, inck=16)
            v_x2 = [0] * NTC
            v_f1_l = [0] * NTC
            for tc in range(NTC):
                fstt = fstt_t[:, (tc % 2) * 12:(tc % 2) * 12 + 12]
                fn = fnb[tc % 2]
                fx = fx1[tc % 2]
                xo = x2c[tc % 2]
                p.emit("scalar", lambda s, fn=fn, fstt=fstt: s.activation(
                    out=sqs3[:], in_=fn[:], func=AF.Square, accum_out=fstt[:, 0:1]),
                    waits=[(sDX, dx_fn[tc]),
                           (sV, v_f1_l[tc - 2] if tc >= 2 else 0)], inc=sA)
                a_s3 = sA.n
                p.cemit("vector", lambda v, fn=fn, fstt=fstt: v.reduce_sum(out=fstt[:, 1:2], in_=fn[:], axis=AX.X),
                        sV, waits=[(sA, a_s3)])
                p.cemit("vector", lambda v, fstt=fstt: v.tensor_scalar(out=fstt[:, 2:3], in0=fstt[:, 1:2],
                                                            scalar1=1.0 / D, scalar2=None, op0=ALU.mult), sV)
                p.cemit("vector", lambda v, fstt=fstt: v.tensor_tensor(out=fstt[:, 3:4], in0=fstt[:, 2:3], in1=fstt[:, 2:3], op=ALU.mult), sV)
                p.cemit("vector", lambda v, fstt=fstt: v.tensor_scalar(out=fstt[:, 4:5], in0=fstt[:, 0:1],
                                                            scalar1=1.0 / D, scalar2=fstt[:, 3:4],
                                                            op0=ALU.mult, op1=ALU.subtract), sV)
                p.cemit("vector", lambda v, fstt=fstt: v.tensor_scalar(out=fstt[:, 4:5], in0=fstt[:, 4:5],
                                                            scalar1=EPS, scalar2=None, op0=ALU.add), sV)
                p.cemit("vector", lambda v, fstt=fstt: v.reciprocal(out=fstt[:, 5:6], in_=fstt[:, 4:5]), sV)
                v_f1 = sV.n
                v_f1_l[tc] = v_f1
                p.emit("scalar", lambda s, fstt=fstt: s.activation(out=fstt[:, 6:7], in_=fstt[:, 5:6], func=AF.Sqrt),
                       waits=[(sV, v_f1)], inc=sA)
                a_f1 = sA.n
                p.cemit("vector", lambda v, fstt=fstt: v.tensor_tensor(out=fstt[:, 7:8], in0=fstt[:, 2:3], in1=fstt[:, 6:7], op=ALU.mult),
                        sV, waits=[(sA, a_f1), (sDX, dx_fx[tc])])
                p.cemit("vector", lambda v, fx=fx, fstt=fstt: v.tensor_scalar(out=fx[:], in0=fx[:],
                                                                   scalar1=fstt[:, 7:8], scalar2=None,
                                                                   op0=ALU.subtract), sV)
                p.cemit("vector", lambda v, fn=fn, fx=fx, xo=xo, fstt=fstt: v.scalar_tensor_tensor(
                    out=xo[:], in0=fn[:], scalar=fstt[:, 6:7], in1=fx[:],
                    op0=ALU.mult, op1=ALU.add),
                    sV, waits=[(sO, (tc - 2 + 1) * 16) if tc >= 2 else (sO, 0)])
                v_x2[tc] = sV.n
                p.emit("sync", lambda h, xo=xo, tc=tc: h.dma_start(
                    out=out_d[tc * 128:(tc + 1) * 128, :], in_=xo[:]),
                    waits=[(sV, v_x2[tc])], inc=sO, inck=16)
                if tc + 2 < NTC:
                    dx_fn[tc + 2] = p.emit("sync", lambda h, tc=tc: h.dma_start(
                        out=fnb[tc % 2][:], in_=fin_d[tc + 2]), waits=[(sV, v_x2[tc])], inc=sDX, inck=16)
                    dx_fx[tc + 2] = p.emit("sync", lambda h, tc=tc: h.dma_start(
                        out=fx1[tc % 2][:], in_=x1_d[tc + 2]),
                        waits=[(sV, v_x2[tc])], inc=sDX, inck=16)
            p.wait("sync", (sO, NTC * 16))

        p.mark("final")
        if phase_limit is not None:
            p.truncate(phase_limit)
        p.check_schedule()
        with nc.Block() as block:
            @block.sync
            def _(h):
                for fn in p.engs["sync"]:
                    fn(h)

            @block.gpsimd
            def _(h):
                for fn in p.engs["gpsimd"]:
                    fn(h)

            @block.tensor
            def _(h):
                for fn in p.engs["tensor"]:
                    fn(h)

            @block.vector
            def _(h):
                for fn in p.engs["vector"]:
                    fn(h)

            @block.scalar
            def _(h):
                for fn in p.engs["scalar"]:
                    fn(h)

    return nc


_NC_CACHE = None


def _get_nc():
    global _NC_CACHE
    if _NC_CACHE is None:
        _NC_CACHE = build_core_program()
    return _NC_CACHE


def _prep_shared(L0_attn_W_attn, L0_attn_W_proj, L0_attn_lambda_q1, L0_attn_lambda_k1,
                 L0_attn_lambda_q2, L0_attn_lambda_k2, L0_ffn_router_w, L0_ffn_router_b,
                 L0_ffn_expert_w, L0_ffn_expert_b):
    C = np.ascontiguousarray
    wa = np.asarray(L0_attn_W_attn, dtype=np.float32)
    cols = []
    for pr in range(NPAIR):
        cols.append(np.concatenate([
            np.arange(128 * pr, 128 * (pr + 1)),
            1024 + np.arange(128 * pr, 128 * (pr + 1)),
            2048 + np.arange(128 * pr, 128 * (pr + 1)),
            3072 + np.arange(128 * pr, 128 * (pr + 1)),
            4096 + np.arange(256 * pr, 256 * (pr + 1)),
        ]))
    wa_re = np.stack([wa[:, c] for c in cols], axis=0)
    wa_re = C(wa_re.reshape(NPAIR, NDC, 128, 768))
    wp_re = C(np.asarray(L0_attn_W_proj, dtype=np.float32).reshape(NDC, 128, D))
    rwt = np.asarray(L0_ffn_router_w, dtype=np.float32).T
    rwt = C(rwt.reshape(D, 16, 128, 8).transpose(0, 1, 3, 2).reshape(D, 16384))
    rb = np.asarray(L0_ffn_router_b, dtype=np.float32)
    rb = C(rb.reshape(16, 128, 8).transpose(0, 2, 1).reshape(1, 16384))
    ew = np.asarray(L0_ffn_expert_w, dtype=np.float32)
    ew_re = C(ew.reshape(8, NOC, 128, NDC, 128).transpose(1, 4, 0, 3, 2))
    ebb = C(np.asarray(L0_ffn_expert_b, dtype=np.float32))
    lam = lambda x: C(np.asarray(x, dtype=np.float32).reshape(128, 16))
    return {
        "wa": wa_re, "wp": wp_re, "rwt": rwt, "rb": rb, "ew": ew_re, "ebb": ebb,
        "lq1": lam(L0_attn_lambda_q1), "lk1": lam(L0_attn_lambda_k1),
        "lq2": lam(L0_attn_lambda_q2), "lk2": lam(L0_attn_lambda_k2),
    }


def kernel(L0_attn_x, L0_attn_W_attn, L0_attn_W_proj, L0_attn_lambda_q1, L0_attn_lambda_k1,
           L0_attn_lambda_q2, L0_attn_lambda_k2, L0_ffn_router_w, L0_ffn_router_b,
           L0_ffn_expert_w, L0_ffn_expert_b):
    x = np.ascontiguousarray(np.asarray(L0_attn_x, dtype=np.float32))
    shared = _prep_shared(L0_attn_W_attn, L0_attn_W_proj, L0_attn_lambda_q1, L0_attn_lambda_k1,
                          L0_attn_lambda_q2, L0_attn_lambda_k2, L0_ffn_router_w, L0_ffn_router_b,
                          L0_ffn_expert_w, L0_ffn_expert_b)
    nc = _get_nc()
    in_maps = []
    for c in range(8):
        m = {"x0": np.ascontiguousarray(x[c])}
        m.update(shared)
        in_maps.append(m)
    res = run_bass_kernel_spmd(nc, in_maps, core_ids=list(range(8)))
    out = np.stack([np.asarray(res.results[c]["out"]) for c in range(8)], axis=0)
    return out.astype(np.float32)


if __name__ == "__main__":
    rng = np.random.default_rng(0)
    args = {
        "L0_attn_x": rng.standard_normal((8, T, D), dtype=np.float32),
        "L0_attn_W_attn": rng.standard_normal((D, 3 * D), dtype=np.float32) * 0.02,
        "L0_attn_W_proj": rng.standard_normal((D, D), dtype=np.float32) * 0.02,
        "L0_attn_lambda_q1": rng.standard_normal(D).astype(np.float32) * 0.1,
        "L0_attn_lambda_k1": rng.standard_normal(D).astype(np.float32) * 0.1,
        "L0_attn_lambda_q2": rng.standard_normal(D).astype(np.float32) * 0.1,
        "L0_attn_lambda_k2": rng.standard_normal(D).astype(np.float32) * 0.1,
        "L0_ffn_router_w": rng.standard_normal((16384, D), dtype=np.float32) * 0.02,
        "L0_ffn_router_b": rng.standard_normal(16384).astype(np.float32) * 0.02,
        "L0_ffn_expert_w": rng.standard_normal((8, D, D), dtype=np.float32) * 0.02,
        "L0_ffn_expert_b": rng.standard_normal((8, D)).astype(np.float32) * 0.02,
    }
    o = kernel(**args)
    print("kernel ran, out:", o.shape, o.dtype, np.abs(o).mean())


# version 18
# speedup vs baseline: 1.0544x; 1.0056x over previous
import math
import numpy as np
from contextlib import ExitStack

import concourse.bass as bass
import concourse.mybir as mybir
from concourse.bass_utils import run_bass_kernel_spmd

F32 = mybir.dt.float32
BF16 = mybir.dt.bfloat16
AF = mybir.ActivationFunctionType
ALU = mybir.AluOpType
AX = mybir.AxisListType

T = 2048
D = 2048
NH = 16
HD = 64
DV = 128
NTC = 16
NDC = 16
NPAIR = 8
NOC = 16
NOEC = 16
EPS = 1e-5
LAMBDA_INIT = 0.8 - 0.6 * math.exp(-0.3 * 0.0)
SCALE = HD ** (-0.5)
DEBUG = False
DEBUG2 = False
SCRATCH_OUT = False


class Sem:
    __slots__ = ("h", "n", "name")

    def __init__(self, name):
        self.h = None
        self.n = 0
        self.name = name


class Plan:
    def __init__(self):
        self.engs = {"sync": [], "gpsimd": [], "tensor": [], "vector": [], "scalar": []}
        self.meta = {k: [] for k in self.engs}

    def emit(self, eng, fn, waits=(), inc=None, inck=1):
        if inc is not None:
            inc.n += inck
        waits = tuple((s, v) for (s, v) in waits)

        def thunk(h, fn=fn, waits=waits, inc=inc, inck=inck):
            for s, v in waits:
                if v > 0:
                    h.wait_ge(s.h, v)
            ins = fn(h)
            if inc is not None:
                assert ins is not None
                ins.then_inc(inc.h, inck)

        self.engs[eng].append(thunk)
        self.meta[eng].append((waits, inc, inck))
        return inc.n if inc is not None else None

    def wait(self, eng, *pairs):
        pairs = tuple(pairs)

        def thunk(h, pairs=pairs):
            for s, v in pairs:
                if v > 0:
                    h.wait_ge(s.h, v)

        self.engs[eng].append(thunk)
        self.meta[eng].append((pairs, None, 0))

    def cemit(self, eng, fn, sem, waits=()):
        w = [(sem, sem.n)] + list(waits)
        return self.emit(eng, fn, waits=w, inc=sem)

    def mark(self, name):
        if not hasattr(self, "marks"):
            self.marks = []
        self.marks.append((name, {k: len(v) for k, v in self.engs.items()}))

    def truncate(self, name):
        snap = dict(self.marks)[name]
        for k in self.engs:
            self.engs[k] = self.engs[k][:snap[k]]
            self.meta[k] = self.meta[k][:snap[k]]

    def check_schedule(self):
        pos = {k: 0 for k in self.engs}
        counts = {}
        for k, ms in self.meta.items():
            for (waits, inc, inck) in ms:
                for s, v in waits:
                    counts.setdefault(s.name, 0)
                if inc is not None:
                    counts.setdefault(inc.name, 0)
        progressed = True
        while progressed:
            progressed = False
            for k, ms in self.meta.items():
                while pos[k] < len(ms):
                    waits, inc, inck = ms[pos[k]]
                    if all(counts.get(s.name, 0) >= v for s, v in waits):
                        if inc is not None:
                            counts[inc.name] = counts.get(inc.name, 0) + inck
                        pos[k] += 1
                        progressed = True
                    else:
                        break
        stuck = {k: pos[k] for k in self.engs if pos[k] < len(self.meta[k])}
        if stuck:
            lines = []
            for k, i in stuck.items():
                waits, inc, inck = self.meta[k][i]
                unmet = [(s.name, v, counts.get(s.name, 0)) for s, v in waits
                         if counts.get(s.name, 0) < v]
                lines.append(f"{k}[{i}/{len(self.meta[k])}] blocked on {unmet}")
            raise RuntimeError("DEADLOCK:\n" + "\n".join(lines))
        return counts


def build_core_program(phase_limit=None):
    nc = bass.Bass("TRN2", target_bir_lowering=False)
    p = Plan()

    x0_d = nc.dram_tensor("x0", [T, D], F32, kind="ExternalInput").ap()
    wa_d = nc.dram_tensor("wa", [NPAIR, NDC, 128, 768], F32, kind="ExternalInput").ap()
    wp_d = nc.dram_tensor("wp", [NDC, 128, D], F32, kind="ExternalInput").ap()
    lq1_d = nc.dram_tensor("lq1", [128, 16], F32, kind="ExternalInput").ap()
    lk1_d = nc.dram_tensor("lk1", [128, 16], F32, kind="ExternalInput").ap()
    lq2_d = nc.dram_tensor("lq2", [128, 16], F32, kind="ExternalInput").ap()
    lk2_d = nc.dram_tensor("lk2", [128, 16], F32, kind="ExternalInput").ap()
    rwt_d = nc.dram_tensor("rwt", [D, 16384], F32, kind="ExternalInput").ap()
    rb_d = nc.dram_tensor("rb", [1, 16384], F32, kind="ExternalInput").ap()
    ew_d = nc.dram_tensor("ew", [NOC, 128, 8, NDC, 128], F32, kind="ExternalInput").ap()
    ebb_d = nc.dram_tensor("ebb", [8, D], F32, kind="ExternalInput").ap()
    out_d = nc.dram_tensor("out", [T, D], F32, kind="ExternalOutput").ap()

    scrkind = "ExternalOutput" if SCRATCH_OUT else "Internal"
    x1_d = nc.dram_tensor("x1scr", [NTC, 128, D], F32, kind=scrkind).ap()
    l2t_d = nc.dram_tensor("l2tscr", [NDC, 128, T], BF16, kind=scrkind).ap()
    fin_d = nc.dram_tensor("finscr", [NTC, 128, D], BF16, kind=scrkind).ap()
    dbg = {}
    if DEBUG2:
        dbg["outt2"] = nc.dram_tensor("dbg_outt2", [128, NH * T], BF16, kind="ExternalOutput").ap()
    if DEBUG:
        dbg["ln1t"] = nc.dram_tensor("dbg_ln1t", [128, NDC * T], BF16, kind="ExternalOutput").ap()
        dbg["outt"] = nc.dram_tensor("dbg_outt", [128, NH * T], BF16, kind="ExternalOutput").ap()
        dbg["c"] = nc.dram_tensor("dbg_c", [128, 128], F32, kind="ExternalOutput").ap()
        dbg["x1"] = nc.dram_tensor("dbg_x1", [NTC, 128, D], F32, kind="ExternalOutput").ap()
        dbg["l2t"] = nc.dram_tensor("dbg_l2t", [NDC, 128, T], BF16, kind="ExternalOutput").ap()
        dbg["fin"] = nc.dram_tensor("dbg_fin", [NTC, 128, D], BF16, kind="ExternalOutput").ap()
        dbg["den"] = nc.dram_tensor("dbg_den", [NH, 2, 1, 1024], F32, kind="ExternalOutput").ap()
        dbg["stt"] = nc.dram_tensor("dbg_stt", [NTC, 128, 8], F32, kind="ExternalOutput").ap()
        dbg["lg0"] = nc.dram_tensor("dbg_lg0", [128, 1024], F32, kind="ExternalOutput").ap()
        dbg["ab0"] = nc.dram_tensor("dbg_ab0", [128, 1024], BF16, kind="ExternalOutput").ap()
        dbg["sg0"] = nc.dram_tensor("dbg_sg0", [128, 1024], BF16, kind="ExternalOutput").ap()
        dbg["wb0"] = nc.dram_tensor("dbg_wb0", [128, 1024], BF16, kind="ExternalOutput").ap()
        dbg["m20"] = nc.dram_tensor("dbg_m20", [128, 128], F32, kind="ExternalOutput").ap()
        dbg["crd0"] = nc.dram_tensor("dbg_crd0", [128, 8], F32, kind="ExternalOutput").ap()

    sDX = Sem("sDX")
    sDW = Sem("sDW")
    sT = Sem("sT")
    sA = Sem("sA")
    sV = Sem("sV")
    sG = Sem("sG")
    sO = Sem("sO")
    sems = [sDX, sDW, sT, sA, sV, sG, sO]

    st = ExitStack()
    SB = lambda name, shape, dt: st.enter_context(nc.sbuf_tensor(name, shape, dt))
    PS = lambda name, shape, dt: st.enter_context(nc.psum_tensor(name, shape, dt))

    with st:
        for s in sems:
            s.h = st.enter_context(nc.semaphore(s.name))

        ones_k = SB("ones_k", [128, 1], BF16)
        onesr_b = SB("onesr_b", [1, 128], BF16)
        onesr_f = SB("onesr_f", [1, 128], F32)
        onesr08 = SB("onesr08", [1, 128], F32)
        ident_b = SB("ident_b", [128, 128], BF16)
        ident_f = SB("ident_f", [128, 128], F32)
        sel = SB("sel", [8, 8 * 128], BF16)
        lamneg = SB("lamneg", [128, 1], F32)
        lq1 = SB("lq1_s", [128, 16], F32)
        lk1 = SB("lk1_s", [128, 16], F32)
        lq2 = SB("lq2_s", [128, 16], F32)
        lk2 = SB("lk2_s", [128, 16], F32)
        lam_sc = SB("lam_sc", [1, 4], F32)
        eb_s = SB("eb_s", [8, D], BF16)

        p.emit("gpsimd", lambda g: g.memset(ones_k[:], 1.0))
        p.emit("gpsimd", lambda g: g.memset(onesr_b[:], 1.0))
        p.emit("gpsimd", lambda g: g.memset(onesr_f[:], 1.0))
        p.emit("gpsimd", lambda g: g.memset(onesr08[:], 1.0 - LAMBDA_INIT))
        p.emit("gpsimd", lambda g: g.memset(ident_b[:], 0.0))
        p.emit("gpsimd", lambda g: g.affine_select(
            out=ident_b[:], in_=ident_b[:], compare_op=ALU.not_equal, fill=1.0,
            base=0, pattern=[[-1, 128]], channel_multiplier=1))
        p.emit("gpsimd", lambda g: g.memset(ident_f[:], 0.0))
        p.emit("gpsimd", lambda g: g.affine_select(
            out=ident_f[:], in_=ident_f[:], compare_op=ALU.not_equal, fill=1.0,
            base=0, pattern=[[-1, 128]], channel_multiplier=1))
        p.emit("gpsimd", lambda g: g.memset(sel[:], 0.0))
        for e in range(8):
            p.emit("gpsimd", lambda g, e=e: g.affine_select(
                out=sel[:, e * 128:(e + 1) * 128], in_=sel[:, e * 128:(e + 1) * 128],
                compare_op=ALU.not_equal, fill=1.0, base=-e,
                pattern=[[0, 128]], channel_multiplier=1))
        g_consts = p.emit("gpsimd", lambda g: g.engine_nop(), inc=sG)

        for src, dst in [(lq1_d, lq1), (lk1_d, lk1), (lq2_d, lq2), (lk2_d, lk2)]:
            p.emit("sync", lambda h, s=src, d=dst: h.dma_start(out=d[:], in_=s), inc=sDX, inck=16)
        p.emit("gpsimd", lambda g: g.dma_start(out=eb_s[:], in_=ebb_d), inc=sDW, inck=16)
        dw_rbeb = sDW.n
        dx_lam = sDX.n

        ph = ExitStack()
        s_ln = ExitStack()
        s_ap = ExitStack()
        ln1T = s_ln.enter_context(nc.sbuf_tensor("ln1T", [128, NDC * T], BF16, side="right"))
        outT = s_ap.enter_context(nc.sbuf_tensor("outT", [128, NH * T], BF16))
        with ph:
            x0c = [ph.enter_context(nc.sbuf_tensor(f"x0c{i}", [128, D], F32)) for i in range(3)]
            ln1n = [ph.enter_context(nc.sbuf_tensor(f"ln1n{i}", [128, D], BF16)) for i in range(2)]
            sqs = ph.enter_context(nc.sbuf_tensor("sqs", [128, D], BF16))
            stt_t = ph.enter_context(nc.sbuf_tensor("stt", [128, 16], F32))
            trP = [ph.enter_context(nc.psum_tensor(f"trP{i}", [128, T], BF16)) for i in range(2)]
            lamP = [ph.enter_context(nc.psum_tensor(f"lamP{i}", [1, 1], F32)) for i in range(2)]
            repP = ph.enter_context(nc.psum_tensor("repP", [128, 1], F32))

            for j, (qa, ka, ps_) in enumerate([(lq1, lk1, lamP[0]), (lq2, lk2, lamP[1])]):
                for f in range(16):
                    p.emit("tensor", lambda t, qa=qa, ka=ka, ps_=ps_, f=f:
                           t.matmul(out=ps_[:], lhsT=qa[:, f:f + 1], rhs=ka[:, f:f + 1],
                                    start=(f == 0), stop=(f == 15)),
                           waits=[(sDX, dx_lam)] if f == 0 else (),
                           inc=sT if f == 15 else None)
            t_lamdots = sT.n
            p.emit("scalar", lambda s: s.activation(out=lam_sc[:, 0:1], in_=lamP[0][:], func=AF.Exp),
                   waits=[(sT, t_lamdots)])
            p.emit("scalar", lambda s: s.activation(out=lam_sc[:, 1:2], in_=lamP[1][:], func=AF.Exp),
                   inc=sA)
            a_lamexp = sA.n
            p.cemit("vector", lambda v: v.tensor_sub(out=lam_sc[:, 2:3], in0=lam_sc[:, 0:1], in1=lam_sc[:, 1:2]),
                    sV, waits=[(sA, a_lamexp)])
            p.cemit("vector", lambda v: v.tensor_scalar(out=lam_sc[:, 2:3], in0=lam_sc[:, 2:3],
                                                        scalar1=LAMBDA_INIT, scalar2=None, op0=ALU.add), sV)
            p.cemit("vector", lambda v: v.tensor_scalar(out=lam_sc[:, 3:4], in0=lam_sc[:, 2:3],
                                                        scalar1=-1.0, scalar2=None, op0=ALU.mult), sV)
            v_lam = sV.n
            p.emit("tensor", lambda t: t.matmul(out=repP[:], lhsT=onesr_f[:], rhs=lam_sc[:, 3:4],
                                                start=True, stop=True),
                   waits=[(sV, v_lam), (sG, g_consts)], inc=sT)
            t_rep = sT.n
            p.emit("scalar", lambda s: s.copy(out=lamneg[:], in_=repP[:]), waits=[(sT, t_rep)], inc=sA)
            a_lamneg = sA.n

            dx_x0 = [0] * NTC
            for tc in range(min(3, NTC)):
                dx_x0[tc] = p.emit("sync", lambda h, tc=tc: h.dma_start(
                    out=x0c[tc % 3][:], in_=x0_d[tc * 128:(tc + 1) * 128, :]), inc=sDX, inck=16)

            a_aff = [0] * NTC
            a_ev = [0] * NTC
            t_tr = [0] * NTC
            v_st_l = [0] * NTC
            dbg_sttd = 0
            for tc in range(NTC):
                xc = x0c[tc % 3]
                ln = ln1n[tc % 2]
                stt = stt_t[:, (tc % 2) * 8:(tc % 2) * 8 + 8]
                p.emit("scalar", lambda s, xc=xc, tc=tc, stt=stt: s.activation(
                    out=sqs[:], in_=xc[:], func=AF.Square, accum_out=stt[:, 0:1]),
                    waits=[(sDX, dx_x0[tc]),
                           (sV, v_st_l[tc - 2] if tc >= 2 else 0),
                           (sV, a_aff[tc - 2] if tc >= 2 else 0)], inc=sA)
                a_sq = sA.n
                p.cemit("vector", lambda v, xc=xc, stt=stt: v.reduce_sum(out=stt[:, 1:2], in_=xc[:], axis=AX.X),
                        sV, waits=[(sDX, dx_x0[tc])])
                p.cemit("vector", lambda v, stt=stt: v.tensor_scalar(out=stt[:, 2:3], in0=stt[:, 1:2],
                                                            scalar1=1.0 / D, scalar2=None, op0=ALU.mult), sV)
                p.cemit("vector", lambda v, stt=stt: v.tensor_tensor(out=stt[:, 3:4], in0=stt[:, 2:3], in1=stt[:, 2:3], op=ALU.mult), sV)
                p.cemit("vector", lambda v, stt=stt: v.tensor_scalar(out=stt[:, 4:5], in0=stt[:, 0:1],
                                                            scalar1=1.0 / D, scalar2=stt[:, 3:4],
                                                            op0=ALU.mult, op1=ALU.subtract), sV,
                        waits=[(sA, a_sq)])
                p.cemit("vector", lambda v, stt=stt: v.tensor_scalar(out=stt[:, 4:5], in0=stt[:, 4:5],
                                                            scalar1=EPS, scalar2=None, op0=ALU.add), sV)
                p.cemit("vector", lambda v, stt=stt: v.reciprocal(out=stt[:, 5:6], in_=stt[:, 4:5]), sV)
                v_st = sV.n
                v_st_l[tc] = v_st
                p.emit("scalar", lambda s, stt=stt: s.activation(out=stt[:, 6:7], in_=stt[:, 5:6], func=AF.Sqrt),
                       waits=[(sV, v_st)], inc=sA)
                a_rstd = sA.n
                a_aff[tc] = p.cemit("vector", lambda v, xc=xc, ln=ln, stt=stt: v.tensor_scalar(
                    out=ln[:], in0=xc[:], scalar1=stt[:, 2:3], scalar2=stt[:, 6:7],
                    op0=ALU.subtract, op1=ALU.mult),
                    sV, waits=[(sA, a_rstd), (sA, a_ev[tc - 2] if tc >= 2 else 0)])
                if DEBUG:
                    p.emit("sync", lambda h, tc=tc: h.dma_start(out=dbg["stt"][tc], in_=stt[:]),
                           waits=[(sV, a_aff[tc])], inc=sDX, inck=16)
                    dbg_sttd = sDX.n
                if tc + 3 < NTC:
                    dx_x0[tc + 3] = p.emit("sync", lambda h, tc=tc: h.dma_start(
                        out=x0c[(tc + 3) % 3][:], in_=x0_d[(tc + 3) * 128:(tc + 4) * 128, :]),
                        waits=[(sV, a_aff[tc])], inc=sDX, inck=16)
                tp = trP[tc % 2]
                for j in range(NDC):
                    p.emit("tensor", lambda t, tp=tp, ln=ln, j=j, tc=tc: t.transpose(
                        out=tp[:, j * 128:(j + 1) * 128], in_=ln[:, j * 128:(j + 1) * 128], identity=ident_b[:]),
                        waits=([(sG, g_consts), (sV, a_aff[tc]),
                                (sA, a_ev[tc - 2] if tc >= 2 else 0)] if j == 0 else ()),
                        inc=sT if j == NDC - 1 else None)
                t_tr[tc] = sT.n
                a_ev[tc] = p.emit("scalar", lambda s, tp=tp, tc=tc: s.activation(
                    out=ln1T[:].rearrange("p (j t) -> p j t", t=T)[:, :, tc * 128:(tc + 1) * 128],
                    in_=tp[:].rearrange("p (j c) -> p j c", c=128),
                    func=AF.Copy),
                    waits=[(sT, t_tr[tc])], inc=sA)
            a_ln1_done = sA.n
            if DEBUG:
                p.emit("sync", lambda h: h.dma_start(out=dbg["ln1t"], in_=ln1T[:]),
                       waits=[(sA, a_ln1_done)], inc=sDX, inck=16)

        p.mark("ln1")
        ph = ExitStack()
        with ph:
            wab = ph.enter_context(nc.sbuf_tensor("wab", [128, NDC * 768], BF16, side="right"))
            qstk = [ph.enter_context(nc.sbuf_tensor(f"qstk{i}", [128, T], BF16)) for i in range(2)]
            kstk = [ph.enter_context(nc.sbuf_tensor(f"kstk{i}", [128, T], BF16)) for i in range(2)]
            vnat = ph.enter_context(nc.sbuf_tensor("vnat", [128, NTC * 256], BF16))
            ptb = [ph.enter_context(nc.sbuf_tensor(f"ptb{i}", [128, 1024], BF16)) for i in range(3)]
            sqb = ph.enter_context(nc.sbuf_tensor("sqb", [128, T], BF16))

            nrm2 = ph.enter_context(nc.sbuf_tensor("nrm", [1, 16], F32))
            den_s = ph.enter_context(nc.sbuf_tensor("den_s", [1, 1024], F32))
            rec_s = ph.enter_context(nc.sbuf_tensor("rec_s", [1, 1024], F32))
            bca_s = ph.enter_context(nc.sbuf_tensor("bca_s", [128, 1024], BF16))
            bM = [ph.enter_context(nc.sbuf_tensor(f"bM{i}", [128, 1], F32)) for i in range(2)]
            qkP = [ph.enter_context(nc.psum_tensor(f"qkP{i}", [128, 1024], F32)) for i in range(2)]
            pvP = ph.enter_context(nc.psum_tensor("pvP", [128, 1024], F32))
            dnP = ph.enter_context(nc.psum_tensor("dnP", [1, 1024], F32))

            dw_wa = [0] * NPAIR
            for dc2 in range(NDC):
                p.emit("gpsimd", lambda g, dc=dc2: g.dma_start(
                    out=wab[:, dc * 768:(dc + 1) * 768], in_=wa_d[0, dc]), inc=sDW, inck=16)
            dw_wa[0] = sDW.n

            t_qkv_done = [0] * NPAIR
            t_qkv_v_end = [0] * NPAIR
            v_out_hist = []
            a_bca_hist = [0]
            a_den_hist = [0]
            v_den_hist = [0]
            pending_tail = [None]

            def flush_tail():
                if pending_tail[0] is not None:
                    fn = pending_tail[0]
                    pending_tail[0] = None
                    fn()

            for pr in range(NPAIR):
                h0 = 2 * pr
                flush_tail()
                a_evq = 0
                grp = 0
                a_evac_hist = []
                for jc in range(4):
                    for th in range(2):
                        ps_ = qkP[grp % 2]
                        w0 = [(sDW, dw_wa[pr]), (sA, a_ln1_done)]
                        if len(a_evac_hist) >= 2:
                            w0.append((sA, a_evac_hist[-2]))
                        if pr == 0 and jc == 0 and th == 0:
                            w0.append((sA, a_lamneg))
                        if jc == 0 and th == 0 and v_out_hist:
                            w0.append((sV, v_out_hist[-1]))
                        for dc in range(NDC):
                            for q in range(2):
                                p.emit("tensor", lambda t, ps_=ps_, dc=dc, jc=jc, th=th, q=q:
                                       t.matmul(out=ps_[:, q * 512:(q + 1) * 512],
                                                lhsT=wab[:, dc * 768 + jc * 128: dc * 768 + (jc + 1) * 128],
                                                rhs=ln1T[:, dc * T + th * 1024 + q * 512: dc * T + th * 1024 + (q + 1) * 512],
                                                start=(dc == 0), stop=(dc == NDC - 1)),
                                       waits=w0 if (dc == 0 and q == 0) else (),
                                       inc=sT if (dc == NDC - 1 and q == 1) else None)
                        t_grp = sT.n
                        if jc == 0:
                            mk = lambda s, ps_=ps_, th=th: (
                                s.activation(out=qstk[0][0:64, th * 1024:(th + 1) * 1024],
                                             in_=ps_[0:64, :], func=AF.Copy, scale=SCALE),
                                s.activation(out=qstk[1][0:64, th * 1024:(th + 1) * 1024],
                                             in_=ps_[64:128, :], func=AF.Copy, scale=SCALE))
                        elif jc == 1:
                            mk = lambda s, ps_=ps_, th=th: (
                                s.activation(out=kstk[0][0:64, th * 1024:(th + 1) * 1024],
                                             in_=ps_[0:64, :], func=AF.Copy),
                                s.activation(out=kstk[1][0:64, th * 1024:(th + 1) * 1024],
                                             in_=ps_[64:128, :], func=AF.Copy))
                        elif jc == 2:
                            mk = lambda s, ps_=ps_, th=th: (
                                s.activation(out=qstk[0][64:128, th * 1024:(th + 1) * 1024],
                                             in_=ps_[0:64, :], func=AF.Copy, scale=SCALE),
                                s.activation(out=qstk[1][64:128, th * 1024:(th + 1) * 1024],
                                             in_=ps_[64:128, :], func=AF.Copy, scale=SCALE))
                        else:
                            mk = lambda s, ps_=ps_, th=th: (
                                s.activation(out=kstk[0][64:128, th * 1024:(th + 1) * 1024],
                                             in_=ps_[0:64, :], func=AF.Copy, scale=lamneg[0:64, :]),
                                s.activation(out=kstk[1][64:128, th * 1024:(th + 1) * 1024],
                                             in_=ps_[64:128, :], func=AF.Copy, scale=lamneg[64:128, :]))

                        def evac(s, mk=mk):
                            a, b = mk(s)
                            return b
                        p.emit("scalar", lambda s, evac=evac: evac(s), waits=[(sT, t_grp)], inc=sA)
                        a_evac_hist.append(sA.n)
                        grp += 1
                a_qk_evac = sA.n

                a_vev = [0] * NTC
                for tc in range(NTC):
                    reg = pvP[:, (tc % 2) * 512:(tc % 2) * 512 + 256]
                    w0 = []
                    if tc >= 2:
                        w0.append((sA, a_vev[tc - 2]))
                    if pr > 0:
                        w0.append((sV, v_out_hist[-1]))
                    for dc in range(NDC):
                        p.emit("tensor", lambda t, reg=reg, dc=dc, tc=tc:
                               t.matmul(out=reg,
                                        lhsT=ln1T[:, dc * T + tc * 128:dc * T + (tc + 1) * 128],
                                        rhs=wab[:, dc * 768 + 512:dc * 768 + 768],
                                        start=(dc == 0), stop=(dc == NDC - 1)),
                               waits=w0 if dc == 0 else (),
                               inc=sT if dc == NDC - 1 else None)
                    t_v = sT.n
                    a_vev[tc] = p.emit("scalar", lambda s, reg=reg, tc=tc: s.activation(
                        out=vnat[:, tc * 256:(tc + 1) * 256], in_=reg, func=AF.Copy),
                        waits=[(sT, t_v)], inc=sA)
                a_v_done = sA.n
                t_qkv_v_end[pr] = sT.n
                p.mark(f"attn_qkv{pr}")

                if pr + 1 < NPAIR:
                    for dc2 in range(NDC):
                        p.emit("gpsimd", lambda g, dc=dc2, pr=pr: g.dma_start(
                            out=wab[:, dc * 768:(dc + 1) * 768], in_=wa_d[pr + 1, dc]),
                            waits=[(sT, t_v)] if dc2 == 0 else (), inc=sDW, inck=16)
                    dw_wa[pr + 1] = sDW.n

                a_bM_l = [0, 0]
                for hoff in range(2):
                    h = h0 + hoff
                    qs, ks = qstk[hoff], kstk[hoff]
                    nrm = nrm2[:, hoff * 8:hoff * 8 + 8]
                    for idx, stk in enumerate((qs, ks)):
                        p.emit("vector", lambda v, stk=stk: v.tensor_mul(out=sqb[:], in0=stk[:], in1=stk[:]),
                               waits=[(sA, a_qk_evac), (sT, sT.n)], inc=sV)
                        v_sq = sV.n
                        for r in range(2):
                            w0 = [(sV, v_sq), (sV, v_den_hist[-1])] if r == 0 else [(sV, sV.n)]
                            for q in range(2):
                                p.emit("tensor", lambda t, r=r, q=q: t.matmul(
                                    out=dnP[:, q * 512:(q + 1) * 512], lhsT=ones_k[:],
                                    rhs=sqb[:, r * 1024 + q * 512:r * 1024 + (q + 1) * 512],
                                    start=True, stop=True),
                                    waits=w0 if q == 0 else (),
                                    inc=sT if q == 1 else None)
                            t_qn = sT.n
                            p.emit("vector", lambda v, idx=idx, r=r, nrm=nrm: v.reduce_max(
                                out=nrm[:, idx * 2 + r:idx * 2 + r + 1], in_=dnP[:], axis=AX.X),
                                waits=[(sT, t_qn)], inc=sV)
                    p.cemit("vector", lambda v, nrm=nrm: v.tensor_max(out=nrm[:, 0:1], in0=nrm[:, 0:1], in1=nrm[:, 1:2]), sV)
                    p.cemit("vector", lambda v, nrm=nrm: v.tensor_max(out=nrm[:, 1:2], in0=nrm[:, 2:3], in1=nrm[:, 3:4]), sV)
                    p.cemit("vector", lambda v, nrm=nrm: v.tensor_add(out=nrm[:, 2:3], in0=nrm[:, 0:1], in1=nrm[:, 1:2]), sV)
                    p.cemit("vector", lambda v, nrm=nrm: v.tensor_scalar(out=nrm[:, 3:4], in0=nrm[:, 2:3],
                                                                scalar1=-0.5, scalar2=None, op0=ALU.mult), sV)
                    v_negm = sV.n
                    w_rep = [(sV, v_negm), (sA, a_v_done)]
                    if hoff == 1:
                        w_rep.append((sA, a_bM_l[0]))
                    if v_out_hist:
                        w_rep.append((sV, v_out_hist[-1]))
                    p.emit("tensor", lambda t, nrm=nrm: t.matmul(out=pvP[:, 0:1], lhsT=onesr_f[:], rhs=nrm[:, 3:4],
                                                        start=True, stop=True),
                           waits=w_rep, inc=sT)
                    t_rep2 = sT.n
                    p.emit("scalar", lambda s, hoff=hoff: s.copy(out=bM[hoff][:], in_=pvP[:, 0:1]),
                           waits=[(sT, t_rep2)], inc=sA)
                    a_bM_l[hoff] = sA.n
                    p.mark(f"pre_h{h}")

                for hoff in range(2):
                    h = h0 + hoff
                    qs, ks = qstk[hoff], kstk[hoff]
                    a_bM = a_bM_l[hoff]
                    for th in range(2):
                        flush_tail()
                        t_s = [0] * NTC
                        a_exp = [0] * NTC
                        t_dp = [0] * NTC
                        def emit_S(sc, w_extra):
                            ps_ = qkP[sc % 2]
                            for q in range(2):
                                p.emit("tensor", lambda t, ps_=ps_, sc=sc, q=q, ks=ks, qs=qs, th=th:
                                       t.matmul(out=ps_[:, q * 512:(q + 1) * 512],
                                                lhsT=ks[:, sc * 128:(sc + 1) * 128],
                                                rhs=qs[:, th * 1024 + q * 512:th * 1024 + (q + 1) * 512],
                                                start=True, stop=True),
                                       waits=w_extra if q == 0 else (),
                                       inc=sT if q == 1 else None)
                            t_s[sc] = sT.n
                        emit_S(0, [(sA, a_bM), (sA, a_bca_hist[-1])])
                        emit_S(1, [(sA, a_bca_hist[-1])])
                        for sc in range(NTC):
                            w_exp = [(sT, t_s[sc])]
                            p.emit("scalar", lambda s, sc=sc, hoff=hoff: s.activation(
                                out=ptb[sc % 3][:], in_=qkP[sc % 2][:], func=AF.Exp, bias=bM[hoff][:]),
                                waits=w_exp, inc=sA)
                            a_exp[sc] = sA.n
                            if sc + 2 < NTC:
                                emit_S(sc + 2, [(sA, a_exp[sc])])
                            for q in range(2):
                                wdn = [(sA, a_exp[sc])] if (q == 0 and sc + 2 >= NTC) else []
                                if sc == 0 and q == 0:
                                    wdn.append((sV, v_den_hist[-1]))
                                p.emit("tensor", lambda t, sc=sc, q=q: t.matmul(
                                    out=dnP[:, q * 512:(q + 1) * 512], lhsT=ones_k[:],
                                    rhs=ptb[sc % 3][:, q * 512:(q + 1) * 512],
                                    start=(sc == 0), stop=(sc == NTC - 1)),
                                    waits=wdn)
                            for q in range(2):
                                wpv = []
                                if sc == 0 and q == 0 and v_out_hist:
                                    wpv.append((sV, v_out_hist[-1]))
                                p.emit("tensor", lambda t, sc=sc, q=q, hoff=hoff: t.matmul(
                                    out=pvP[:, q * 512:(q + 1) * 512],
                                    lhsT=vnat[:, sc * 256 + hoff * 128:sc * 256 + (hoff + 1) * 128],
                                    rhs=ptb[sc % 3][:, q * 512:(q + 1) * 512],
                                    start=(sc == 0), stop=(sc == NTC - 1)),
                                    waits=wpv,
                                    inc=sT if (q == 1 and sc == NTC - 1) else None)
                            t_dp[sc] = sT.n
                        p.cemit("vector", lambda v: v.reciprocal(out=rec_s[:], in_=dnP[:]),
                                sV, waits=[(sT, t_dp[NTC - 1])])
                        v_rec = sV.n
                        a_den_hist.append(0)
                        v_den_hist.append(v_rec)
                        if DEBUG:
                            p.emit("sync", lambda hd_, h=h, th=th: hd_.dma_start(
                                out=dbg["den"][h, th], in_=rec_s[:]), waits=[(sV, v_rec)], inc=sDX, inck=16)

                        def tail2(h=h, th=th, v_rec=v_rec, a_exp15=a_exp[NTC - 1], t_dp15=t_dp[NTC - 1]):
                            for q in range(2):
                                p.emit("tensor", lambda t, q=q: t.matmul(
                                    out=qkP[0][:, q * 512:(q + 1) * 512], lhsT=onesr08[:],
                                    rhs=rec_s[:, q * 512:(q + 1) * 512], start=True, stop=True),
                                    waits=[(sV, v_rec), (sA, a_exp15)] if q == 0 else (),
                                    inc=sT if q == 1 else None)
                            t_bc = sT.n
                            p.emit("scalar", lambda s: s.activation(out=bca_s[:], in_=qkP[0][:], func=AF.Copy),
                                   waits=[(sT, t_bc)], inc=sA)
                            a_bca = sA.n
                            a_bca_hist.append(a_bca)
                            p.emit("vector", lambda v, h=h, th=th: v.tensor_mul(
                                out=outT[:, h * T + th * 1024:h * T + (th + 1) * 1024],
                                in0=pvP[:], in1=bca_s[:]),
                                waits=[(sA, a_bca), (sT, t_dp15)], inc=sV)
                            v_out_hist.append(sV.n)
                        pending_tail[0] = tail2
                        p.mark(f"th_h{h}_{th}")
                    p.mark(f"attn_h{h}")
            flush_tail()
            a_attn_v_done = v_out_hist[-1]
            if DEBUG2:
                p.emit("sync", lambda h: h.dma_start(out=dbg["outt2"], in_=outT[:]),
                       waits=[(sV, a_attn_v_done)], inc=sDX, inck=16)
            if DEBUG:
                p.emit("sync", lambda h: h.dma_start(out=dbg["outt"], in_=outT[:]),
                       waits=[(sV, a_attn_v_done)], inc=sDX, inck=16)

        p.mark("attn")
        s_ln.close()

        ph = ExitStack()
        with ph:
            wpb = ph.enter_context(nc.sbuf_tensor("wpb", [128, NDC * D], BF16, side="right"))
            px0l = [ph.enter_context(nc.sbuf_tensor(f"px0{i}", [128, D], F32)) for i in range(2)]
            x1c = [ph.enter_context(nc.sbuf_tensor(f"px1{i}", [128, D], F32)) for i in range(2)]
            l2n = [ph.enter_context(nc.sbuf_tensor(f"l2n{i}", [128, D], BF16)) for i in range(3)]
            l2st = [ph.enter_context(nc.sbuf_tensor(f"l2st{i}", [128, D], BF16)) for i in range(2)]
            pj_s = [ph.enter_context(nc.sbuf_tensor(f"pj_s{i}", [128, D], F32)) for i in range(2)]

            pst_t = ph.enter_context(nc.sbuf_tensor("pst", [128, 24], F32))
            pjP = [ph.enter_context(nc.psum_tensor(f"pjP{i}", [128, 1024], F32)) for i in range(3)]
            trP2 = ph.enter_context(nc.psum_tensor("trP2", [128, T], BF16))

            for dc2 in range(NDC):
                p.emit("gpsimd", lambda g, dc=dc2: g.dma_start(
                    out=wpb[:, dc * D:(dc + 1) * D], in_=wp_d[dc]),
                    waits=[(sT, t_qkv_v_end[NPAIR - 1])] if dc2 == 0 else (),
                    inc=sDW, inck=16)
            dw_wp = sDW.n

            dx_px0 = [0] * NTC
            for tc in range(2):
                dx_px0[tc] = p.emit("sync", lambda h, tc=tc: h.dma_start(
                    out=px0l[tc % 2][:], in_=x0_d[tc * 128:(tc + 1) * 128, :]),
                    waits=[(sV, a_attn_v_done), (sA, a_bca_hist[-1])] if tc == 0 else (),
                    inc=sDX, inck=16)

            v_x1 = [0] * NTC
            dx_x1st = [0] * NTC
            a_l2aff_l = [0] * NTC
            dx_l2st = [0] * NTC
            v_r2_l = [0] * NTC
            a_pjh_l = [0] * (2 * NTC)
            a_l2ev = [0] * NTC
            t_half = {}
            pend_l2 = []
            for tc in range(NTC):
                pst = pst_t[:, (tc % 2) * 12:(tc % 2) * 12 + 12]
                for half in range(2):
                    ps_ = pjP[(2 * tc + half) % 3]
                    w0 = [(sDW, dw_wp), (sV, a_attn_v_done)]
                    if 2 * tc + half >= 3:
                        w0.append((sA, a_pjh_l[2 * tc + half - 3]))
                    for dvc in range(NDC):
                        for q in range(2):
                            p.emit("tensor", lambda t, ps_=ps_, dvc=dvc, tc=tc, half=half, q=q:
                                   t.matmul(out=ps_[:, q * 512:(q + 1) * 512],
                                            lhsT=outT[:, dvc * T + tc * 128:dvc * T + (tc + 1) * 128],
                                            rhs=wpb[:, dvc * D + half * 1024 + q * 512:dvc * D + half * 1024 + (q + 1) * 512],
                                            start=(dvc == 0), stop=(dvc == NDC - 1)),
                                   waits=w0 if (dvc == 0 and q == 0) else (),
                                   inc=sT if (dvc == NDC - 1 and q == 1) else None)
                    t_half[(tc, half)] = sT.n
                    if half == 1 and len(pend_l2) >= 2:
                        pend_l2.pop(0)()
                pj = pj_s[tc % 2]
                for half in range(2):
                    p.emit("scalar", lambda s, half=half, pj=pj, tc=tc: s.activation(
                        out=pj[:, half * 1024:(half + 1) * 1024], in_=pjP[(2 * tc + half) % 3][:], func=AF.Copy),
                        waits=[(sT, t_half[(tc, half)]),
                               (sV, v_x1[tc - 1] if tc >= 1 else 0)], inc=sA)
                    a_pjh_l[2 * tc + half] = sA.n
                scr = pj_s[(tc + 1) % 2]
                for half in range(2):
                    p.emit("scalar", lambda s, half=half, pj=pj, pst=pst, scr=scr: s.activation(
                        out=scr[:, half * 1024:(half + 1) * 1024], in_=pj[:, half * 1024:(half + 1) * 1024],
                        func=AF.Square,
                        accum_out=pst[:, half:half + 1]),
                        waits=[(sV, v_r2_l[tc - 2] if tc >= 2 else 0),
                               (sV, v_x1[tc - 1] if tc >= 1 else 0)], inc=sA)
                    a_sqh = sA.n
                    p.emit("vector", lambda v, half=half, pj=pj, pst=pst: v.reduce_sum(
                        out=pst[:, 2 + half:3 + half], in_=pj[:, half * 1024:(half + 1) * 1024], axis=AX.X),
                        waits=[(sA, a_sqh)], inc=sV)
                p.cemit("vector", lambda v, pst=pst: v.tensor_add(out=pst[:, 4:5], in0=pst[:, 2:3], in1=pst[:, 3:4]), sV)
                p.cemit("vector", lambda v, pst=pst: v.tensor_scalar(out=pst[:, 4:5], in0=pst[:, 4:5],
                                                            scalar1=1.0 / D, scalar2=None, op0=ALU.mult), sV)
                p.cemit("vector", lambda v, pst=pst: v.tensor_add(out=pst[:, 5:6], in0=pst[:, 0:1], in1=pst[:, 1:2]), sV)
                p.cemit("vector", lambda v, pst=pst: v.tensor_tensor(out=pst[:, 6:7], in0=pst[:, 4:5], in1=pst[:, 4:5], op=ALU.mult), sV)
                p.cemit("vector", lambda v, pst=pst: v.tensor_scalar(out=pst[:, 5:6], in0=pst[:, 5:6],
                                                            scalar1=1.0 / D, scalar2=pst[:, 6:7],
                                                            op0=ALU.mult, op1=ALU.subtract), sV)
                p.cemit("vector", lambda v, pst=pst: v.tensor_scalar(out=pst[:, 5:6], in0=pst[:, 5:6],
                                                            scalar1=EPS, scalar2=None, op0=ALU.add), sV)
                p.cemit("vector", lambda v, pst=pst: v.reciprocal(out=pst[:, 7:8], in_=pst[:, 5:6]), sV)
                v_r1 = sV.n
                p.emit("scalar", lambda s, pst=pst: s.activation(out=pst[:, 8:9], in_=pst[:, 7:8], func=AF.Sqrt),
                       waits=[(sV, v_r1)], inc=sA)
                a_r1 = sA.n
                p.cemit("vector", lambda v, pst=pst: v.tensor_tensor(out=pst[:, 9:10], in0=pst[:, 4:5], in1=pst[:, 8:9], op=ALU.mult),
                        sV, waits=[(sA, a_r1), (sDX, dx_px0[tc])])
                xc = px0l[tc % 2]
                xo = x1c[tc % 2]
                p.cemit("vector", lambda v, xc=xc, pst=pst: v.tensor_scalar(out=xc[:], in0=xc[:],
                                                                   scalar1=pst[:, 9:10], scalar2=None,
                                                                   op0=ALU.subtract), sV)
                for half in range(2):
                    wstt = [(sV, sV.n)]
                    if half == 0 and tc >= 2:
                        wstt.append((sDX, dx_x1st[tc - 2]))
                    p.emit("vector", lambda v, xc=xc, xo=xo, half=half, pj=pj, pst=pst: v.scalar_tensor_tensor(
                        out=xo[:, half * 1024:(half + 1) * 1024], in0=pj[:, half * 1024:(half + 1) * 1024],
                        scalar=pst[:, 8:9], in1=xc[:, half * 1024:(half + 1) * 1024],
                        op0=ALU.mult, op1=ALU.add), waits=wstt, inc=sV)
                v_x1[tc] = sV.n
                dx_x1st[tc] = p.emit("sync", lambda h, xo=xo, tc=tc: h.dma_start(out=x1_d[tc], in_=xo[:]),
                       waits=[(sV, v_x1[tc])], inc=sDX, inck=16)
                if DEBUG:
                    p.emit("sync", lambda h, xo=xo, tc=tc: h.dma_start(out=dbg["x1"][tc], in_=xo[:]),
                           inc=sDX, inck=16)
                if tc + 2 < NTC:
                    dx_px0[tc + 2] = p.emit("sync", lambda h, tc=tc: h.dma_start(
                        out=px0l[tc % 2][:], in_=x0_d[(tc + 2) * 128:(tc + 3) * 128, :]),
                        waits=[(sV, v_x1[tc])], inc=sDX, inck=16)
                p.emit("scalar", lambda s, xo=xo, pst=pst, scr=scr: s.activation(
                    out=scr[:], in_=xo[:], func=AF.Square, accum_out=pst[:, 0:1]),
                    waits=[(sV, v_x1[tc])], inc=sA)
                a_sq2 = sA.n
                p.cemit("vector", lambda v, xo=xo, pst=pst: v.reduce_sum(out=pst[:, 2:3], in_=xo[:], axis=AX.X),
                        sV, waits=[(sA, a_sq2)])
                p.cemit("vector", lambda v, pst=pst: v.tensor_scalar(out=pst[:, 4:5], in0=pst[:, 2:3],
                                                            scalar1=1.0 / D, scalar2=None, op0=ALU.mult), sV)
                p.cemit("vector", lambda v, pst=pst: v.tensor_tensor(out=pst[:, 6:7], in0=pst[:, 4:5], in1=pst[:, 4:5], op=ALU.mult), sV)
                p.cemit("vector", lambda v, pst=pst: v.tensor_scalar(out=pst[:, 5:6], in0=pst[:, 0:1],
                                                            scalar1=1.0 / D, scalar2=pst[:, 6:7],
                                                            op0=ALU.mult, op1=ALU.subtract), sV)
                p.cemit("vector", lambda v, pst=pst: v.tensor_scalar(out=pst[:, 5:6], in0=pst[:, 5:6],
                                                            scalar1=EPS, scalar2=None, op0=ALU.add), sV)
                p.cemit("vector", lambda v, pst=pst: v.reciprocal(out=pst[:, 7:8], in_=pst[:, 5:6]), sV)
                v_r2 = sV.n
                v_r2_l[tc] = v_r2
                p.emit("scalar", lambda s, pst=pst: s.activation(out=pst[:, 10:11], in_=pst[:, 7:8], func=AF.Sqrt),
                       waits=[(sV, v_r2)], inc=sA)
                a_r2 = sA.n
                ln = l2n[tc % 3]
                p.cemit("vector", lambda v, xo=xo, ln=ln, pst=pst: v.tensor_scalar(
                    out=ln[:], in0=xo[:], scalar1=pst[:, 4:5], scalar2=pst[:, 10:11],
                    op0=ALU.subtract, op1=ALU.mult),
                    sV, waits=[(sA, a_r2), (sA, a_l2ev[tc - 3] if tc >= 3 else 0)])
                a_l2aff = sV.n
                a_l2aff_l[tc] = a_l2aff
                def l2move(tc=tc, ln=ln, a_l2aff=a_l2aff):
                    for j in range(NDC):
                        p.emit("tensor", lambda t, ln=ln, j=j: t.transpose(
                            out=trP2[:, j * 128:(j + 1) * 128], in_=ln[:, j * 128:(j + 1) * 128],
                            identity=ident_b[:]),
                            waits=[(sV, a_l2aff), (sA, a_l2ev[tc - 1] if tc >= 1 else 0)] if j == 0 else (),
                            inc=sT if j == NDC - 1 else None)
                    t_tr2 = sT.n
                    stg = l2st[tc % 2]
                    wev = [(sT, t_tr2)]
                    if tc >= 2:
                        wev.append((sDX, dx_l2st[tc - 2]))
                    a_l2ev[tc] = p.emit("scalar", lambda s, stg=stg: s.activation(
                        out=stg[:], in_=trP2[:], func=AF.Copy),
                        waits=wev, inc=sA)
                    for j2 in range(2):
                        p.emit("sync", lambda h, stg=stg, tc=tc, j2=j2: h.dma_start(
                            out=l2t_d[:, :, tc * 128:(tc + 1) * 128].transpose([1, 0, 2])[:, j2 * 8:(j2 + 1) * 8, :],
                            in_=stg[:].rearrange("p (j c) -> p j c", c=128)[:, j2 * 8:(j2 + 1) * 8, :]),
                            waits=[(sA, a_l2ev[tc])], inc=sDX, inck=16)
                    dx_l2st[tc] = sDX.n
                pend_l2.append(l2move)
            while pend_l2:
                pend_l2.pop(0)()
            dx_l2t = sDX.n
            a_proj_done = sA.n
            v_proj_done = sV.n

        p.mark("proj")
        s_ap.close()

        l2T = st.enter_context(nc.sbuf_tensor("l2T", [128, NDC * T], BF16))
        rwt = [st.enter_context(nc.sbuf_tensor(f"rwt{i}", [128, NDC * 1024], BF16)) for i in range(2)]
        ewt = rwt
        c_nat = st.enter_context(nc.sbuf_tensor("c_nat", [128, 128], F32))

        for dc2 in range(NDC):
            p.emit("sync", lambda h, dc=dc2: h.dma_start(
                out=l2T[:, dc * T:(dc + 1) * T], in_=l2t_d[dc]),
                waits=[(sDX, dx_l2t)] if dc2 == 0 else (), inc=sDX, inck=16)
        dx_l2T = sDX.n
        dw_l2T = 0
        if DEBUG:
            for dc2 in range(NDC):
                p.emit("sync", lambda h, dc=dc2: h.dma_start(out=dbg["l2t"][dc], in_=l2T[:, dc * T:(dc + 1) * T]),
                       waits=[(sDX, dx_l2T)], inc=sDX, inck=16)
        p.emit("gpsimd", lambda g: g.memset(c_nat[:], 0.0),
               waits=[(sV, v_proj_done), (sDX, dx_l2t)], inc=sG)
        g_czero = sG.n

        ph = ExitStack()
        with ph:
            RW = 1024
            NOE2 = 16384 // RW
            G = RW // 8

            rbb = [ph.enter_context(nc.sbuf_tensor(f"rbb{i}", [1, RW], BF16)) for i in range(2)]
            lgb = [ph.enter_context(nc.sbuf_tensor(f"lgb{i}", [128, RW], F32)) for i in range(3)]
            Ab = ph.enter_context(nc.sbuf_tensor("Ab", [128, G * 4], F32))
            Bb = ph.enter_context(nc.sbuf_tensor("Bb", [128, G * 4], F32))
            Cb = ph.enter_context(nc.sbuf_tensor("Cb", [128, G * 2], F32))
            Db = ph.enter_context(nc.sbuf_tensor("Db", [128, G * 2], F32))
            Bm = ph.enter_context(nc.sbuf_tensor("Bm", [128, G * 2], F32))
            sm1 = [ph.enter_context(nc.sbuf_tensor(f"sm1_{i}", [128, G], F32)) for i in range(2)]
            sm2 = [ph.enter_context(nc.sbuf_tensor(f"sm2_{i}", [128, G], F32)) for i in range(2)]
            sEg = ph.enter_context(nc.sbuf_tensor("sEg", [128, G], F32))
            sB2 = ph.enter_context(nc.sbuf_tensor("sB2", [128, G], F32))
            sDm = ph.enter_context(nc.sbuf_tensor("sDm", [128, G], F32))
            ssum = [ph.enter_context(nc.sbuf_tensor(f"ssum{i}", [128, G], F32)) for i in range(2)]
            ab = [ph.enter_context(nc.sbuf_tensor(f"ab{i}", [128, RW], BF16)) for i in range(2)]
            d2b = [ph.enter_context(nc.sbuf_tensor(f"d2b{i}", [128, RW], BF16)) for i in range(2)]
            sgb = [ph.enter_context(nc.sbuf_tensor(f"sgb{i}", [128, RW], BF16)) for i in range(2)]
            wb = [ph.enter_context(nc.sbuf_tensor(f"wbuf{i}", [128, RW], BF16)) for i in range(2)]

            crd = ph.enter_context(nc.sbuf_tensor("crd", [128, 8], F32))
            lgP = [ph.enter_context(nc.psum_tensor(f"lgP{i}", [128, RW], F32)) for i in range(3)]

            dw_rw = [0] * NOE2
            p.emit("gpsimd", lambda g: g.dma_start(out=rbb[0][:], in_=rb_d[:, 0:RW]),
                   waits=[(sV, v_proj_done), (sDX, dx_l2t)], inc=sDW, inck=16)
            for dc2 in range(NDC):
                p.emit("gpsimd", lambda g, dc=dc2: g.dma_start(
                    out=rwt[0][:, dc * RW:(dc + 1) * RW],
                    in_=rwt_d[dc * 128:(dc + 1) * 128, 0:RW]), inc=sDW, inck=16)
            dw_rw[0] = sDW.n

            t_lg = {}
            a_lg = {}
            dbg_t0 = 0
            tile_idx = 0
            a_sg = {}
            v_w_hist = {}
            v_d2_hist = {}
            t_last_router_mm = 0
            for oe in range(NOE2):
                rw = rwt[oe % 2]
                rb_cur = rbb[oe % 2]
                for tc in range(NTC):
                    ps_ = lgP[tile_idx % 3]
                    w0 = [(sDW, dw_rw[oe]), (sDX, dx_l2T), (sDW, dw_l2T)]
                    if tile_idx >= 3:
                        w0.append((sA, a_lg[tile_idx - 3]))
                    for dc in range(NDC):
                        for q in range(2):
                            p.emit("tensor", lambda t, ps_=ps_, rw=rw, dc=dc, tc=tc, q=q:
                                   t.matmul(out=ps_[:, q * 512:(q + 1) * 512],
                                            lhsT=l2T[:, dc * T + tc * 128:dc * T + (tc + 1) * 128],
                                            rhs=rw[:, dc * RW + q * 512:dc * RW + (q + 1) * 512],
                                            start=(dc == 0), stop=False),
                                   waits=w0 if (dc == 0 and q == 0) else ())
                    for q in range(2):
                        p.emit("tensor", lambda t, ps_=ps_, rb_cur=rb_cur, q=q: t.matmul(
                            out=ps_[:, q * 512:(q + 1) * 512], lhsT=onesr_b[:],
                            rhs=rb_cur[:, q * 512:(q + 1) * 512],
                            start=False, stop=True), inc=sT if q == 1 else None)
                    t_lg[tile_idx] = sT.n
                    t_last_router_mm = sT.n
                    lg = lgb[tile_idx % 3]
                    wlg = [(sT, t_lg[tile_idx])]
                    if tile_idx >= 3:
                        wlg.append((sV, v_d2_hist[tile_idx - 3]))
                    a_lg[tile_idx] = p.emit("scalar", lambda s, ps_=ps_, lg=lg: s.activation(
                        out=lg[:], in_=ps_[:], func=AF.Copy),
                        waits=wlg, inc=sA)
                    par = tile_idx % 2
                    Lv = lg[:].rearrange("p (e g) -> p e g", g=G)
                    Av = Ab[:].rearrange("p (e g) -> p e g", g=G)
                    Bv = Bb[:].rearrange("p (e g) -> p e g", g=G)
                    Cv = Cb[:].rearrange("p (e g) -> p e g", g=G)
                    Dv = Db[:].rearrange("p (e g) -> p e g", g=G)
                    Bmv = Bm[:].rearrange("p (e g) -> p e g", g=G)
                    wl = [(sA, a_lg[tile_idx])]
                    if tile_idx >= 2:
                        wl.append((sV, v_d2_hist[tile_idx - 2]))
                    p.emit("vector", lambda v, Lv=Lv, Av=Av: v.tensor_tensor(
                        out=Av, in0=Lv[:, 0::2, :], in1=Lv[:, 1::2, :], op=ALU.max), waits=wl)
                    p.emit("vector", lambda v, Lv=Lv, Bv=Bv: v.tensor_tensor(
                        out=Bv, in0=Lv[:, 0::2, :], in1=Lv[:, 1::2, :], op=ALU.min))
                    p.emit("vector", lambda v, Av=Av, Cv=Cv: v.tensor_tensor(
                        out=Cv, in0=Av[:, 0::2, :], in1=Av[:, 1::2, :], op=ALU.max))
                    p.emit("vector", lambda v, Av=Av, Dv=Dv: v.tensor_tensor(
                        out=Dv, in0=Av[:, 0::2, :], in1=Av[:, 1::2, :], op=ALU.min))
                    p.emit("vector", lambda v, Bv=Bv, Bmv=Bmv: v.tensor_tensor(
                        out=Bmv, in0=Bv[:, 0::2, :], in1=Bv[:, 1::2, :], op=ALU.max))
                    p.cemit("vector", lambda v, Cv=Cv, par=par: v.tensor_tensor(
                        out=sm1[par][:], in0=Cv[:, 0, :], in1=Cv[:, 1, :], op=ALU.max), sV)
                    p.cemit("vector", lambda v, Cv=Cv: v.tensor_tensor(
                        out=sEg[:], in0=Cv[:, 0, :], in1=Cv[:, 1, :], op=ALU.min), sV)
                    p.cemit("vector", lambda v, Dv=Dv: v.tensor_tensor(
                        out=sDm[:], in0=Dv[:, 0, :], in1=Dv[:, 1, :], op=ALU.max), sV)
                    p.cemit("vector", lambda v, Bmv=Bmv: v.tensor_tensor(
                        out=sB2[:], in0=Bmv[:, 0, :], in1=Bmv[:, 1, :], op=ALU.max), sV)
                    p.cemit("vector", lambda v, par=par: v.tensor_tensor(
                        out=sm2[par][:], in0=sB2[:], in1=sDm[:], op=ALU.max), sV)
                    p.cemit("vector", lambda v, par=par: v.tensor_tensor(
                        out=sm2[par][:], in0=sm2[par][:], in1=sEg[:], op=ALU.max), sV)
                    p.cemit("vector", lambda v, par=par: v.tensor_tensor(
                        out=ssum[par][:], in0=sm1[par][:], in1=sm2[par][:], op=ALU.add), sV)
                    m2v = sm2[par][:].rearrange("p (one g) -> p one g", one=1)
                    sv = ssum[par][:].rearrange("p (one g) -> p one g", one=1)
                    m2bc, Lbc = bass.broadcast_tensor_aps(m2v, Lv)
                    p.emit("vector", lambda v, Lbc=Lbc, m2bc=m2bc, par=par: v.tensor_tensor(
                        out=ab[par][:].rearrange("p (e g) -> p e g", g=G),
                        in0=Lbc, in1=m2bc, op=ALU.is_ge), waits=[(sV, sV.n)])
                    sbc, _ = bass.broadcast_tensor_aps(sv, Lv)
                    wd2 = [(sV, sV.n)]
                    if tile_idx >= 2:
                        wd2.append((sA, a_sg[tile_idx - 2]))
                    p.emit("vector", lambda v, Lv=Lv, sbc=sbc, par=par: v.scalar_tensor_tensor(
                        out=d2b[par][:].rearrange("p (e g) -> p e g", g=G),
                        in0=Lv, scalar=2.0, in1=sbc, op0=ALU.mult, op1=ALU.subtract),
                        waits=wd2, inc=sV)
                    v_d2 = sV.n
                    v_d2_hist[tile_idx] = v_d2
                    wsg = [(sV, v_d2)]
                    if tile_idx >= 2:
                        wsg.append((sV, v_w_hist[tile_idx - 2]))
                    a_sg[tile_idx] = p.emit("scalar", lambda s, par=par: s.activation(
                        out=sgb[par][:], in_=d2b[par][:], func=AF.Sigmoid),
                        waits=wsg, inc=sA)
                    if DEBUG and tile_idx == 0:
                        p.emit("sync", lambda h: h.dma_start(out=dbg["lg0"], in_=lgb[0][:]),
                               waits=[(sV, v_d2)], inc=sDX, inck=16)
                        p.emit("sync", lambda h: h.dma_start(out=dbg["ab0"], in_=ab[0][:]),
                               inc=sDX, inck=16)
                        p.emit("sync", lambda h: h.dma_start(out=dbg["m20"], in_=sm2[0][:]),
                               inc=sDX, inck=16)
                        p.emit("sync", lambda h: h.dma_start(out=dbg["sg0"], in_=sgb[0][:]),
                               waits=[(sA, a_sg[0])], inc=sDX, inck=16)
                    def fin_tile(ti):
                        par2 = ti % 2
                        tcl = ti % NTC
                        p.cemit("vector", lambda v, par2=par2: v.tensor_tensor(
                            out=wb[par2][:], in0=ab[par2][:], in1=sgb[par2][:], op=ALU.mult),
                            sV, waits=[(sA, a_sg[ti])])
                        if DEBUG and ti == 0:
                            p.emit("sync", lambda h: h.dma_start(out=dbg["wb0"], in_=wb[0][:]),
                                   waits=[(sV, sV.n)], inc=sDX, inck=16)
                        p.cemit("vector", lambda v, par2=par2: v.reduce_sum(
                            out=crd[:], in_=wb[par2][:].rearrange("p (e g) -> p e g", g=G), axis=AX.X), sV)
                        if DEBUG and ti == 0:
                            p.emit("sync", lambda h: h.dma_start(out=dbg["crd0"], in_=crd[:]),
                                   waits=[(sV, sV.n)], inc=sDX, inck=16)
                            dbg_t0 = sDX.n
                        wca = [(sG, g_czero)] if ti < NTC else []
                        if DEBUG and ti == 0:
                            wca.append((sDX, dbg_t0))
                        p.cemit("vector", lambda v, tcl=tcl: v.tensor_add(
                            out=c_nat[:, tcl * 8:(tcl + 1) * 8], in0=c_nat[:, tcl * 8:(tcl + 1) * 8],
                            in1=crd[:]), sV, waits=wca)
                        v_w_hist[ti] = sV.n
                    if tile_idx >= 1:
                        fin_tile(tile_idx - 1)
                    tile_idx += 1
                if oe + 1 < NOE2:
                    w_rwt = []
                    if oe >= 1:
                        w_rwt.append((sT, t_lg[oe * NTC - 1]))
                    p.emit("gpsimd", lambda g, oe=oe: g.dma_start(
                        out=rbb[(oe + 1) % 2][:], in_=rb_d[:, (oe + 1) * RW:(oe + 2) * RW]),
                        waits=w_rwt, inc=sDW, inck=16)
                    for dc2 in range(NDC):
                        p.emit("gpsimd", lambda g, dc=dc2, oe=oe: g.dma_start(
                            out=rwt[(oe + 1) % 2][:, dc * RW:(dc + 1) * RW],
                            in_=rwt_d[dc * 128:(dc + 1) * 128, (oe + 1) * RW:(oe + 2) * RW]),
                            inc=sDW, inck=16)
                    dw_rw[oe + 1] = sDW.n
            fin_tile(tile_idx - 1)
            v_comb_done = v_w_hist[tile_idx - 1]
            p.emit("gpsimd", lambda g: g.dma_start(out=ewt[0][:], in_=ew_d[0]),
                   waits=[(sT, t_last_router_mm)], inc=sDW, inck=16)
            dw_ew0 = sDW.n
            if DEBUG:
                p.emit("sync", lambda h: h.dma_start(out=dbg["c"], in_=c_nat[:]),
                       waits=[(sV, v_comb_done)], inc=sDX, inck=16)

        p.mark("router")
        ph = ExitStack()
        with ph:
            cTb = ph.enter_context(nc.sbuf_tensor("cTb", [8, T], BF16))
            cb = [ph.enter_context(nc.sbuf_tensor(f"cb{i}", [128, T], BF16)) for i in range(8)]
            facc = ph.enter_context(nc.sbuf_tensor("facc", [128, D], F32))
            tmpb = [ph.enter_context(nc.sbuf_tensor(f"tmpb{i}", [128, 1024], BF16)) for i in range(2)]
            eob = [ph.enter_context(nc.sbuf_tensor(f"eob{i}", [128, 1024], BF16)) for i in range(2)]
            fTb = ph.enter_context(nc.sbuf_tensor("fTb", [128, D], BF16))
            fst = [ph.enter_context(nc.sbuf_tensor(f"fst{i}", [128, D], BF16)) for i in range(2)]
            eoH = [ph.enter_context(nc.psum_tensor(f"eoH{i}", [128, 1024], F32)) for i in range(3)]
            trPf = ph.enter_context(nc.psum_tensor("trPf", [128, T], BF16))

            t_ctr = [0] * NTC
            for tc in range(NTC):
                reg = eoH[0][0:8, (tc % 2) * 512:(tc % 2) * 512 + 128]
                p.emit("tensor", lambda t, tc=tc, reg=reg: t.transpose(
                    out=reg, in_=c_nat[:, tc * 8:(tc + 1) * 8], identity=ident_f[:]),
                    waits=[(sV, v_comb_done)] if tc == 0 else
                          ([(sA, t_ctr[tc - 2])] if tc >= 2 else ()),
                    inc=sT)
                t_now = sT.n
                t_ctr[tc] = p.emit("scalar", lambda s, tc=tc, reg=reg: s.activation(
                    out=cTb[:, tc * 128:(tc + 1) * 128], in_=reg, func=AF.Copy),
                    waits=[(sT, t_now)], inc=sA)
            p.emit("vector", lambda v: v.engine_nop(), waits=[(sA, t_ctr[NTC - 1])], inc=sV)
            v_ctb = sV.n
            a_cb = [0] * 8
            for e in range(8):
                for half in range(2):
                    pse = eoH[1 + half]
                    for q in range(2):
                        p.emit("tensor", lambda t, e=e, half=half, q=q, pse=pse: t.matmul(
                            out=pse[:, q * 512:(q + 1) * 512], lhsT=sel[:, e * 128:(e + 1) * 128],
                            rhs=cTb[:, half * 1024 + q * 512:half * 1024 + (q + 1) * 512],
                            start=True, stop=True),
                            waits=[(sV, v_ctb)] if (e == 0 and half == 0 and q == 0) else
                                  ([(sA, a_cb[e - 1])] if (half == 0 and q == 0 and e >= 1) else ()),
                            inc=sT if q == 1 else None)
                    t_cb = sT.n
                    p.emit("scalar", lambda s, e=e, half=half, pse=pse: s.activation(
                        out=cb[e][:, half * 1024:(half + 1) * 1024], in_=pse[:], func=AF.Copy),
                        waits=[(sT, t_cb)], inc=sA)
                a_cb[e] = sA.n
            a_cb_done = sA.n

            dw_ew = [0] * NOC
            dw_ew[0] = dw_ew0

            a_fstg = [0] * NOC
            g_add_hist = []
            v_tmp_hist = []
            a_eo_hist = []
            t_oc_end = [0] * NOC
            v_ft_l = [0] * NOC
            dx_fst = [0] * NOC
            for oc in range(NOC):
                eb_ = ewt[oc % 2]
                fp = 0
                w_bias = [(sA, a_cb_done), (sDW, dw_ew[oc]), (sDW, dw_rbeb)]
                if oc >= 1:
                    w_bias.append((sV, v_ft_l[oc - 1]))
                for half in range(2):
                    pse = eoH[half]
                    for q in range(2):
                        p.emit("tensor", lambda t, oc=oc, half=half, q=q, pse=pse: t.matmul(
                            out=pse[:, q * 512:(q + 1) * 512], lhsT=eb_s[:, oc * 128:(oc + 1) * 128],
                            rhs=cTb[:, half * 1024 + q * 512:half * 1024 + (q + 1) * 512],
                            start=True, stop=True),
                            waits=(w_bias if (half == 0 and q == 0) else ()),
                            inc=sT if q == 1 else None)
                    t_b = sT.n
                    p.emit("scalar", lambda s, half=half, pse=pse: s.activation(
                        out=facc[:, half * 1024:(half + 1) * 1024], in_=pse[:], func=AF.Copy),
                        waits=[(sT, t_b)], inc=sA)
                a_binit = sA.n
                if oc + 1 < NOC:
                    w_ew = [(sT, t_oc_end[oc - 1])] if oc >= 1 else []
                    p.emit("gpsimd", lambda g, oc=oc: g.dma_start(
                        out=ewt[(oc + 1) % 2][:], in_=ew_d[oc + 1]),
                        waits=w_ew, inc=sDW, inck=16)
                    dw_ew[oc + 1] = sDW.n
                hk = 0
                for e in range(8):
                    for half in range(2):
                        pse = eoH[hk % 3]
                        w0 = []
                        if hk >= 3:
                            w0.append((sA, a_eo_hist[-3]))
                        if e == 0 and half == 0:
                            w0.append((sA, a_binit))
                        for dc in range(NDC):
                            for q in range(2):
                                p.emit("tensor", lambda t, pse=pse, eb_=eb_, e=e, dc=dc, half=half, q=q:
                                       t.matmul(out=pse[:, q * 512:(q + 1) * 512],
                                                lhsT=eb_[:, (e * NDC + dc) * 128:(e * NDC + dc + 1) * 128],
                                                rhs=l2T[:, dc * T + half * 1024 + q * 512:dc * T + half * 1024 + (q + 1) * 512],
                                                start=(dc == 0), stop=(dc == NDC - 1)),
                                       waits=w0 if (dc == 0 and q == 0) else (),
                                       inc=sT if (dc == NDC - 1 and q == 1) else None)
                        t_eo = sT.n
                        eb2 = eob[hk % 2]
                        p.emit("scalar", lambda s, pse=pse, eb2=eb2: s.activation(
                            out=eb2[:], in_=pse[:], func=AF.Copy),
                            waits=[(sT, t_eo), (sV, v_tmp_hist[-2] if len(v_tmp_hist) >= 2 else 0)],
                            inc=sA)
                        a_eo = sA.n
                        tb = tmpb[hk % 2]
                        wv = [(sA, a_eo), (sA, a_cb_done)]
                        if len(g_add_hist) >= 2:
                            wv.append((sG, g_add_hist[-2]))
                        p.emit("vector", lambda v, eb2=eb2, tb=tb, e=e, half=half: v.tensor_mul(
                            out=tb[:], in0=eb2[:], in1=cb[e][:, half * 1024:(half + 1) * 1024]),
                            waits=wv, inc=sV)
                        v_tmp_hist.append(sV.n)
                        a_eo_hist.append(a_eo)
                        p.emit("gpsimd", lambda g, tb=tb, half=half: g.tensor_tensor(
                            out=facc[:, half * 1024:(half + 1) * 1024],
                            in0=facc[:, half * 1024:(half + 1) * 1024],
                            in1=tb[:], op=ALU.add),
                            waits=[(sV, v_tmp_hist[-1])], inc=sG)
                        g_add_hist.append(sG.n)
                        hk += 1
                g_oc = sG.n
                t_oc_end[oc] = sT.n
                p.emit("vector", lambda v: v.tensor_copy(out=fTb[:], in_=facc[:]),
                       waits=[(sG, g_oc)], inc=sV)
                v_ft = sV.n
                v_ft_l[oc] = v_ft
                for j in range(NTC):
                    p.emit("tensor", lambda t, j=j: t.transpose(
                        out=trPf[:, j * 128:(j + 1) * 128], in_=fTb[:, j * 128:(j + 1) * 128],
                        identity=ident_b[:]),
                        waits=[(sV, v_ft), (sA, a_fstg[oc - 1] if oc >= 1 else 0)] if j == 0 else (),
                        inc=sT if j == NTC - 1 else None)
                t_trf = sT.n
                stg = fst[oc % 2]
                wfs = [(sT, t_trf)]
                if oc >= 2:
                    wfs.append((sDX, dx_fst[oc - 2]))
                a_fstg[oc] = p.emit("scalar", lambda s, stg=stg: s.activation(
                    out=stg[:], in_=trPf[:], func=AF.Copy),
                    waits=wfs, inc=sA)
                for j2 in range(2):
                    p.emit("sync", lambda h, stg=stg, oc=oc, j2=j2: h.dma_start(
                        out=fin_d[:, :, oc * 128:(oc + 1) * 128].transpose([1, 0, 2])[:, j2 * 8:(j2 + 1) * 8, :],
                        in_=stg[:].rearrange("p (j c) -> p j c", c=128)[:, j2 * 8:(j2 + 1) * 8, :]),
                        waits=[(sA, a_fstg[oc])], inc=sDX, inck=16)
                dx_fst[oc] = sDX.n
            dx_fin = sDX.n
            if DEBUG:
                for tc in range(NTC):
                    p.emit("sync", lambda h, tc=tc: h.dma_start(out=dbg["fin"][tc], in_=fin_d[tc]),
                           waits=[(sDX, dx_fin)], inc=sDX, inck=16)

        p.mark("expert")
        ph = ExitStack()
        with ph:
            fnb = [ph.enter_context(nc.sbuf_tensor(f"fnb{i}", [128, D], BF16)) for i in range(2)]
            fx1 = [ph.enter_context(nc.sbuf_tensor(f"fx1{i}", [128, D], F32)) for i in range(2)]
            x2c = [ph.enter_context(nc.sbuf_tensor(f"x2c{i}", [128, D], F32)) for i in range(2)]
            sqs3 = ph.enter_context(nc.sbuf_tensor("sqs3", [128, D], BF16))
            fstt_t = ph.enter_context(nc.sbuf_tensor("fstt", [128, 24], F32))

            dx_fn = [0] * NTC
            dx_fx = [0] * NTC
            for tc in range(2):
                dx_fn[tc] = p.emit("sync", lambda h, tc=tc: h.dma_start(
                    out=fnb[tc % 2][:], in_=fin_d[tc]), waits=[(sDX, dx_fin)], inc=sDX, inck=16)
                dx_fx[tc] = p.emit("sync", lambda h, tc=tc: h.dma_start(
                    out=fx1[tc % 2][:], in_=x1_d[tc]), inc=sDX, inck=16)
            v_x2 = [0] * NTC
            v_f1_l = [0] * NTC
            for tc in range(NTC):
                fstt = fstt_t[:, (tc % 2) * 12:(tc % 2) * 12 + 12]
                fn = fnb[tc % 2]
                fx = fx1[tc % 2]
                xo = x2c[tc % 2]
                p.emit("scalar", lambda s, fn=fn, fstt=fstt: s.activation(
                    out=sqs3[:], in_=fn[:], func=AF.Square, accum_out=fstt[:, 0:1]),
                    waits=[(sDX, dx_fn[tc]),
                           (sV, v_f1_l[tc - 2] if tc >= 2 else 0)], inc=sA)
                a_s3 = sA.n
                p.cemit("vector", lambda v, fn=fn, fstt=fstt: v.reduce_sum(out=fstt[:, 1:2], in_=fn[:], axis=AX.X),
                        sV, waits=[(sA, a_s3)])
                p.cemit("vector", lambda v, fstt=fstt: v.tensor_scalar(out=fstt[:, 2:3], in0=fstt[:, 1:2],
                                                            scalar1=1.0 / D, scalar2=None, op0=ALU.mult), sV)
                p.cemit("vector", lambda v, fstt=fstt: v.tensor_tensor(out=fstt[:, 3:4], in0=fstt[:, 2:3], in1=fstt[:, 2:3], op=ALU.mult), sV)
                p.cemit("vector", lambda v, fstt=fstt: v.tensor_scalar(out=fstt[:, 4:5], in0=fstt[:, 0:1],
                                                            scalar1=1.0 / D, scalar2=fstt[:, 3:4],
                                                            op0=ALU.mult, op1=ALU.subtract), sV)
                p.cemit("vector", lambda v, fstt=fstt: v.tensor_scalar(out=fstt[:, 4:5], in0=fstt[:, 4:5],
                                                            scalar1=EPS, scalar2=None, op0=ALU.add), sV)
                p.cemit("vector", lambda v, fstt=fstt: v.reciprocal(out=fstt[:, 5:6], in_=fstt[:, 4:5]), sV)
                v_f1 = sV.n
                v_f1_l[tc] = v_f1
                p.emit("scalar", lambda s, fstt=fstt: s.activation(out=fstt[:, 6:7], in_=fstt[:, 5:6], func=AF.Sqrt),
                       waits=[(sV, v_f1)], inc=sA)
                a_f1 = sA.n
                p.cemit("vector", lambda v, fstt=fstt: v.tensor_tensor(out=fstt[:, 7:8], in0=fstt[:, 2:3], in1=fstt[:, 6:7], op=ALU.mult),
                        sV, waits=[(sA, a_f1), (sDX, dx_fx[tc])])
                p.cemit("vector", lambda v, fx=fx, fstt=fstt: v.tensor_scalar(out=fx[:], in0=fx[:],
                                                                   scalar1=fstt[:, 7:8], scalar2=None,
                                                                   op0=ALU.subtract), sV)
                p.cemit("vector", lambda v, fn=fn, fx=fx, xo=xo, fstt=fstt: v.scalar_tensor_tensor(
                    out=xo[:], in0=fn[:], scalar=fstt[:, 6:7], in1=fx[:],
                    op0=ALU.mult, op1=ALU.add),
                    sV, waits=[(sO, (tc - 2 + 1) * 16) if tc >= 2 else (sO, 0)])
                v_x2[tc] = sV.n
                p.emit("sync", lambda h, xo=xo, tc=tc: h.dma_start(
                    out=out_d[tc * 128:(tc + 1) * 128, :], in_=xo[:]),
                    waits=[(sV, v_x2[tc])], inc=sO, inck=16)
                if tc + 2 < NTC:
                    dx_fn[tc + 2] = p.emit("sync", lambda h, tc=tc: h.dma_start(
                        out=fnb[tc % 2][:], in_=fin_d[tc + 2]), waits=[(sV, v_x2[tc])], inc=sDX, inck=16)
                    dx_fx[tc + 2] = p.emit("sync", lambda h, tc=tc: h.dma_start(
                        out=fx1[tc % 2][:], in_=x1_d[tc + 2]),
                        waits=[(sV, v_x2[tc])], inc=sDX, inck=16)
            p.wait("sync", (sO, NTC * 16))

        p.mark("final")
        if phase_limit is not None:
            p.truncate(phase_limit)
        p.check_schedule()
        with nc.Block() as block:
            @block.sync
            def _(h):
                for fn in p.engs["sync"]:
                    fn(h)

            @block.gpsimd
            def _(h):
                for fn in p.engs["gpsimd"]:
                    fn(h)

            @block.tensor
            def _(h):
                for fn in p.engs["tensor"]:
                    fn(h)

            @block.vector
            def _(h):
                for fn in p.engs["vector"]:
                    fn(h)

            @block.scalar
            def _(h):
                for fn in p.engs["scalar"]:
                    fn(h)

    return nc


_NC_CACHE = None


def _get_nc():
    global _NC_CACHE
    if _NC_CACHE is None:
        _NC_CACHE = build_core_program()
    return _NC_CACHE


def _prep_shared(L0_attn_W_attn, L0_attn_W_proj, L0_attn_lambda_q1, L0_attn_lambda_k1,
                 L0_attn_lambda_q2, L0_attn_lambda_k2, L0_ffn_router_w, L0_ffn_router_b,
                 L0_ffn_expert_w, L0_ffn_expert_b):
    C = np.ascontiguousarray
    wa = np.asarray(L0_attn_W_attn, dtype=np.float32)
    cols = []
    for pr in range(NPAIR):
        cols.append(np.concatenate([
            np.arange(128 * pr, 128 * (pr + 1)),
            1024 + np.arange(128 * pr, 128 * (pr + 1)),
            2048 + np.arange(128 * pr, 128 * (pr + 1)),
            3072 + np.arange(128 * pr, 128 * (pr + 1)),
            4096 + np.arange(256 * pr, 256 * (pr + 1)),
        ]))
    wa_re = np.stack([wa[:, c] for c in cols], axis=0)
    wa_re = C(wa_re.reshape(NPAIR, NDC, 128, 768))
    wp_re = C(np.asarray(L0_attn_W_proj, dtype=np.float32).reshape(NDC, 128, D))
    rwt = np.asarray(L0_ffn_router_w, dtype=np.float32).T
    rwt = C(rwt.reshape(D, 16, 128, 8).transpose(0, 1, 3, 2).reshape(D, 16384))
    rb = np.asarray(L0_ffn_router_b, dtype=np.float32)
    rb = C(rb.reshape(16, 128, 8).transpose(0, 2, 1).reshape(1, 16384))
    ew = np.asarray(L0_ffn_expert_w, dtype=np.float32)
    ew_re = C(ew.reshape(8, NOC, 128, NDC, 128).transpose(1, 4, 0, 3, 2))
    ebb = C(np.asarray(L0_ffn_expert_b, dtype=np.float32))
    lam = lambda x: C(np.asarray(x, dtype=np.float32).reshape(128, 16))
    return {
        "wa": wa_re, "wp": wp_re, "rwt": rwt, "rb": rb, "ew": ew_re, "ebb": ebb,
        "lq1": lam(L0_attn_lambda_q1), "lk1": lam(L0_attn_lambda_k1),
        "lq2": lam(L0_attn_lambda_q2), "lk2": lam(L0_attn_lambda_k2),
    }


def kernel(L0_attn_x, L0_attn_W_attn, L0_attn_W_proj, L0_attn_lambda_q1, L0_attn_lambda_k1,
           L0_attn_lambda_q2, L0_attn_lambda_k2, L0_ffn_router_w, L0_ffn_router_b,
           L0_ffn_expert_w, L0_ffn_expert_b):
    x = np.ascontiguousarray(np.asarray(L0_attn_x, dtype=np.float32))
    shared = _prep_shared(L0_attn_W_attn, L0_attn_W_proj, L0_attn_lambda_q1, L0_attn_lambda_k1,
                          L0_attn_lambda_q2, L0_attn_lambda_k2, L0_ffn_router_w, L0_ffn_router_b,
                          L0_ffn_expert_w, L0_ffn_expert_b)
    nc = _get_nc()
    in_maps = []
    for c in range(8):
        m = {"x0": np.ascontiguousarray(x[c])}
        m.update(shared)
        in_maps.append(m)
    res = run_bass_kernel_spmd(nc, in_maps, core_ids=list(range(8)))
    out = np.stack([np.asarray(res.results[c]["out"]) for c in range(8)], axis=0)
    return out.astype(np.float32)


if __name__ == "__main__":
    rng = np.random.default_rng(0)
    args = {
        "L0_attn_x": rng.standard_normal((8, T, D), dtype=np.float32),
        "L0_attn_W_attn": rng.standard_normal((D, 3 * D), dtype=np.float32) * 0.02,
        "L0_attn_W_proj": rng.standard_normal((D, D), dtype=np.float32) * 0.02,
        "L0_attn_lambda_q1": rng.standard_normal(D).astype(np.float32) * 0.1,
        "L0_attn_lambda_k1": rng.standard_normal(D).astype(np.float32) * 0.1,
        "L0_attn_lambda_q2": rng.standard_normal(D).astype(np.float32) * 0.1,
        "L0_attn_lambda_k2": rng.standard_normal(D).astype(np.float32) * 0.1,
        "L0_ffn_router_w": rng.standard_normal((16384, D), dtype=np.float32) * 0.02,
        "L0_ffn_router_b": rng.standard_normal(16384).astype(np.float32) * 0.02,
        "L0_ffn_expert_w": rng.standard_normal((8, D, D), dtype=np.float32) * 0.02,
        "L0_ffn_expert_b": rng.standard_normal((8, D)).astype(np.float32) * 0.02,
    }
    o = kernel(**args)
    print("kernel ran, out:", o.shape, o.dtype, np.abs(o).mean())
